# version 20
# baseline (speedup 1.0000x reference)
import numpy as np
import contextlib
from collections import defaultdict

import concourse.bass as bass
import concourse.mybir as mybir

F32 = mybir.dt.float32
BF16 = mybir.dt.bfloat16
I8 = mybir.dt.int8
I32 = mybir.dt.int32
U32 = mybir.dt.uint32
AF = mybir.ActivationFunctionType
ALU = mybir.AluOpType
AX = mybir.AxisListType

ENGINES = ('sync', 'scalar', 'vector', 'tensor', 'gpsimd')


class PB:

    def __init__(self, nc):
        self.nc = nc
        self.es = contextlib.ExitStack()
        self.q = {e: [] for e in ENGINES}
        self.sem_names = []
        self.sems = {}
        self.cnt = defaultdict(int)
        self.wm = {e: defaultdict(int) for e in ENGINES}
        self.res = {}
        self._tensors = {}
        self.n_dma_sems = 12
        self._dma_rr = defaultdict(int)
        self._pending = {e: ([], []) for e in ENGINES}

    def sem(self, name):
        if name not in self.sems:
            self.sems[name] = self.es.enter_context(self.nc.semaphore(name))
            self.sem_names.append(name)
        return self.sems[name]

    def sbuf(self, name, shape, dt):
        t = self.es.enter_context(self.nc.sbuf_tensor(name, list(shape), dt))
        self._tensors[name] = t
        import math
        nb = {2: 2, 4: 4, 1: 1}[mybir.dt.size(dt)]
        self.sbuf_bytes = getattr(self, 'sbuf_bytes', 0) +             int(np.prod(shape[1:])) * mybir.dt.size(dt)
        return t

    def psum(self, name, shape, dt=F32):
        t = self.es.enter_context(self.nc.psum_tensor(name, list(shape), dt))
        self._tensors[name] = t
        return t

    def _need(self, key, for_write):
        rec = self.res.get(key)
        if rec is None:
            return []
        out = list(rec['w'])
        if for_write:
            out.extend(rec['r'])
        return out

    def emit(self, eng, fn, reads=(), writes=(), sem=None, dma=False,
             inc=True, tag=None):
        needed = {}
        for r in reads:
            for (s, c) in self._need(r, False):
                needed[s] = max(needed.get(s, 0), c)
        for w in writes:
            for (s, c) in self._need(w, True):
                needed[s] = max(needed.get(s, 0), c)
        if dma:
            sname = sem or ('DMA_%s_%d' % (eng, self._dma_rr[eng] % self.n_dma_sems))
            if sem is None:
                self._dma_rr[eng] += 1
            prev = self.cnt[sname]
            if prev:
                needed[sname] = max(needed.get(sname, 0), prev)
        else:
            sname = sem or ('E' + eng)
        self.sem(sname)
        wl = []
        for s, c in sorted(needed.items()):
            if self.wm[eng][s] < c:
                wl.append((s, c))
                self.wm[eng][s] = c
        amt = 16 if dma else 1
        if inc:
            self.cnt[sname] += amt
            cafter = self.cnt[sname]
        else:
            cafter = None

        def run(h, wl=wl, fn=fn, sname=sname, amt=amt, inc=inc):
            for (s, c) in wl:
                h.wait_ge(self.sems[s], c)
            try:
                r = fn(h)
            except Exception:
                import traceback as _tb
                _tb.print_exc()
                raise
            if inc:
                if r is None:
                    raise RuntimeError(f'op on {eng} returned None (need inst)')
                r.then_inc(self.sems[sname], amt)
        self.q[eng].append(run)
        if inc:
            pr, pw = self._pending[eng]
            allw = list(writes) + pw
            allr = list(reads) + pr
            self._pending[eng] = ([], [])
            for w in allw:
                prev = self.res.get(w)
                if dma and prev is not None and prev.get('dma'):
                    chain = (prev['w'] + [(sname, cafter)])[-8:]
                else:
                    chain = [(sname, cafter)]
                self.res[w] = {'w': chain, 'r': [], 'dma': dma}
            for r in allr:
                rec = self.res.get(r)
                if rec is not None:
                    rec['r'].append((sname, cafter))
        else:
            pr, pw = self._pending[eng]
            pr.extend(reads)
            pw.extend(writes)
        return (sname, cafter)

    def raw(self, eng, fn):
        self.q[eng].append(fn)

    def wait_all_done(self, eng='sync'):
        snap = dict(self.cnt)
        def run(h, snap=snap):
            for s, c in sorted(snap.items()):
                if c > self.wm[eng].get(s, 0):
                    h.wait_ge(self.sems[s], c)
        self.q[eng].append(run)

    def build(self):
        with self.nc.Block() as block:
            for e in ENGINES:
                ops = self.q[e]
                if not ops:
                    continue
                def body(h, ops=ops):
                    for op in ops:
                        op(h)
                getattr(block, e)(body)
        self.es.close()


def dsplit(x):
    hi = np.float32(x)
    lo = np.float32(np.float64(x) - np.float64(hi))
    return float(hi), float(lo)



MSK = -1.0e30


def get_slopes(n):
    import math

    def pow2(n):
        start = 2 ** (-(2 ** -(math.log2(n) - 3)))
        return [start * start ** i for i in range(n)]
    if math.log2(n).is_integer():
        return pow2(n)
    c = 2 ** math.floor(math.log2(n))
    return pow2(c) + get_slopes(2 * c)[0::2][:n - c]


class CFG:
    def __init__(self, SEQ=4096, DM=1024, INNER=4096, H=16, HD=64, QS=512):
        self.SEQ, self.DM, self.INNER, self.H, self.HD = SEQ, DM, INNER, H, HD
        self.OWN = SEQ // 2
        self.NT = SEQ // 128
        self.NTO = self.OWN // 128
        self.NC = DM // 128
        self.QS = QS
        assert DM % 128 == 0 and SEQ % 512 == 0 and H * HD == DM and HD == 64
        self.KE = 74 if DM >= 128 else 0


def host_stage(inputs, cfg):
    import numpy as np
    import ml_dtypes
    bf = ml_dtypes.bfloat16
    SEQ, DM, INNER, H = cfg.SEQ, cfg.DM, cfg.INNER, cfg.H
    u = np.ascontiguousarray(inputs['L0_attn_u'])
    B = u.shape[0]

    def T(x):
        return np.ascontiguousarray(np.asarray(x).T)

    shared = {
        'Bt': T(inputs['L0_attn_Bmat']),
        'Ct': T(inputs['L0_attn_Cmat']),
        'Dt': T(inputs['L0_attn_Dmat']),
        'Adiag': np.ascontiguousarray(np.asarray(inputs['L0_attn_A_diag']).reshape(cfg.NC, 128).T),
        'h0v': np.ascontiguousarray(np.asarray(inputs['L0_attn_h0']).reshape(cfg.NC, 128).T),
        'lg': np.ascontiguousarray(inputs['L0_attn_ln_gamma']),
        'lb': np.ascontiguousarray(inputs['L0_attn_ln_beta']),
        'W8t': T(inputs['L0_ffn_weight_q']),
        'wsc': np.ascontiguousarray(inputs['L0_ffn_weight_scale']),
        'fbias': np.ascontiguousarray(inputs['L0_ffn_bias']),
        'Wqt': np.ascontiguousarray(inputs['L1_attn_Wq']),
        'Wkt': np.ascontiguousarray(inputs['L1_attn_Wk']),
        'Wvt': np.ascontiguousarray(inputs['L1_attn_Wv']),
        'Wot': np.ascontiguousarray(inputs['L1_attn_Wo']),
        'bo': np.ascontiguousarray(inputs['L1_attn_bo']),
        'Wgt': T(inputs['L1_ffn_w_gate']),
        'Wut': T(inputs['L1_ffn_w_up']),
        'Wdt': T(inputs['L1_ffn_w_down']),
    }
    ident = np.zeros((128, 128), dtype=bf)
    np.fill_diagonal(ident, 1.0)
    shared['identb'] = ident
    identf = np.zeros((128, 128), dtype=np.float32)
    np.fill_diagonal(identf, 1.0)
    shared['identf'] = identf
    QSc = cfg.QS
    nblk = QSc // 128
    mb = np.zeros((nblk, 128, QSc), dtype=np.float32)
    for i in range(nblk):
        for b in range(nblk):
            blk = mb[i][:, b * 128:(b + 1) * 128]
            if b < i:
                blk[:] = MSK
            elif b == i:
                blk[:] = np.where(np.arange(128)[None, :] >= np.arange(128)[:, None],
                                  0.0, MSK)
    shared['maskb'] = mb.transpose(1, 0, 2).reshape(128, nblk * QSc).astype(bf)
    md = np.where(np.arange(128)[None, :] >= np.arange(128)[:, None], 0.0, MSK)
    shared['maskdiag'] = md.astype(np.float32)
    slopes = np.array(get_slopes(H), dtype=np.float64)

    maps = []
    for c in range(2 * B):
        b, half = c // 2, c % 2
        Q0 = cfg.OWN * half
        m = dict(shared)
        if half == 0:
            m['u'] = np.ascontiguousarray(
                np.concatenate([u[b][cfg.OWN:], u[b][:cfg.OWN]], axis=0))
        else:
            m['u'] = u[b]
        m['mflag'] = np.full((128, 1), float(half), dtype=np.float32)
        qpos = np.arange(Q0, Q0 + cfg.OWN, dtype=np.float64)
        RA = cfg.OWN * (1 - half)
        kslot_tok = (np.arange(SEQ) + RA) % SEQ
        kvalid = np.zeros(SEQ)
        if half == 0:
            kvalid[:cfg.OWN] = MSK
        qext = np.zeros((H, 10, cfg.OWN), dtype=np.float64)
        kext = np.zeros((H, 10, SEQ), dtype=np.float64)
        ihi, ilo = np.floor(qpos / 16.0), qpos % 16.0
        jhi, jlo = np.floor(kslot_tok / 16.0), kslot_tok % 16.0
        for h in range(H):
            sl = slopes[h]
            e = 2.0 * sl
            f = sl / 8.0
            e_hi = np.float32(e); e_lo = np.float32(e - np.float64(e_hi))
            f_hi = np.float32(f); f_lo = np.float32(f - np.float64(f_hi))
            qext[h, 1], qext[h, 2], qext[h, 3], qext[h, 4] = e_hi, e_lo, f_hi, f_lo
            qext[h, 5] = ihi; qext[h, 6] = ihi; qext[h, 7] = ilo; qext[h, 8] = ilo
            qext[h, 9] = 1.0
            kext[h, 0] = 1.0
            kext[h, 1] = jhi; kext[h, 2] = jhi; kext[h, 3] = jlo; kext[h, 4] = jlo
            g2 = -e; g3 = -f
            g2_hi = np.float32(g2); g2_lo = np.float32(g2 - np.float64(g2_hi))
            g3_hi = np.float32(g3); g3_lo = np.float32(g3 - np.float64(g3_hi))
            kext[h, 5], kext[h, 6], kext[h, 7], kext[h, 8] = g2_hi, g2_lo, g3_hi, g3_lo
            kext[h, 9] = kvalid
        for h in range(H):
            for (r_hi, r_lo) in ((1, 2), (3, 4)):
                v = qext[h, r_hi] + qext[h, r_lo]
                hi = np.asarray(v, dtype=bf).astype(np.float64)
                qext[h, r_hi] = hi
                qext[h, r_lo] = v - hi
            for (r_hi, r_lo) in ((5, 6), (7, 8)):
                v = kext[h, r_hi] + kext[h, r_lo]
                hi = np.asarray(v, dtype=bf).astype(np.float64)
                kext[h, r_hi] = hi
                kext[h, r_lo] = v - hi
        m['qext'] = qext.astype(np.float32)
        m['kext'] = kext.astype(np.float32)
        maps.append(m)
    return maps


def build(cfg, debug_out=None):
    nc = bass.Bass('TRN2', target_bir_lowering=False)
    SEQ, DM, INNER, H, HD = cfg.SEQ, cfg.DM, cfg.INNER, cfg.H, cfg.HD
    NT, NC, OWN, NTO, QS = cfg.NT, cfg.NC, cfg.OWN, cfg.NTO, cfg.QS
    LNEPS_O = np.float32(1e-5)
    LNEPS_I = np.float32(1e-5)
    FEPS = 1e-6

    HL = [(i * 512, min(512, DM - i * 512)) for i in range((DM + 511) // 512)]

    D = {}

    def din(name, shape, dt):
        D[name] = nc.dram_tensor(name, list(shape), dt, kind='ExternalInput').ap()
        return D[name]

    def dint(name, shape, dt):
        D[name] = nc.dram_tensor(name, list(shape), dt).ap()
        return D[name]

    din('u', [SEQ, DM], F32)
    din('Bt', [DM, DM], F32); din('Ct', [DM, DM], F32); din('Dt', [DM, DM], F32)
    din('Adiag', [128, NC], F32); din('h0v', [128, NC], F32)
    din('lg', [DM], F32); din('lb', [DM], F32)
    din('W8t', [DM, DM], I8); din('wsc', [DM], F32); din('fbias', [DM], F32)
    din('Wqt', [DM, DM], F32); din('Wkt', [DM, DM], F32)
    din('Wvt', [DM, DM], F32); din('Wot', [DM, DM], F32); din('bo', [DM], F32)
    din('Wgt', [DM, INNER], F32); din('Wut', [DM, INNER], F32)
    din('Wdt', [INNER, DM], F32)
    din('identb', [128, 128], BF16)
    din('identf', [128, 128], F32)
    din('maskdiag', [128, 128], F32)
    din('maskb', [128, (QS // 128) * QS], BF16)
    din('qext', [H, 10, OWN], F32)
    din('kext', [H, 10, SEQ], F32)
    din('mflag', [128, 1], F32)

    out_ap = nc.dram_tensor('out', [OWN, DM], F32, kind='ExternalOutput').ap()
    out_names = ['out']
    dbg_ap = None
    if debug_out:
        dbg_ap = nc.dram_tensor('dbg', [SEQ, DM], F32, kind='ExternalOutput').ap()
        out_names.append('dbg')

    dint('qTd', [DM, OWN], BF16)
    dint('x1d', [SEQ, DM], F32)
    dint('x2d', [SEQ, DM], F32)
    dint('x3d', [OWN, DM], F32)
    dint('kppd', [H, cfg.KE, SEQ], BF16)
    dint('vppd', [SEQ, DM], BF16)
    dint('Tgd', [DM, INNER], BF16)
    dint('Tud', [DM, INNER], BF16)
    dint('Tdd', [INNER, DM], BF16)
    dint('hd', [OWN, INNER], BF16)
    dint('hd2', [OWN, INNER], BF16)

    pb = PB(nc)
    S = pb.sbuf
    E = pb.emit

    X = S('X', [128, NT * DM], BF16)
    _sp = S('spool', [128, NT * 16 + NC * 4 + 128 * 3 + 280], F32)
    _spc = [0]

    def SP(w, p0=128):
        o = _spc[0]
        _spc[0] += w
        assert _spc[0] <= _sp.shape[1]
        return _sp[0:p0, o:o + w]

    stat = SP(NT * 12)
    mv = SP(NT * 2)
    rs_t = SP(NT)
    nmrs = SP(NT)
    sq = SP(NT)
    amax = SP(NT)
    qrec = SP(NT)
    qscl = SP(NT)

    bc1 = S('bc1', [128, DM], BF16)
    bc2 = S('bc2', [128, DM], BF16)
    gbc, bbc, wscbc, fbbc, bobc = bc1, bc2, bc1, bc2, bc1
    identb = S('identb_s', [128, 128], BF16)
    adec = SP(NC)
    h0s = SP(NC)
    zrow = S('zrow', [128, 1], BF16)
    mflag = S('mflag_s', [128, 1], F32)

    WA = S('WA', [128, NC * DM], BF16)
    WB = S('WB', [128, NC * DM], BF16)

    z0f0_ = S('z0f0', [128, DM], BF16)
    z0f = [z0f0_, z0f0_]
    z1f0 = S('z1f0', [128, DM], F32)
    z1f = [z1f0, z1f0]
    xf_a = S('xf_a', [128, DM], F32)
    qws = S('qws', [128, DM], F32)
    xf_b = z1f0
    zt0 = S('zt0', [128, NC * 128], BF16)
    zt = [zt0, zt0]
    acc0_ = S('acc0', [128, DM], BF16)
    acc_t = [acc0_, acc0_]
    nrm0 = S('nrm0', [128, DM], BF16)
    nrm_t = [nrm0, nrm0]
    res0 = acc0_
    res_t = [res0, res0]
    m320 = S('m320', [128, 512], U32)
    m32_t = [m320, m320]
    z0Ts0 = S('z0Ts0', [128, NC * 512], BF16)
    z0Ts = [z0Ts0, z0Ts0]

    buT = S('buT', [128, NC * SEQ], BF16)
    xf_a2 = S('xf_a2', [128, DM], F32)
    z1fb = S('z1fb', [128, DM], F32)
    qws2 = qws
    IH0 = INNER // 2
    _arena_need = 4 * DM + 2 * IH0 + 2 * INNER
    _use_arena = (NC * OWN) >= _arena_need
    _aoff = [0]

    def bview(w, nm):
        if _use_arena:
            o = _aoff[0]
            _aoff[0] += w
            return buT[:, o:o + w]
        return S(nm, [128, w], BF16)[:]

    q_rv = bview(DM, 'q_rA')
    q_dv = bview(DM, 'q_dA')
    q_cv = bview(DM, 'q_cA')
    xq_tv = bview(DM, 'xq_tA')
    gtile = bview(IH0, 'gtileA')
    util = bview(IH0, 'utilA')
    htile = bview(INNER, 'htileA')
    hqT = bview(INNER, 'hqTA')
    HALF0 = NC * OWN

    def bview2(off, w, nm):
        if _use_arena:
            return buT[:, HALF0 + off:HALF0 + off + w]
        return S(nm, [128, w], BF16)[:]

    zt2v = bview2(0, NC * 128, 'zt2A')
    xq_t2v = bview2(DM, DM, 'xq2A')
    q_c2v = bview2(2 * DM, DM, 'c2A')
    acc2_ = bview2(3 * DM, DM, 'acc2A')
    nrm2_ = bview2(4 * DM, DM, 'nrm2A')
    htile2 = bview2(5 * DM, INNER, 'htile2A')
    hqT2 = bview2(5 * DM + INNER, INNER, 'hqT2A')
    PAR1KEYS = ['zt1k', 'xq1k', 'qc1k', 'acc1k', 'nrm1k', 'htile1', 'hqT1_0', 'hqT1_1']

    q_v = [z1f0, z1fb]
    q_r = [q_rv, q_rv]
    q_d = [q_dv, q_dv]
    q_c = [q_cv, q_c2v]
    xq_t = [xq_tv, xq_t2v]
    qws_t = [qws, qws2]
    xfa_t = [xf_a, xf_a2]
    q_i = [None, None]
    zt[1] = zt2v
    acc_t[1] = acc2_
    nrm_t[1] = nrm2_
    z1f[1] = z1fb

    ps_big = pb.psum('ps_big', [128, 1024], F32)
    ps_y = pb.psum('ps_y', [128, DM], F32)
    ps_t = pb.psum('ps_t', [128, 512], F32)
    ps_c = pb.psum('ps_c', [128, 512], F32)
    ps_tb = pb.psum('ps_tb', [128, NC * 128], BF16)

    ur = D['u'].rearrange('(t p) d -> p t d', p=128)
    Xr = X[:].rearrange('p (t d) -> p t d', d=DM)
    ng = 4
    for g in range(ng):
        t0, t1 = g * NT // ng, (g + 1) * NT // ng
        E('gpsimd', lambda h, t0=t0, t1=t1: h.dma_start(out=Xr[:, t0:t1], in_=ur[:, t0:t1]),
          writes=[f'X{t}' for t in range(t0, t1)], dma=True)

    E('sync', lambda h: h.dma_start(out=mflag[:], in_=D['mflag'][:]), writes=['mflag'], dma=True)
    E('sync', lambda h: h.dma_start(out=identb[:], in_=D['identb'][:]), writes=['identb'], dma=True)

    E('sync', lambda h: h.dma_start(out=adec[:], in_=D['Adiag'][:]), writes=['adec'], dma=True)
    E('sync', lambda h: h.dma_start(out=h0s[:], in_=D['h0v'][:]), writes=['h0s'], dma=True)
    E('vector', lambda h: h.memset(zrow[:], 0.0), writes=['zrow'])

    def bcast_load(dst, src_name, key, dt):
        src = D[src_name]
        srcb = bass.AP(src.tensor, 0, [[0, 128], [1, DM]])
        E('gpsimd', lambda h: h.dma_start(out=dst[:], in_=srcb), writes=[key], dma=True)
    bcast_load(gbc, 'lg', 'gbc', BF16)
    bcast_load(bbc, 'lb', 'bbc', BF16)

    E('scalar', lambda h: h.activation(out=adec[:], in_=adec[:], func=AF.Exp),
      reads=['adec'], writes=['adec'])
    E('scalar', lambda h: h.activation(out=adec[:], in_=adec[:], func=AF.Ln, bias=np.float32(1.0), scale=np.float32(1.0)),
      reads=['adec'], writes=['adec'])
    E('vector', lambda h: h.tensor_single_scalar(adec[:], adec[:], -1.0, ALU.mult),
      reads=['adec'], writes=['adec'])

    def wload(dst, src_name, key, width):
        src = D[src_name].rearrange('(c p) o -> p c o', p=128)
        E('gpsimd', lambda h: h.dma_start(
            out=dst[:].rearrange('p (c o) -> p c o', o=width), in_=src),
          writes=[key], dma=True)
    wload(WA, 'Bt', 'WA', DM)
    wload(WB, 'Ct', 'WB', DM)

    def ln_stats(src_key_fn, src_ap_fn, tag, n=None):
        n = n if n is not None else NT
        for t in range(n):
            ap = src_ap_fn(t)
            for hh, (c0, cw) in enumerate(HL):
                E('vector', lambda h, t=t, hh=hh, c0=c0, cw=cw, ap=ap: h.bn_stats(
                    stat[:, t * 12 + hh * 6: t * 12 + hh * 6 + 6],
                    ap[:, c0:c0 + cw]),
                  reads=[src_key_fn(t)], writes=[f'st{t}'])
            E('vector', lambda h, t=t: h.bn_aggr(
                mv[:, t * 2: t * 2 + 2], stat[:, t * 12: t * 12 + 6 * len(HL)]),
              reads=[f'st{t}'], writes=[f'mv{t}'])
        mvr = [f'mv{t}' for t in range(n)]
        E('scalar', lambda h: h.activation(out=sq[:], in_=mv[:, 1::2], func=AF.Sqrt,
                                           bias=LNEPS_O, scale=np.float32(1.0)),
          reads=mvr, writes=['sq'])
        E('vector', lambda h: h.reciprocal(rs_t[:], sq[:]), reads=['sq'], writes=['rs'])
        E('vector', lambda h: h.scalar_tensor_tensor(nmrs[:], mv[:, 0::2], -1.0, rs_t[:],
                                                     ALU.mult, ALU.mult),
          reads=mvr + ['rs'], writes=['nmrs'])

    def apply_ln(dst, dst_key, t, src_ap, src_key, func=AF.Identity):
        E('scalar', lambda h: h.activation(out=dst[:], in_=src_ap, func=func,
                                           bias=nmrs[:, t:t + 1], scale=rs_t[:, t:t + 1]),
          reads=[src_key, 'rs', 'nmrs'], writes=[dst_key])

    gstat2 = SP(12)
    gmv2 = SP(2)
    gsq2 = SP(1)
    grs2 = SP(1)
    gnm2 = SP(1)

    def tile_stats(ap_, key_, eps, par=0):
        st_, mv_, sq_, rs_, nm_ = ((ginner_stat, gmv, gsq, grs, gnm) if par == 0
                                   else (gstat2, gmv2, gsq2, grs2, gnm2))
        sfx = '' if par == 0 else '2'
        for hh, (c0, cw) in enumerate(HL):
            E('vector', lambda h, hh=hh, c0=c0, cw=cw, st_=st_: h.bn_stats(
                st_[:, hh * 6:hh * 6 + 6], ap_[:, c0:c0 + cw]),
              reads=[key_], writes=['gst' + sfx])
        E('vector', lambda h, st_=st_, mv_=mv_: h.bn_aggr(mv_[:], st_[:, :6 * len(HL)]),
          reads=['gst' + sfx], writes=['gmv' + sfx])
        E('scalar', lambda h, eps=eps, mv_=mv_, sq_=sq_: h.activation(
            out=sq_[:], in_=mv_[:, 1:2], func=AF.Sqrt, bias=eps, scale=np.float32(1.0)),
          reads=['gmv' + sfx], writes=['gsq' + sfx])
        E('vector', lambda h, sq_=sq_, rs_=rs_: h.reciprocal(rs_[:], sq_[:]),
          reads=['gsq' + sfx], writes=['grs' + sfx])
        E('vector', lambda h, mv_=mv_, rs_=rs_, nm_=nm_: h.scalar_tensor_tensor(
            nm_[:], mv_[:, 0:1], -1.0, rs_[:], ALU.mult, ALU.mult),
          reads=['gmv' + sfx, 'grs' + sfx], writes=['gnm' + sfx])
        return rs_, nm_, 'grs' + sfx, 'gnm' + sfx

    def quant_round(i, vv_, vkey, K=1):
        r, d, xq, xi = q_r[i], q_d[i], xq_t[i], q_i[i]
        rk, dk, xqk, xik = ('q_rv', 'q_dv', 'xq_tv', 'q_iv')
        for (c0, cw) in HL:
            sl = slice(c0, c0 + cw)
            E('vector', lambda h, sl=sl, cw=cw: h.tensor_copy(xi[:, 0:cw], vv_[:, sl]),
              reads=[vkey], writes=[xik])
            E('vector', lambda h, sl=sl, cw=cw, K=K: h.tensor_single_scalar(
                r[:, sl], xi[:, 0:cw], float(-K), ALU.add),
              reads=[xik], writes=[rk])
        E('vector', lambda h: h.scalar_tensor_tensor(d, r, -1.0, vv_[:],
                                                     ALU.mult, ALU.add),
          reads=[rk, vkey], writes=[dk])
        for j in range(1, 2 * K + 1):
            th = float(j) - 0.5
            if j < 2 * K:
                E('vector', lambda h, th=th: h.scalar_tensor_tensor(
                    r, d, th, r, ALU.is_gt, ALU.add),
                  reads=[dk, rk], writes=[rk])
            else:
                E('vector', lambda h, th=th: h.scalar_tensor_tensor(
                    xq, d, th, r, ALU.is_gt, ALU.add),
                  reads=[dk, rk], writes=[xqk])
        return xq, xqk

    def quant_round_full(i, vv_, vkey, wsave=None, wk2=None, eng='vector'):
        xq, xqk = xq_t[i], f'xq{i}k'
        c, ck = q_c[i], f'qc{i}k'
        if wsave is None:
            wsave, wk2 = qws_t[i], f'qws{i}k'
        E(eng, lambda h: h.tensor_single_scalar(vv_[:], vv_[:], 128.5, ALU.add),
          reads=[vkey], writes=[vkey])
        if eng == 'vector':
            E(eng, lambda h, wsave=wsave: h.tensor_copy(wsave[:], vv_[:]),
              reads=[vkey], writes=[wk2])
            for s in (128.0, 64.0, 32.0, 16.0, 8.0, 4.0, 2.0, 1.0):
                E(eng, lambda h, s=s: h.tensor_single_scalar(c, vv_[:], s, ALU.is_ge),
                  reads=[vkey], writes=[ck])
                E(eng, lambda h, s=s: h.scalar_tensor_tensor(
                    vv_[:], c, -s, vv_[:], ALU.mult, ALU.add),
                  reads=[ck, vkey], writes=[vkey])
            E(eng, lambda h, wsave=wsave: h.tensor_single_scalar(
                wsave[:], wsave[:], -128.0, ALU.add),
              reads=[wk2], writes=[wk2])
            E(eng, lambda h, wsave=wsave: h.scalar_tensor_tensor(
                xq, vv_[:], -1.0, wsave[:], ALU.mult, ALU.add),
              reads=[vkey, wk2], writes=[xqk])
        else:
            E(eng, lambda h: h.tensor_scalar(xq, vv_[:], 0.0, -128.0,
                                             ALU.mult, ALU.add),
              reads=[vkey], writes=[xqk])
            for s in (128.0, 64.0, 32.0, 16.0, 8.0, 4.0, 2.0, 1.0):
                E(eng, lambda h, s=s: h.tensor_scalar(c, vv_[:], s, -s,
                                                      ALU.is_ge, ALU.mult),
                  reads=[vkey], writes=[ck])
                E(eng, lambda h, s=s: h.tensor_tensor(vv_[:], vv_[:], c, ALU.add),
                  reads=[ck, vkey], writes=[vkey])
                E(eng, lambda h, s=s: h.tensor_tensor(xq, xq, c, ALU.subtract),
                  reads=[ck, xqk], writes=[xqk])
        return xq, xqk

    regions = {}

    def wkey(region, key):
        regions.setdefault(region, []).append(key)
        return key

    def rkeys(region):
        return list(regions.get(region, []))

    def _dbg_exit():
        E('vector', lambda h: h.tensor_copy(z1f0[:], Xt(0)), reads=['X0'], writes=['z1fk'])
        E('sync', lambda h: h.dma_start(out=out_ap[0:128, :], in_=z1f0[:]),
          reads=['z1fk'], dma=True)
        pb.wait_all_done('sync')
        pb.build()
        return nc, out_names

    def Xt(t):
        return X[:, t * DM:(t + 1) * DM]

    if debug_out == 'p0':
        return _dbg_exit()
    ln_stats(lambda t: f'X{t}', Xt, 'ln0')
    if debug_out == 'ln0':
        return _dbg_exit()

    nslice = SEQ // 512
    for ts in range(nslice):
        zb = z0Ts[0]
        zbk = 'z0Ts0'
        for i in range(4):
            t = ts * 4 + i
            zf = z0f[0]
            apply_ln(zf, 'z0f0k', t, Xt(t), f'X{t}')
            for c in range(NC):
                E('tensor', lambda h, c=c, zf=zf: h.transpose(
                    ps_tb[:, c * 128:(c + 1) * 128], zf[:, c * 128:(c + 1) * 128], identb[:]),
                  reads=['z0f0k', 'identb'], writes=['ps_tb'],
                  inc=(c == NC - 1))
            E('scalar', lambda h, i=i, zb=zb: h.activation(
                out=zb[:].rearrange('p (c q) -> p c q', q=512)[:, :, i * 128:(i + 1) * 128],
                in_=ps_tb[:, :NC * 128].rearrange('p (c q) -> p c q', q=128),
                func=AF.Copy),
              reads=['ps_tb'], writes=[f'{zbk}_{i}'])
        for j in range(NC):
            pst, pstk = (ps_t, 'ps_t') if j % 2 == 0 else (ps_c, 'ps_c')
            for kc in range(NC):
                E('tensor', lambda h, j=j, kc=kc, zb=zb, pst=pst: h.matmul(
                    pst[:, :512], WA[:, kc * DM + j * 128: kc * DM + (j + 1) * 128],
                    zb[:, kc * 512:(kc + 1) * 512],
                    start=(kc == 0), stop=(kc == NC - 1)),
                  reads=['WA'] + [f'{zbk}_{i}' for i in range(4)],
                  writes=[pstk], inc=(kc == NC - 1))
            E('scalar', lambda h, j=j, ts=ts, pst=pst: h.activation(
                out=buT[:, j * SEQ + ts * 512: j * SEQ + (ts + 1) * 512],
                in_=pst[:, :512], func=AF.Copy),
              reads=[pstk], writes=[f'bu{j}_{ts}'])

    if debug_out == 'bu':
        return _dbg_exit()
    if debug_out == 'buc':
        for j in range(NC):
            for q in range(SEQ // DM):
                E('vector', lambda h, j=j, q=q: h.tensor_copy(
                    z1f0[:], buT[:, j * SEQ + q * DM: j * SEQ + (q + 1) * DM]),
                  reads=[f'bu{j}_{ts}' for ts in range(nslice)], writes=['z1fk'])
                E('sync', lambda h, j=j, q=q: h.dma_start(
                    out=dbg_ap[(j * (SEQ // DM) + q) * 128:(j * (SEQ // DM) + q + 1) * 128, :],
                    in_=z1f0[:]),
                  reads=['z1fk'], dma=True)
        pb.wait_all_done('sync')
        pb.build()
        return nc, out_names
    L2 = SEQ // 2
    PW = min(L2, 1024)
    if _use_arena:
        pwt = z0Ts0[:, 0:PW]
    else:
        pwt = S('pwt', [128, PW], BF16)[:]
    RW = min(PW, DM)
    lnab = SP(NC)
    sc_end = SP(2)
    sc_ini = SP(2)
    sc_tmp = SP(2)
    aL2 = SP(NC)
    mneg = SP(1)

    E('scalar', lambda h: h.activation(out=lnab[:], in_=adec[:], func=AF.Ln,
                                       scale=np.float32(-1.0)),
      reads=['adec'], writes=['lnab'])
    E('vector', lambda h: h.tensor_single_scalar(aL2[:], lnab[:], float(L2), ALU.mult),
      reads=['lnab'], writes=['aL2'])
    E('scalar', lambda h: h.activation(out=aL2[:], in_=aL2[:], func=AF.Exp),
      reads=['aL2'], writes=['aL2'])
    E('vector', lambda h: h.engine_nop(),
      reads=['z0Ts0_0', 'z0Ts0_1', 'z0Ts0_2', 'z0Ts0_3'], writes=['pwt'])
    E('vector', lambda h: h.tensor_single_scalar(mneg[:], mflag[:], -1.0, ALU.mult),
      reads=['mflag'], writes=['mneg'])
    E('vector', lambda h: h.tensor_single_scalar(mneg[:], mneg[:], 1.0, ALU.add),
      reads=['mneg'], writes=['mneg'])
    for j in range(NC):
        burd = [f'bu{j}_{ts}' for ts in range(nslice)]
        for hf in range(2):
            E('vector', lambda h, j=j, hf=hf: h.tensor_tensor_scan(
                buT[:, j * SEQ + hf * L2: j * SEQ + (hf + 1) * L2],
                adec[:, j:j + 1].broadcast_to([128, L2]),
                buT[:, j * SEQ + hf * L2: j * SEQ + (hf + 1) * L2],
                0.0, ALU.mult, ALU.add),
              reads=burd + ['adec'], writes=[f'xsl{j}_{hf}'])
        E('vector', lambda h, j=j: h.tensor_copy(
            sc_end[:, 0:1], buT[:, j * SEQ + L2 - 1: j * SEQ + L2]),
          reads=[f'xsl{j}_0'], writes=['sc_end'])
        E('vector', lambda h, j=j: h.tensor_copy(
            sc_end[:, 1:2], buT[:, j * SEQ + SEQ - 1: j * SEQ + SEQ]),
          reads=[f'xsl{j}_1'], writes=['sc_end2'])
        E('vector', lambda h, j=j: h.tensor_scalar(
            sc_tmp[:, 0:2], sc_end[:, 0:2], aL2[:, j:j + 1], None, ALU.bypass),
          reads=['sc_end', 'sc_end2', 'aL2'], writes=['sc_tmpz'], inc=True) if False else None
        E('vector', lambda h, j=j: h.scalar_tensor_tensor(
            sc_tmp[:, 0:1], h0s[:, j:j + 1], aL2[:, j:j + 1], sc_end[:, 0:1],
            ALU.mult, ALU.add),
          reads=['h0s', 'aL2', 'sc_end'], writes=['sc_tmpL'])
        E('vector', lambda h, j=j: h.scalar_tensor_tensor(
            sc_tmp[:, 1:2], h0s[:, j:j + 1], aL2[:, j:j + 1], sc_end[:, 1:2],
            ALU.mult, ALU.add),
          reads=['h0s', 'aL2', 'sc_end2'], writes=['sc_tmpR'])
        E('vector', lambda h, j=j: h.tensor_scalar(
            sc_ini[:, 0:1], h0s[:, j:j + 1], mflag[:, 0:1], None, ALU.mult),
          reads=['h0s', 'mflag'], writes=['sc_iniL'])
        E('vector', lambda h, j=j: h.scalar_tensor_tensor(
            sc_ini[:, 0:1], sc_tmp[:, 1:2], mneg[:, 0:1], sc_ini[:, 0:1],
            ALU.mult, ALU.add),
          reads=['sc_tmpR', 'mneg', 'sc_iniL'], writes=['sc_iniL'])
        E('vector', lambda h, j=j: h.tensor_scalar(
            sc_ini[:, 1:2], h0s[:, j:j + 1], mneg[:, 0:1], None, ALU.mult),
          reads=['h0s', 'mneg'], writes=['sc_iniR'])
        E('vector', lambda h, j=j: h.scalar_tensor_tensor(
            sc_ini[:, 1:2], sc_tmp[:, 0:1], mflag[:, 0:1], sc_ini[:, 1:2],
            ALU.mult, ALU.add),
          reads=['sc_tmpL', 'mflag', 'sc_iniR'], writes=['sc_iniR'])
        E('vector', lambda h: h.tensor_single_scalar(sc_tmp[:, 0:2], sc_ini[:, 0:2],
                                                     -1.0, ALU.mult),
          reads=['sc_iniL', 'sc_iniR'], writes=['sc_nin'])
        for sg in range(L2 // PW):
            for rh in range(PW // RW):
                E('gpsimd', lambda h, sg=sg, rh=rh: h.iota(
                    z1f0[:, 0:RW], [[1, RW]], base=1 + sg * PW + rh * RW,
                    channel_multiplier=0, allow_small_or_imprecise_dtypes=True),
                  writes=['z1fk'])
                E('scalar', lambda h, j=j, rh=rh: h.activation(
                    out=pwt[:, rh * RW:(rh + 1) * RW], in_=z1f0[:, 0:RW], func=AF.Exp,
                    scale=lnab[:, j:j + 1]),
                  reads=['z1fk', 'lnab'], writes=['pwt'])
            for hf in range(2):
                col = hf
                sb_ = buT[:, j * SEQ + hf * L2 + sg * PW: j * SEQ + hf * L2 + (sg + 1) * PW]
                E('vector', lambda h, j=j, hf=hf, col=col, sb_=sb_: h.scalar_tensor_tensor(
                    sb_[:, 0::2], pwt[:, 0::2], sc_tmp[:, col:col + 1], sb_[:, 0::2],
                    ALU.mult, ALU.add),
                  reads=['pwt', 'sc_nin', f'xsl{j}_{hf}'], writes=[f'xsl{j}_{hf}'])
                E('vector', lambda h, j=j, hf=hf, col=col, sb_=sb_: h.scalar_tensor_tensor(
                    sb_[:, 1::2], pwt[:, 1::2], sc_ini[:, col:col + 1], sb_[:, 1::2],
                    ALU.mult, ALU.add),
                  reads=['pwt', 'sc_iniL', 'sc_iniR', f'xsl{j}_{hf}'],
                  writes=[f'xsl{j}_{hf}'] + ([f'xs{j}'] if hf == 1 and sg == L2 // PW - 1 else []))

    if debug_out == 'scan':
        return _dbg_exit()
    if debug_out == 'xs':
        for j in range(NC):
            for q in range(SEQ // DM):
                E('vector', lambda h, j=j, q=q: h.tensor_copy(
                    z1f0[:], buT[:, j * SEQ + q * DM: j * SEQ + (q + 1) * DM]),
                  reads=[f'xs{j}'], writes=['z1fk'])
                E('sync', lambda h, j=j, q=q: h.dma_start(
                    out=dbg_ap[(j * (SEQ // DM) + q) * 128:(j * (SEQ // DM) + q + 1) * 128, :],
                    in_=z1f0[:]),
                  reads=['z1fk'], dma=True)
        pb.wait_all_done('sync')
        pb.build()
        return nc, out_names

    wload(WA, 'Dt', 'WA', DM)
    ginner_stat = SP(12)
    gmv = SP(2)
    gsq = SP(1)
    grs = SP(1)
    gnm = SP(1)
    for t in range(NT):
        zf = z0f[0]
        zc = zt[0]
        zck = 'zt0k'
        apply_ln(zf, 'z0f0k', t, Xt(t), f'X{t}')
        for c in range(NC):
            E('tensor', lambda h, c=c, zf=zf: h.transpose(
                ps_tb[:, c * 128:(c + 1) * 128], zf[:, c * 128:(c + 1) * 128], identb[:]),
              reads=['z0f0k', 'identb'], writes=['ps_tb'],
              inc=(c == NC - 1))
        E('scalar', lambda h, zc=zc: h.activation(out=zc[:, :NC * 128], in_=ps_tb[:, :NC * 128],
                                                  func=AF.Copy),
          reads=['ps_tb'], writes=[zck])
        for j in range(NC):
            for hi_, (c0, cw) in enumerate(HL):
                E('tensor', lambda h, j=j, t=t, c0=c0, cw=cw: h.matmul(
                    ps_y[:, c0:c0 + cw], buT[:, j * SEQ + t * 128: j * SEQ + (t + 1) * 128],
                    WB[:, j * DM + c0: j * DM + c0 + cw],
                    start=(j == 0), stop=False),
                  reads=[f'xs{j}', 'WB'], writes=['ps_y'], inc=False)
        for j in range(NC):
            for hi_, (c0, cw) in enumerate(HL):
                E('tensor', lambda h, j=j, zc=zc, c0=c0, cw=cw: h.matmul(
                    ps_y[:, c0:c0 + cw], zc[:, j * 128:(j + 1) * 128],
                    WA[:, j * DM + c0: j * DM + c0 + cw],
                    start=False, stop=(j == NC - 1)),
                  reads=[zck, 'WA'], writes=['ps_y'],
                  inc=(j == NC - 1 and hi_ == len(HL) - 1))
        at = acc_t[0]
        atk = 'acck'
        E('scalar', lambda h, at=at: h.activation(out=at[:], in_=ps_y[:, :DM], func=AF.Gelu),
          reads=['ps_y'], writes=[atk])
        for hh, (c0, cw) in enumerate(HL):
            E('vector', lambda h, hh=hh, c0=c0, cw=cw, at=at: h.bn_stats(
                ginner_stat[:, hh * 6:hh * 6 + 6], at[:, c0:c0 + cw]),
              reads=[atk], writes=['gst'])
        E('vector', lambda h: h.bn_aggr(gmv[:], ginner_stat[:, :6 * len(HL)]),
          reads=['gst'], writes=['gmv'])
        E('scalar', lambda h: h.activation(out=gsq[:], in_=gmv[:, 1:2], func=AF.Sqrt,
                                           bias=LNEPS_I, scale=np.float32(1.0)),
          reads=['gmv'], writes=['gsq'])
        E('vector', lambda h: h.reciprocal(grs[:], gsq[:]), reads=['gsq'], writes=['grs'])
        E('vector', lambda h: h.scalar_tensor_tensor(gnm[:], gmv[:, 0:1], -1.0, grs[:],
                                                     ALU.mult, ALU.mult),
          reads=['gmv', 'grs'], writes=['gnm'])
        nt_ = nrm_t[0]
        ntk = 'nrm0k'
        E('scalar', lambda h, at=at, nt_=nt_: h.activation(
            out=nt_[:], in_=at[:], func=AF.Identity, bias=gnm[:, 0:1], scale=grs[:, 0:1]),
          reads=[atk, 'grs', 'gnm'], writes=[ntk])
        E('vector', lambda h, nt_=nt_: h.tensor_tensor(nt_[:], nt_[:], gbc[:], ALU.mult),
          reads=[ntk, 'gbc'], writes=[ntk])
        E('vector', lambda h, nt_=nt_: h.tensor_tensor(nt_[:], nt_[:], bbc[:], ALU.add),
          reads=[ntk, 'bbc'], writes=[ntk])
        E('vector', lambda h, nt_=nt_: h.tensor_scalar(nt_[:], nt_[:], -1e6, 1e6,
                                                       ALU.max, ALU.min),
          reads=[ntk], writes=[ntk])
        mt = m32_t[0]
        for hh, (c0, cw) in enumerate(HL):
            E('vector', lambda h, nt_=nt_, mt=mt, c0=c0, cw=cw: h.tensor_tensor(
                mt[:, 0:cw], nt_[:, c0:c0 + cw], nt_[:, c0:c0 + cw], ALU.not_equal),
              reads=[ntk], writes=['m32k'])
            E('vector', lambda h, nt_=nt_, mt=mt, c0=c0, cw=cw: h.copy_predicated(
                nt_[:, c0:c0 + cw], mt[:, 0:cw], zrow[:, 0:1].broadcast_to([128, cw])),
              reads=['m32k', 'zrow', ntk], writes=[ntk])
        for hh, (c0, cw) in enumerate(HL):
            E('vector', lambda h, hh=hh, c0=c0, cw=cw, nt_=nt_: h.bn_stats(
                ginner_stat[:, hh * 6:hh * 6 + 6], nt_[:, c0:c0 + cw]),
              reads=[ntk], writes=['gst'])
        E('vector', lambda h: h.bn_aggr(gmv[:], ginner_stat[:, :6 * len(HL)]),
          reads=['gst'], writes=['gmv'])
        E('scalar', lambda h: h.activation(out=gsq[:], in_=gmv[:, 1:2], func=AF.Sqrt,
                                           bias=LNEPS_O, scale=np.float32(1.0)),
          reads=['gmv'], writes=['gsq'])
        E('vector', lambda h: h.reciprocal(grs[:], gsq[:]), reads=['gsq'], writes=['grs'])
        E('vector', lambda h: h.scalar_tensor_tensor(gnm[:], gmv[:, 0:1], -1.0, grs[:],
                                                     ALU.mult, ALU.mult),
          reads=['gmv', 'grs'], writes=['gnm'])
        rt = res_t[0]
        rtk = 'acck'
        E('scalar', lambda h, nt_=nt_, rt=rt: h.activation(
            out=rt[:], in_=nt_[:], func=AF.Identity, bias=gnm[:, 0:1], scale=grs[:, 0:1]),
          reads=[ntk, 'grs', 'gnm'], writes=[rtk])
        E('sync', lambda h, t=t: h.dma_start(out=xf_a[:], in_=D['u'][t * 128:(t + 1) * 128, :]),
          writes=['xfa'], dma=True)
        E('vector', lambda h, rt=rt: h.tensor_tensor(xf_a[:], xf_a[:], rt[:], ALU.add),
          reads=['xfa', rtk], writes=['xfa'])
        E('scalar', lambda h, t=t: h.dma_start(out=D['x1d'][t * 128:(t + 1) * 128, :],
                                               in_=xf_a[:]),
          reads=['xfa'], writes=[wkey('x1d', f'x1d{t}')], dma=True)
        E('vector', lambda h, rt=rt, t=t: h.tensor_tensor(Xt(t), Xt(t), rt[:], ALU.add),
          reads=[rtk, f'X{t}'], writes=[f'X{t}'])

    if debug_out == 'p1':
        return _dbg_exit()
    if debug_out == 'x1':
        for t in range(NT):
            E('sync', lambda h, t=t: h.dma_start(out=xf_a[:], in_=D['x1d'][t * 128:(t + 1) * 128, :]),
              reads=rkeys('x1d'), writes=['xfa'], dma=True)
            E('scalar', lambda h, t=t: h.dma_start(out=dbg_ap[t * 128:(t + 1) * 128, :], in_=xf_a[:]),
              reads=['xfa'], dma=True)
        pb.wait_all_done('sync')
        pb.build()
        return nc, out_names

    bcast_load(wscbc, 'wsc', 'gbc', BF16)
    bcast_load(fbbc, 'fbias', 'bbc', BF16)
    wload(WB, 'W8t', 'WB', DM)
    for t in range(NT):
        par = t % 2
        xfa_, xfak = xfa_t[par], ('xfa' if par == 0 else 'xfa1')
        zf32, zfk = z1f[par], ('z1fk' if par == 0 else 'z1fbk')
        zc, zck = zt[par], f'zt{par}k'
        at, atk = acc_t[par], ('acck' if par == 0 else 'acc1k')
        nt_, ntk = nrm_t[par], ('nrm0k' if par == 0 else 'nrm1k')
        psY, psYk = (ps_y, 'ps_y') if par == 0 else (ps_big, 'ps_big')
        E('sync', lambda h, t=t, xfa_=xfa_: h.dma_start(
            out=xfa_[:], in_=D['x1d'][t * 128:(t + 1) * 128, :]),
          reads=rkeys('x1d'), writes=[xfak], dma=True)
        rs_, nm_, rsk, nmk = tile_stats(xfa_, xfak, LNEPS_O, par)
        E('scalar', lambda h, zf32=zf32, xfa_=xfa_, rs_=rs_, nm_=nm_: h.activation(
            out=zf32[:], in_=xfa_[:], func=AF.Identity, bias=nm_[:, 0:1], scale=rs_[:, 0:1]),
          reads=[xfak, rsk, nmk], writes=[zfk])
        E('vector', lambda h, zf32=zf32, t=t: h.reduce_max(
            amax[:, t:t + 1], zf32[:], AX.X, apply_absolute_value=True),
          reads=[zfk], writes=[f'am{t}'])
        E('vector', lambda h, t=t: h.tensor_scalar(
            qscl[:, t:t + 1], amax[:, t:t + 1], FEPS, float(1.0 / 127.0), ALU.max, ALU.mult),
          reads=[f'am{t}'], writes=[f'qs{t}'])
        E('vector', lambda h, t=t: h.reciprocal(qrec[:, t:t + 1], qscl[:, t:t + 1]),
          reads=[f'qs{t}'], writes=[f'qr{t}'])
        vv_ = q_v[par]
        E('vector', lambda h, vv_=vv_, zf32=zf32, t=t: h.tensor_scalar(
            vv_[:], zf32[:], qrec[:, t:t + 1], None, ALU.mult),
          reads=[zfk, f'qr{t}'], writes=[zfk])
        xq = quant_round_full(par, vv_, zfk,
                              eng='vector' if par == 0 else 'gpsimd')
        for c in range(NC):
            E('tensor', lambda h, c=c, xq=xq[0]: h.transpose(
                ps_tb[:, c * 128:(c + 1) * 128], xq[:, c * 128:(c + 1) * 128], identb[:]),
              reads=[xq[1], 'identb'], writes=['ps_tb'], inc=(c == NC - 1))
        E('scalar', lambda h, zc=zc: h.activation(out=zc[:, :NC * 128],
                                                  in_=ps_tb[:, :NC * 128], func=AF.Copy),
          reads=['ps_tb'], writes=[zck])
        if debug_out == 'xq0':
            E('vector', lambda h, zc=zc, zf32=zf32: h.tensor_copy(zf32[:], zc[:, :NC * 128]),
              reads=[zck], writes=[zfk])
            E('sync', lambda h, t=t, zf32=zf32: h.dma_start(
                out=dbg_ap[t * 128:(t + 1) * 128, :], in_=zf32[:]),
              reads=[zfk], dma=True)
        for kc in range(NC):
            for hi_, (c0, cw) in enumerate(HL):
                E('tensor', lambda h, kc=kc, zc=zc, c0=c0, cw=cw, psY=psY: h.matmul(
                    psY[:, c0:c0 + cw], zc[:, kc * 128:(kc + 1) * 128],
                    WB[:, kc * DM + c0: kc * DM + c0 + cw],
                    start=(kc == 0), stop=(kc == NC - 1)),
                  reads=[zck, 'WB'], writes=[psYk],
                  inc=(kc == NC - 1 and hi_ == len(HL) - 1))
        E('scalar', lambda h, at=at, t=t, psY=psY: h.activation(
            out=at[:], in_=psY[:, :DM], func=AF.Identity, bias=np.float32(0.0),
            scale=qscl[:, t:t + 1]),
          reads=[psYk, f'qs{t}'], writes=[atk])
        E('vector', lambda h, at=at, nt_=nt_: h.scalar_tensor_tensor(
            nt_[:], at[:], 1.0, wscbc[:], ALU.mult, ALU.mult),
          reads=[atk, 'gbc'], writes=[ntk])
        E('vector', lambda h, nt_=nt_: h.tensor_tensor(nt_[:], nt_[:], fbbc[:], ALU.add),
          reads=[ntk, 'bbc'], writes=[ntk])
        rs_, nm_, rsk, nmk = tile_stats(nt_, ntk, LNEPS_O, par)
        E('scalar', lambda h, nt_=nt_, zf32=zf32, rs_=rs_, nm_=nm_: h.activation(
            out=zf32[:], in_=nt_[:], func=AF.Identity, bias=nm_[:, 0:1], scale=rs_[:, 0:1]),
          reads=[ntk, rsk, nmk], writes=[zfk])
        E('vector', lambda h, xfa_=xfa_, zf32=zf32: h.tensor_tensor(
            xfa_[:], xfa_[:], zf32[:], ALU.add),
          reads=[xfak, zfk], writes=[xfak])
        E('scalar', lambda h, t=t, xfa_=xfa_: h.dma_start(
            out=D['x2d'][t * 128:(t + 1) * 128, :], in_=xfa_[:]),
          reads=[xfak], writes=[wkey('x2d', f'x2d{t}')], dma=True)
        E('vector', lambda h, t=t, xfa_=xfa_: h.tensor_copy(Xt(t), xfa_[:]),
          reads=[xfak, f'X{t}'], writes=[f'X{t}'])

    if debug_out == 'x2':
        for t in range(NT):
            fo = z1f[0]
            E('vector', lambda h, fo=fo, t=t: h.tensor_copy(fo[:], Xt(t)),
              reads=[f'X{t}'], writes=['z1fk'])
            E('sync', lambda h, fo=fo, t=t: h.dma_start(
                out=dbg_ap[t * 128:(t + 1) * 128, :], in_=fo[:]),
              reads=['z1fk'], dma=True)
        E('sync', lambda h: h.dma_start(out=out_ap[0:128, :], in_=z1f[0][:]),
          reads=['z1f0'], dma=True)
        pb.wait_all_done('sync')
        pb.build()
        return nc, out_names

    _tw = max(INNER, DM)
    if _use_arena and NC * OWN + 3 * _tw <= NC * SEQ:
        HALF_ = NC * OWN
        twb0 = buT[:, HALF_:HALF_ + _tw]
        tc1 = buT[:, HALF_ + _tw:HALF_ + 2 * _tw]
        tc2 = buT[:, HALF_ + 2 * _tw:HALF_ + 3 * _tw]
    else:
        twb0 = S('twb0', [128, _tw], BF16)[:]
        tc1 = S('tc1', [128, _tw], BF16)[:]
        tc2 = S('tc2', [128, _tw], BF16)[:]
    twb = [twb0, twb0]
    tsum = SP(1)
    tsum2 = SP(1)
    tth = SP(1)
    tthn = SP(1)
    trow = SP(128, p0=1)
    identf = SP(128)
    wmg = SP(1)
    wmu = SP(1)
    wmd = SP(1)
    onescol = SP(1)
    onesrow = SP(128, p0=1)
    kmaxc = SP(1)
    kmx0 = SP(1, p0=1)
    E('sync', lambda h: h.dma_start(out=identf[:], in_=D['identf'][:]),
      writes=['identf'], dma=True)
    E('vector', lambda h: h.memset(onescol[:], 1.0), writes=['onescol'])
    E('vector', lambda h: h.memset(onesrow[:], 1.0), writes=['onesrow'])
    E('vector', lambda h: h.memset(kmaxc[:], 0.0), writes=['kmaxc'])

    tern_units = []

    def TU(fn):
        tern_units.append(fn)

    TU(lambda: E('vector', lambda h: h.engine_nop(),
                 reads=['zt1k', 'xq1k', 'qc1k', 'acc1k', 'nrm1k', 'z1fbk', 'xfa1',
                        'qws1k', 'acck', 'kslb', 'nrm0k'],
                 writes=['twbk', 'tc1', 'tc2']))

    for wsrc, tdst, wmt, wmk, nrow, ncol in (
            ('Wgt', 'Tgd', wmg, 'wmg', DM, INNER),
            ('Wut', 'Tud', wmu, 'wmu', DM, INNER),
            ('Wdt', 'Tdd', wmd, 'wmd', INNER, DM)):
        nch = nrow // 128
        TU(lambda: E('vector', lambda h: h.memset(tsum[:], 0.0), writes=['tsum']))
        for c in range(nch):
            def u1(c=c, wsrc=wsrc, ncol=ncol):
                tb, tbk = twb[c % 2], 'twbk'
                E('gpsimd', lambda h, c=c, tb=tb, wsrc=wsrc, ncol=ncol: h.dma_start(
                    out=tb[:, 0:ncol], in_=D[wsrc][c * 128:(c + 1) * 128, :]),
                  writes=[tbk], dma=True)
                E('vector', lambda h, tb=tb, ncol=ncol: h.tensor_reduce(
                    tsum2[:], tb[:, 0:ncol], AX.X, ALU.add, apply_absolute_value=True),
                  reads=[tbk], writes=['tsum2'])
                E('vector', lambda h: h.tensor_tensor(tsum[:], tsum[:], tsum2[:], ALU.add),
                  reads=['tsum', 'tsum2'], writes=['tsum'])
            TU(u1)
        def uder(wmt=wmt, wmk=wmk, nrow=nrow, ncol=ncol):
            E('tensor', lambda h: h.transpose(
                ps_t[:, 0:128], tsum[:, 0:1].broadcast_to([128, 128]), identf[:]),
              reads=['tsum', 'identf'], writes=['ps_t'])
            E('vector', lambda h: h.tensor_copy(trow[:], ps_t[0:1, 0:128]),
              reads=['ps_t'], writes=['trow'])
            E('vector', lambda h: h.tensor_reduce(tth[0:1, 0:1], trow[:], AX.X, ALU.add),
              reads=['trow'], writes=['tth1'])
            E('vector', lambda h, nrow=nrow, ncol=ncol: h.tensor_scalar(
                tth[0:1, 0:1], tth[0:1, 0:1], float(1.0 / (nrow * ncol)), 1e-5,
                ALU.mult, ALU.max),
              reads=['tth1'], writes=['tth1'])
            E('vector', lambda h, wmt=wmt: h.tensor_copy(wmt[0:1, 0:1], tth[0:1, 0:1]),
              reads=['tth1'], writes=[wmk + '0'])
            E('tensor', lambda h, wmt=wmt: h.matmul(ps_t[:, 0:1], onesrow[0:1, 0:128],
                                                    wmt[0:1, 0:1], start=True, stop=True),
              reads=[wmk + '0', 'onesrow'], writes=['ps_t'])
            E('vector', lambda h, wmt=wmt: h.tensor_copy(wmt[:, 0:1], ps_t[:, 0:1]),
              reads=['ps_t'], writes=[wmk])
            E('tensor', lambda h: h.matmul(ps_t[:, 0:1], onesrow[0:1, 0:128],
                                           tth[0:1, 0:1], start=True, stop=True),
              reads=['tth1', 'onesrow'], writes=['ps_t'])
            E('vector', lambda h: h.tensor_copy(tth[:, 0:1], ps_t[:, 0:1]),
              reads=['ps_t'], writes=['tthb'])
            E('vector', lambda h: h.tensor_scalar(tth[:, 0:1], tth[:, 0:1],
                                                  0.5, None, ALU.mult),
              reads=['tthb'], writes=['tthb'])
            E('vector', lambda h: h.tensor_single_scalar(tthn[:, 0:1], tth[:, 0:1], -1.0,
                                                         ALU.mult),
              reads=['tthb'], writes=['tthn'])
        TU(uder)
        jw = min(DM, ncol)
        for c in range(nch):
            def u2(c=c, wsrc=wsrc, tdst=tdst, ncol=ncol, jw=jw):
                for j in range(ncol // jw):
                    E('sync', lambda h, c=c, j=j, wsrc=wsrc, jw=jw: h.dma_start(
                        out=xf_a[:, 0:jw],
                        in_=D[wsrc][c * 128:(c + 1) * 128, j * jw:(j + 1) * jw]),
                      writes=['xfa'], dma=True)
                    E('vector', lambda h, j=j, jw=jw: h.tensor_scalar(
                        tc1[:, j * jw:(j + 1) * jw], xf_a[:, 0:jw], tth[:, 0:1],
                        None, ALU.is_gt),
                      reads=['xfa', 'tthb'], writes=['tc1'])
                    E('vector', lambda h, j=j, jw=jw: h.tensor_scalar(
                        tc2[:, j * jw:(j + 1) * jw], xf_a[:, 0:jw], tthn[:, 0:1],
                        None, ALU.is_lt),
                      reads=['xfa', 'tthn'], writes=['tc2'])
                E('vector', lambda h, ncol=ncol: h.tensor_tensor(
                    tc1[:, 0:ncol], tc1[:, 0:ncol], tc2[:, 0:ncol], ALU.subtract),
                  reads=['tc1', 'tc2'], writes=['tc1'])
                E('scalar', lambda h, c=c, tdst=tdst, ncol=ncol: h.dma_start(
                    out=D[tdst][c * 128:(c + 1) * 128, :], in_=tc1[:, 0:ncol]),
                  reads=['tc1'], writes=[wkey(tdst, f'{tdst}_{c}')], dma=True)
            TU(u2)

    if debug_out in ('tw', 'twu', 'twd'):
        for u in tern_units:
            u()
        tern_units = []
        src = {'tw': 'Tgd', 'twu': 'Tud', 'twd': 'Tdd'}[debug_out]
        nr = DM if src != 'Tdd' else INNER
        nc_ = INNER if src != 'Tdd' else DM
        for c in range(nr // 128):
            E('sync', lambda h, c=c, src=src, nc_=nc_: h.dma_start(
                out=twb0[:, 0:nc_], in_=D[src][c * 128:(c + 1) * 128, :]),
              reads=rkeys(src), writes=['twbk'], dma=True)
            for j in range(nc_ // DM):
                E('vector', lambda h, j=j: h.tensor_copy(xf_a[:], twb0[:, j * DM:(j + 1) * DM]),
                  reads=['twbk'], writes=['xfa'])
                E('scalar', lambda h, c=c, j=j, nc_=nc_: h.dma_start(
                    out=dbg_ap[(c * (nc_ // DM) + j) * 128:(c * (nc_ // DM) + j + 1) * 128, :],
                    in_=xf_a[:]),
                  reads=['xfa'], dma=True)
        pb.wait_all_done('sync')
        pb.build()
        return nc, out_names

    ln_stats(lambda t: f'X{t}', Xt, 'ln2')

    nslice = SEQ // 512
    ksl = zt2v[:, 0:512] if _use_arena else S('kslb', [128, 512], BF16)[:]
    vsl = acc_t[1]

    def build_z2Ts(ts):
        zb = z0Ts[0]
        zbk = 'z0Ts0'
        for i in range(4):
            t = ts * 4 + i
            zf = z0f[0]
            apply_ln(zf, 'z0f0k', t, Xt(t), f'X{t}')
            for c in range(NC):
                E('tensor', lambda h, c=c, zf=zf: h.transpose(
                    ps_tb[:, c * 128:(c + 1) * 128], zf[:, c * 128:(c + 1) * 128], identb[:]),
                  reads=['z0f0k', 'identb'], writes=['ps_tb'], inc=(c == NC - 1))
            E('scalar', lambda h, i=i, zb=zb: h.activation(
                out=zb[:].rearrange('p (c q) -> p c q', q=512)[:, :, i * 128:(i + 1) * 128],
                in_=ps_tb[:, :NC * 128].rearrange('p (c q) -> p c q', q=128),
                func=AF.Copy),
              reads=['ps_tb'], writes=[f'{zbk}_{i}'])
        return zb, [f'{zbk}_{i}' for i in range(4)]

    E('vector', lambda h: h.engine_nop(),
      reads=['pwt'], writes=['z0Ts0_0', 'z0Ts0_1', 'z0Ts0_2', 'z0Ts0_3'])
    wload(WA, 'Wkt', 'WA', DM)
    for ts in range(nslice):
        zb, zrd = build_z2Ts(ts)
        for oc in range(NC):
            alt = oc % 2 == 1 and DM >= 512
            pst, pstk = (ps_c, 'ps_c') if alt else (ps_t, 'ps_t')
            ksl_o, kslk = (vsl, 'acc1k') if alt else (ksl, 'kslb')
            for kc in range(NC):
                E('tensor', lambda h, oc=oc, kc=kc, zb=zb, pst=pst: h.matmul(
                    pst[:, :512], WA[:, kc * DM + oc * 128: kc * DM + (oc + 1) * 128],
                    zb[:, kc * 512:(kc + 1) * 512],
                    start=(kc == 0), stop=(kc == NC - 1)),
                  reads=['WA'] + zrd, writes=[pstk], inc=(kc == NC - 1))
            E('scalar', lambda h, pst=pst, ksl_o=ksl_o: h.activation(
                out=ksl_o[:, 0:512], in_=pst[:, :512], func=AF.Copy),
              reads=[pstk], writes=[kslk])
            E('vector', lambda h, ksl_o=ksl_o: h.tensor_reduce(
                tsum2[:], ksl_o[:, 0:512], AX.X, ALU.max, apply_absolute_value=True),
              reads=[kslk], writes=['tsum2'])
            E('vector', lambda h: h.tensor_tensor(kmaxc[:], kmaxc[:], tsum2[:], ALU.max),
              reads=['kmaxc', 'tsum2'], writes=['kmaxc'])
            for sub in range(2):
                E('scalar', lambda h, oc=oc, sub=sub, ts=ts, ksl_o=ksl_o: h.dma_start(
                    out=D['kppd'][2 * oc + sub, 0:64, ts * 512:(ts + 1) * 512],
                    in_=ksl_o[64 * sub:64 * sub + 64, 0:512]),
                  reads=[kslk],
                  writes=[wkey('kppd', f'kpp{ts}_{oc}_{sub}')], dma=True)

    wload(WB, 'Wvt', 'WB', DM)
    for ts in range(nslice):
        zb, zrd = build_z2Ts(ts)
        for i in range(4):
            psd, psk = (ps_y, 'ps_y') if i % 2 == 0 else (ps_big, 'ps_big')
            for kc in range(NC):
                for hi_, (c0, cw) in enumerate(HL):
                    E('tensor', lambda h, i=i, kc=kc, zb=zb, c0=c0, cw=cw, psd=psd: h.matmul(
                        psd[:, c0:c0 + cw], zb[:, kc * 512 + i * 128: kc * 512 + (i + 1) * 128],
                        WB[:, kc * DM + c0: kc * DM + c0 + cw],
                        start=(kc == 0), stop=(kc == NC - 1)),
                      reads=['WB'] + zrd, writes=[psk],
                      inc=(kc == NC - 1 and hi_ == len(HL) - 1))
            vsl_i, vslk = (vsl, 'acc1k') if i % 2 == 0 else (nrm_t[0], 'nrm0k')
            E('scalar', lambda h, psd=psd, vsl_i=vsl_i: h.activation(
                out=vsl_i[:, :DM], in_=psd[:, :DM], func=AF.Copy),
              reads=[psk], writes=[vslk])
            E('scalar', lambda h, i=i, ts=ts, vsl_i=vsl_i: h.dma_start(
                out=D['vppd'][ts * 512 + i * 128: ts * 512 + (i + 1) * 128, :],
                in_=vsl_i[:, :DM]),
              reads=[vslk], writes=[wkey('vppd', f'vpp{ts}_{i}')], dma=True)

    wload(WA, 'Wqt', 'WA', DM)
    E('vector', lambda h: h.tensor_single_scalar(WA[:], WA[:], 0.125, ALU.mult),
      reads=['WA'], writes=['WA'])
    for ts in range(nslice):
        ov0 = max(ts * 512, SEQ // 2)
        ov1 = min((ts + 1) * 512, SEQ)
        if ov1 <= ov0:
            continue
        zb, zrd = build_z2Ts(ts)
        lo = ov0 - ts * 512
        w_ = ov1 - ov0
        qo = ov0 - SEQ // 2
        for oc in range(NC):
            alt = oc % 2 == 1 and DM >= 512
            pst, pstk = (ps_c, 'ps_c') if alt else (ps_t, 'ps_t')
            ksl_o, kslk = (vsl, 'acc1k') if alt else (ksl, 'kslb')
            for kc in range(NC):
                E('tensor', lambda h, oc=oc, kc=kc, zb=zb, pst=pst: h.matmul(
                    pst[:, :512], WA[:, kc * DM + oc * 128: kc * DM + (oc + 1) * 128],
                    zb[:, kc * 512:(kc + 1) * 512],
                    start=(kc == 0), stop=(kc == NC - 1)),
                  reads=['WA'] + zrd, writes=[pstk], inc=(kc == NC - 1))
            E('scalar', lambda h, pst=pst, ksl_o=ksl_o: h.activation(
                out=ksl_o[:, 0:512], in_=pst[:, :512], func=AF.Copy),
              reads=[pstk], writes=[kslk])
            E('scalar', lambda h, oc=oc, qo=qo, lo=lo, w_=w_, ksl_o=ksl_o: h.dma_start(
                out=D['qTd'][oc * 128:(oc + 1) * 128, qo:qo + w_],
                in_=ksl_o[:, lo:lo + w_]),
              reads=[kslk], writes=[wkey('qTd', f'qT{ts}_{oc}')], dma=True)

    E('tensor', lambda h: h.transpose(
        ps_t[:, 0:128], kmaxc[:, 0:1].broadcast_to([128, 128]), identf[:]),
      reads=['kmaxc', 'identf'], writes=['ps_t'])
    E('vector', lambda h: h.tensor_copy(trow[:], ps_t[0:1, 0:128]),
      reads=['ps_t'], writes=['trow'])
    E('vector', lambda h: h.tensor_reduce(kmx0[0:1, 0:1], trow[:], AX.X, ALU.max),
      reads=['trow'], writes=['kmx0'])

    crow = z1f0[0:1, 512:1024] if DM >= 1024 else S('crow', [1, 512], F32)[0:1, :]
    crowb = zt2v[0:1, 512:1024] if _use_arena else S('crowb', [1, 512], BF16)[0:1, :]

    def qpp(h_):
        return X[0:74, h_ * OWN:(h_ + 1) * OWN]
    qpr = {}
    for h_ in range(H):
        oc = (h_ * HD) // 128
        sub = h_ % 2
        E('gpsimd', lambda h, h_=h_, oc=oc, sub=sub: h.dma_start(
            out=qpp(h_)[0:64, :],
            in_=D['qTd'][oc * 128 + sub * 64: oc * 128 + sub * 64 + 64, 0:OWN]),
          reads=rkeys('qTd'),
          writes=[f'qpp{h_}'] + [f'X{t}' for t in range(NT)], dma=True)
        E('gpsimd', lambda h, h_=h_: h.dma_start(
            out=qpp(h_)[64:74, :], in_=D['qext'][h_]),
          writes=[f'qpe{h_}'], dma=True)
        qpr[h_] = [f'qpp{h_}', f'qpe{h_}', f'qpc{h_}']
    for h_ in range(H):
        qw = min(512, OWN)
        for cs0 in range(0, OWN, qw):
            E('scalar', lambda h, h_=h_, cs0=cs0, qw=qw: h.activation(
                out=z1f0[0:64, 0:qw], in_=qpp(h_)[0:64, cs0:cs0 + qw], func=AF.Square),
              reads=[f'qpp{h_}'], writes=['z1fk'])
            E('tensor', lambda h, qw=qw: h.matmul(
                ps_c[0:1, 0:qw], onescol[0:64, 0:1], z1f0[0:64, 0:qw],
                start=True, stop=True),
              reads=['z1fk', 'onescol'], writes=['ps_c'])
            E('scalar', lambda h, qw=qw: h.activation(
                out=crow[:, 0:qw], in_=ps_c[0:1, 0:qw], func=AF.Sqrt),
              reads=['ps_c'], writes=['crow'])
            E('vector', lambda h, qw=qw: h.tensor_scalar(
                crow[:, 0:qw], crow[:, 0:qw],
                kmx0[0:1, 0:1], -1.0, ALU.mult, ALU.mult),
              reads=['crow', 'kmx0'], writes=['crow'])
            E('vector', lambda h, qw=qw: h.tensor_copy(crowb[0:1, 0:qw], crow[:, 0:qw]),
              reads=['crow'], writes=['crowb'])
            E('gpsimd', lambda h, h_=h_, cs0=cs0, qw=qw: h.dma_start(
                out=qpp(h_)[64:65, cs0:cs0 + qw], in_=crowb[0:1, 0:qw]),
              reads=['crowb', f'qpe{h_}'], writes=[f'qpc{h_}'], dma=True)

    NKB = 1024 // QS
    nstrip = OWN // QS
    nblk = QS // 128
    KH = SEQ // 2
    KHC = KH // 128
    NKB = min(NKB, KHC)
    assert KHC % NKB == 0
    _vw = (SEQ // 128) * 65
    if _use_arena and _vw + SEQ <= NC * DM:
        vb0 = WA[0:128, 0:_vw]
        kb0 = WA[0:74, _vw:_vw + SEQ]
        vb1 = WB[0:128, 0:_vw]
        kb1 = WB[0:74, _vw:_vw + SEQ]
    else:
        vb0 = S('vbufs0', [128, _vw], BF16)[:, :]
        kb0 = S('kbufs0', [74, SEQ], BF16)[:, :]
        vb1, kb1 = vb0, kb0
    kbufs = [kb0, kb1]
    vbufs = [vb0, vb1]
    ps_z = pb.psum('ps_z', [128, 512], F32)
    assert NKB * QS + nblk * QS <= NC * 512 or not _use_arena
    if _use_arena and NC * 512 >= 2 * NKB * QS + nblk * QS:
        es0_ = z0Ts0[:, 0:NKB * QS]
        maskb_s = z0Ts0[:, NKB * QS:NKB * QS + nblk * QS]
        es1_ = z0Ts0[:, NKB * QS + nblk * QS:2 * NKB * QS + nblk * QS]
    elif _use_arena:
        es0_ = z0Ts0[:, 0:NKB * QS]
        maskb_s = z0Ts0[:, NKB * QS:NKB * QS + nblk * QS]
        es1_ = es0_
    else:
        es0_ = S('esb0', [128, NKB * QS], BF16)[:]
        maskb_s = S('maskb_s', [128, nblk * QS], BF16)[:]
        es1_ = S('esb1', [128, NKB * QS], BF16)[:]
    esb = [es0_, es1_]
    rzrow = z1f0[:, 0:QS]
    rz0 = z1f0[0:1, QS:2 * QS] if DM >= 2 * QS else S('rz0', [1, QS], F32)[0:1, :]
    rzbc = z0f0_[0:64, 0:QS] if DM >= QS else S('rzbc', [64, QS], BF16)[0:64, :]
    E('vector', lambda h: h.engine_nop(),
      reads=['z0Ts0_0', 'z0Ts0_1', 'z0Ts0_2', 'z0Ts0_3', 'z1fk'],
      writes=['esb0', 'maskb', 'rzrow', 'rz0k'])
    E('sync', lambda h: h.dma_start(out=maskb_s[:, :], in_=D['maskb'][:]),
      reads=['maskb'], writes=['maskb'], dma=True)
    E('vector', lambda h: h.memset(vb0[:, :], 1.0), reads=['WA'], writes=['vbinit0', 'kbwa'])
    for h_ in range(H):
        for qq in range(SEQ // DM):
            E('sync', lambda h, h_=h_, qq=qq: h.dma_start(
                out=xf_a[0:10, :], in_=D['kext'][h_][:, qq * DM:(qq + 1) * DM]),
              writes=['xfa'], dma=True)
            E('vector', lambda h: h.tensor_copy(zt[0][0:10, 0:DM], xf_a[0:10, :]),
              reads=['xfa'], writes=['zt0k'])
            E('gpsimd', lambda h, h_=h_, qq=qq: h.dma_start(
                out=D['kppd'][h_, 64:74, qq * DM:(qq + 1) * DM], in_=zt[0][0:10, 0:DM]),
              reads=['zt0k'], writes=[wkey('kppd', f'kppe{h_}_{qq}')], dma=True)
    tern_per_head = (len(tern_units) + H - 1) // H
    pcS = [(ps_c, 'ps_c'), (ps_t, 'ps_t')]
    if vb1 is not vb0:
        E('vector', lambda h: h.memset(vb1[:, :], 1.0), reads=['WB'],
          writes=['vbinit1', 'kbwb'])
    pend = [None]

    def emit_epilogue(h_, qs, par):
        pc, pck = pcS[par]
        E('vector', lambda h, pc=pc: h.reciprocal(rzrow[64:65, 0:QS], pc[64:65, 0:QS]),
          reads=[pck], writes=['rzrow'])
        E('gpsimd', lambda h: h.dma_start(out=rz0[0:1, :], in_=rzrow[64:65, 0:QS]),
          reads=['rzrow'], writes=['rz0k'], dma=True)
        E('tensor', lambda h: h.matmul(ps_z[0:64, 0:QS], onesrow[0:1, 0:64],
                                       rz0[0:1, :], start=True, stop=True),
          reads=['rz0k', 'onesrow'], writes=['ps_z'])
        E('scalar', lambda h: h.activation(out=rzbc[0:64, :], in_=ps_z[0:64, 0:QS],
                                           func=AF.Copy),
          reads=['ps_z'], writes=['rzbc'])
        ctv = buT[64 * (h_ % 2): 64 * (h_ % 2) + 64,
                  (h_ // 2) * OWN + qs * QS: (h_ // 2) * OWN + (qs + 1) * QS]
        E('vector', lambda h, ctv=ctv, pc=pc: h.scalar_tensor_tensor(
            ctv, pc[0:64, 0:QS], 1.0, rzbc[0:64, :], ALU.bypass, ALU.mult),
          reads=[pck, 'rzbc'], writes=[wkey('ctx', f'ctx{h_}_{qs}')])

    it_c = [0]

    def prefetch_head(h_):
        par = h_ % 2 if kb1 is not kb0 else 0
        kb, kbk = kbufs[par], f'kbufs{par}'
        vb, vbk = vbufs[par], f'vbufs{par}'
        kbbar = 'kbwa' if kb is kb0 else 'kbwb'
        E('sync', lambda h, h_=h_, kb=kb: h.dma_start(
            out=kb[0:74, :], in_=D['kppd'][h_, :, :]),
          reads=rkeys('kppd') + [kbbar], writes=[kbk], dma=True)
        E('sync', lambda h, h_=h_, vb=vb: h.dma_start(
            out=vb.rearrange('p (c w) -> p c w', w=65)[:, :, 0:64],
            in_=D['vppd'].rearrange('(c p) d -> p c d', p=128)[:, :, h_ * HD:(h_ + 1) * HD]),
          reads=rkeys('vppd') + ['vbinit0' if par == 0 else 'vbinit1'],
          writes=[vbk], dma=True)

    prefetch_head(0)
    for h_ in range(H):
        hu = tern_units[h_ * tern_per_head:(h_ + 1) * tern_per_head]
        if hu and pend[0] is not None:
            emit_epilogue(*pend[0])
            pend[0] = None
        for u in hu:
            u()
        if h_ + 1 < H and (kb1 is not kb0):
            prefetch_head(h_ + 1)
        par = h_ % 2 if kb1 is not kb0 else 0
        kb, kbk = kbufs[par], f'kbufs{par}'
        vb, vbk = vbufs[par], f'vbufs{par}'
        if kb1 is kb0 and h_ > 0:
            prefetch_head(h_)
        for qs in range(nstrip):
            kend = (OWN + (qs + 1) * QS) // 128
            band0 = kend - nblk
            psS = ([(ps_big, 'ps_big'), (ps_y, 'ps_y')] if DM >= NKB * QS
                   else [(ps_big, 'ps_big'), (ps_big, 'ps_big')])

            def emit_scores(idx, g0, gn, h_=h_, qs=qs, kend=kend, band0=band0,
                            kb=kb, kbk=kbk):
                ps, psk = psS[idx % 2]
                for gi in range(gn):
                    kc = g0 + gi
                    E('tensor', lambda h, h_=h_, kc=kc, gi=gi, qs=qs, kb=kb, ps=ps: h.matmul(
                        ps[:, gi * QS:(gi + 1) * QS],
                        kb[0:74, kc * 128:(kc + 1) * 128],
                        qpp(h_)[0:74, qs * QS:(qs + 1) * QS],
                        start=True, stop=True),
                      reads=[kbk] + qpr[h_], writes=[psk], inc=(gi == gn - 1))
                for gi in range(gn):
                    kc = g0 + gi
                    if kc >= band0:
                        ib = kc - band0
                        E('vector', lambda h, gi=gi, ib=ib, ps=ps: h.tensor_tensor(
                            ps[:, gi * QS:(gi + 1) * QS],
                            ps[:, gi * QS:(gi + 1) * QS],
                            maskb_s[:, ib * QS:(ib + 1) * QS], ALU.add),
                          reads=[psk, 'maskb'], writes=[psk])
                es = esb[idx % 2]
                esk = f'esb{idx % 2}'
                E('scalar', lambda h, gn=gn, es=es, ps=ps: h.activation(
                    out=es[:, :gn * QS], in_=ps[:, :gn * QS], func=AF.Exp),
                  reads=[psk], writes=[esk])

            pc_par = it_c[0] % 2
            pc, pck = pcS[pc_par]

            def emit_ctx(idx, g0, gn, kend=kend, vb=vb, vbk=vbk, pc=pc, pck=pck):
                es = esb[idx % 2]
                esk = f'esb{idx % 2}'
                for gi in range(gn):
                    kc = g0 + gi
                    E('tensor', lambda h, kc=kc, gi=gi, es=es, vb=vb, kend=kend,
                      pc=pc: h.matmul(
                        pc[0:65, 0:QS], vb[:, kc * 65:(kc + 1) * 65],
                        es[:, gi * QS:(gi + 1) * QS],
                        start=(kc == 0), stop=(kc == kend - 1)),
                      reads=[esk, vbk], writes=[pck], inc=(kc == kend - 1))

            groups = [(g0, min(NKB, kend - g0)) for g0 in range(0, kend, NKB)]
            prev = None
            for idx, (g0, gn) in enumerate(groups):
                emit_scores(idx, g0, gn)
                if prev is not None:
                    emit_ctx(prev[0], prev[1], prev[2])
                prev = (idx, g0, gn)
            emit_ctx(prev[0], prev[1], prev[2])
            if pend[0] is not None:
                emit_epilogue(*pend[0])
            pend[0] = (h_, qs, pc_par)
            it_c[0] += 1
    emit_epilogue(*pend[0])

    bcast_load(bobc, 'bo', 'gbc', BF16)
    if kb1 is not kb0:
        E('vector', lambda h: h.engine_nop(),
          reads=['kbufs1', 'vbufs1'], writes=['WB'])
    wload(WB, 'Wot', 'WB', DM)
    for t in range(NTO):
        for hi_, (c0, cw) in enumerate(HL):
            for kc in range(NC):
                E('tensor', lambda h, kc=kc, t=t, c0=c0, cw=cw: h.matmul(
                    ps_y[:, c0:c0 + cw],
                    buT[:, kc * OWN + t * 128: kc * OWN + (t + 1) * 128],
                    WB[:, kc * DM + c0: kc * DM + c0 + cw],
                    start=(kc == 0), stop=(kc == NC - 1)),
                  reads=rkeys('ctx') + ['WB'], writes=['ps_y'], inc=(kc == NC - 1))
        nt_, ntk = nrm_t[0], 'nrm0k'
        E('vector', lambda h, nt_=nt_: h.scalar_tensor_tensor(
            nt_[:], ps_y[:, :DM], 1.0, bobc[:], ALU.bypass, ALU.add),
          reads=['ps_y', 'gbc'], writes=[ntk])
        tile_stats(nt_, ntk, LNEPS_O)
        E('scalar', lambda h, nt_=nt_: h.activation(
            out=xf_b[:], in_=nt_[:], func=AF.Identity, bias=gnm[:, 0:1], scale=grs[:, 0:1]),
          reads=[ntk, 'grs', 'gnm', 'rzrow', 'rz0k'], writes=['z1fk'])
        E('sync', lambda h, t=t: h.dma_start(
            out=xf_a[:], in_=D['x2d'][OWN + t * 128: OWN + (t + 1) * 128, :]),
          reads=rkeys('x2d'), writes=['xfa'], dma=True)
        E('vector', lambda h: h.tensor_tensor(xf_a[:], xf_a[:], xf_b[:], ALU.add),
          reads=['xfa', 'z1fk'], writes=['xfa'])
        E('scalar', lambda h, t=t: h.dma_start(out=D['x3d'][t * 128:(t + 1) * 128, :],
                                               in_=xf_a[:]),
          reads=['xfa'], writes=[wkey('x3d', f'x3d{t}')], dma=True)

    if debug_out == 'x3':
        for t in range(NTO):
            E('sync', lambda h, t=t: h.dma_start(out=xf_a[:], in_=D['x3d'][t * 128:(t + 1) * 128, :]),
              reads=rkeys('x3d'), writes=['xfa'], dma=True)
            E('scalar', lambda h, t=t: h.dma_start(out=out_ap[t * 128:(t + 1) * 128, :], in_=xf_a[:]),
              reads=['xfa'], dma=True)
        pb.wait_all_done('sync')
        pb.build()
        return nc, out_names

    hamax = SP(NTO)
    hrec = SP(NTO)
    hscl = SP(NTO)
    gsc = SP(NTO)
    usc = SP(NTO)
    dsc = SP(NTO)
    E('vector', lambda h: h.memset(hamax[:], 0.0), writes=['hamax'])
    E('vector', lambda h: h.engine_nop(),
      reads=['twbk', 'tc1', 'tc2'], writes=PAR1KEYS + ['acc1k', 'qc1k'])
    for t in range(NTO):
        par = t % 2
        xfa_, xfak = xfa_t[par], ('xfa' if par == 0 else 'xfa1')
        zf32, zfk = z1f[par], ('z1fk' if par == 0 else 'z1fbk')
        zc, zck = zt[par], f'zt{par}k'
        E('sync', lambda h, t=t, xfa_=xfa_: h.dma_start(
            out=xfa_[:], in_=D['x3d'][t * 128:(t + 1) * 128, :]),
          reads=rkeys('x3d'), writes=[xfak], dma=True)
        rs_, nm_, rsk, nmk = tile_stats(xfa_, xfak, LNEPS_O, par)
        E('scalar', lambda h, zf32=zf32, xfa_=xfa_, rs_=rs_, nm_=nm_: h.activation(
            out=zf32[:], in_=xfa_[:], func=AF.Identity, bias=nm_[:, 0:1], scale=rs_[:, 0:1]),
          reads=[xfak, rsk, nmk], writes=[zfk])
        E('vector', lambda h, zf32=zf32, t=t: h.reduce_max(
            amax[:, t:t + 1], zf32[:], AX.X, apply_absolute_value=True),
          reads=[zfk], writes=[f'am{t}'])
        E('vector', lambda h, t=t: h.tensor_single_scalar(
            qscl[:, t:t + 1], amax[:, t:t + 1], 1e-5, ALU.max),
          reads=[f'am{t}'], writes=[f'qs{t}'])
        E('vector', lambda h, t=t: h.reciprocal(qrec[:, t:t + 1], qscl[:, t:t + 1]),
          reads=[f'qs{t}'], writes=[f'qr{t}'])
        E('vector', lambda h, t=t: h.tensor_single_scalar(
            qrec[:, t:t + 1], qrec[:, t:t + 1], 127.0, ALU.mult),
          reads=[f'qr{t}'], writes=[f'qr{t}'])
        vv_ = q_v[par]
        E('vector', lambda h, vv_=vv_, zf32=zf32, t=t: h.tensor_scalar(
            vv_[:], zf32[:], qrec[:, t:t + 1], None, ALU.mult),
          reads=[zfk, f'qr{t}'], writes=[zfk])
        xq = quant_round_full(par, vv_, zfk,
                              eng='vector' if par == 0 else 'gpsimd')
        for c in range(NC):
            E('tensor', lambda h, c=c, xq=xq[0]: h.transpose(
                ps_tb[:, c * 128:(c + 1) * 128], xq[:, c * 128:(c + 1) * 128], identb[:]),
              reads=[xq[1], 'identb'], writes=['ps_tb'], inc=(c == NC - 1))
        E('scalar', lambda h, zc=zc: h.activation(out=zc[:, :NC * 128],
                                                  in_=ps_tb[:, :NC * 128], func=AF.Copy),
          reads=['ps_tb'], writes=[zck])
        E('scalar', lambda h, t=t, zc=zc: h.dma_start(
            out=D['qTd'][0:DM, t * 128:(t + 1) * 128].rearrange('(c p) w -> p c w', p=128),
            in_=zc[:, :NC * 128].rearrange('p (c w) -> p c w', w=128)),
          reads=[zck] + [f'qpp{h2}' for h2 in range(H)],
          writes=[wkey('xqT', f'xqT{t}')], dma=True)
    if debug_out == 'xq':
        for c in range(DM // 128):
            E('sync', lambda h, c=c: h.dma_start(
                out=twb0[:, 0:OWN], in_=D['qTd'][c * 128:(c + 1) * 128, 0:OWN]),
              reads=rkeys('xqT'), writes=['twbk'], dma=True)
            for j in range(OWN // DM):
                E('vector', lambda h, j=j: h.tensor_copy(
                    xf_a[:], twb0[:, j * DM:(j + 1) * DM]),
                  reads=['twbk'], writes=['xfa'])
                E('scalar', lambda h, c=c, j=j: h.dma_start(
                    out=dbg_ap[(c * (OWN // DM) + j) * 128:(c * (OWN // DM) + j + 1) * 128, :],
                    in_=xf_a[:]),
                  reads=['xfa'], dma=True)
        pb.wait_all_done('sync')
        pb.build()
        return nc, out_names

    E('vector', lambda h: h.tensor_scalar(gsc[:], qscl[:, 0:NTO], wmg[:, 0:1],
                                          float(1.0 / 127.0), ALU.mult, ALU.mult),
      reads=['wmg'] + [f'qs{t}' for t in range(NTO)], writes=['gsc'])
    E('vector', lambda h: h.tensor_scalar(usc[:], qscl[:, 0:NTO], wmu[:, 0:1],
                                          float(1.0 / 127.0), ALU.mult, ALU.mult),
      reads=['wmu'] + [f'qs{t}' for t in range(NTO)], writes=['usc'])

    NIH = 2
    IH = INNER // NIH
    cgw = min(512, IH)
    for ih in range(NIH):
        TgV = X[:, 0:NC * IH]
        TuV = X[:, NC * IH: 2 * NC * IH]
        qfree = [f'qpp{h2}' for h2 in range(H)] + [f'qpe{h2}' for h2 in range(H)] +                 [f'qpc{h2}' for h2 in range(H)]
        E('sync', lambda h, ih=ih, TgV=TgV: h.dma_start(
            out=TgV.rearrange('p (c w) -> p c w', w=IH),
            in_=D['Tgd'].rearrange('(c p) n -> p c n', p=128)[:, :, ih * IH:(ih + 1) * IH]),
          reads=rkeys('Tgd'), writes=['TgV'] + (qfree if ih == 0 else []), dma=True)
        E('sync', lambda h, ih=ih, TuV=TuV: h.dma_start(
            out=TuV.rearrange('p (c w) -> p c w', w=IH),
            in_=D['Tud'].rearrange('(c p) n -> p c n', p=128)[:, :, ih * IH:(ih + 1) * IH]),
          reads=rkeys('Tud'), writes=['TuV'], dma=True)
        GW = min(DM, 1024, IH)
        for t in range(NTO):
            zc, zck = zt[0], 'zt0k'
            E('gpsimd', lambda h, t=t, zc=zc: h.dma_start(
                out=zc[:, :NC * 128].rearrange('p (c w) -> p c w', w=128),
                in_=D['qTd'][0:DM, t * 128:(t + 1) * 128].rearrange('(c p) w -> p c w', p=128)),
              reads=rkeys('xqT'), writes=[zck], dma=True)
            for grp in range(IH // GW):
                for kc in range(NC):
                    for part in ('g', 'u'):
                        WV, wk2 = (TgV, 'TgV') if part == 'g' else (TuV, 'TuV')
                        psd = ps_y if part == 'g' else ps_big
                        psk = 'ps_y' if part == 'g' else 'ps_big'
                        for cg in range(GW // cgw):
                            col0 = grp * GW + cg * cgw
                            E('tensor', lambda h, kc=kc, cg=cg, zc=zc, WV=WV,
                              col0=col0, psd=psd: h.matmul(
                                psd[:, cg * cgw:(cg + 1) * cgw], zc[:, kc * 128:(kc + 1) * 128],
                                WV[:, kc * IH + col0: kc * IH + col0 + cgw],
                                start=(kc == 0), stop=(kc == NC - 1)),
                              reads=[zck, wk2], writes=[psk],
                              inc=(kc == NC - 1 and part == 'u' and cg == GW // cgw - 1))
                E('scalar', lambda h, t=t: h.activation(
                    out=z1f0[:, 0:GW], in_=ps_y[:, 0:GW],
                    func=AF.Silu, bias=np.float32(0.0), scale=gsc[:, t:t + 1]),
                  reads=['ps_y', 'ps_big', 'gsc'], writes=['z1fk'])
                E('scalar', lambda h, t=t: h.activation(
                    out=xf_a[:, 0:GW], in_=ps_big[:, 0:GW],
                    func=AF.Identity, bias=np.float32(0.0), scale=usc[:, t:t + 1]),
                  reads=['ps_y', 'ps_big', 'usc'], writes=['xfa'])
                if debug_out == 'gu':
                    E('vector', lambda h, grp=grp: h.tensor_copy(
                        gtile[:, grp * GW:(grp + 1) * GW], z1f0[:, 0:GW]),
                      reads=['z1fk'], writes=['gtile'])
                    E('vector', lambda h, grp=grp: h.tensor_copy(
                        util[:, grp * GW:(grp + 1) * GW], xf_a[:, 0:GW]),
                      reads=['xfa'], writes=['util'])
                    continue
                E('vector', lambda h: h.tensor_tensor(z1f0[:, 0:GW], z1f0[:, 0:GW],
                                                      xf_a[:, 0:GW], ALU.mult),
                  reads=['z1fk', 'xfa'], writes=['z1fk'])
                E('vector', lambda h, grp=grp: h.tensor_copy(
                    gtile[:, grp * GW:(grp + 1) * GW], z1f0[:, 0:GW]),
                  reads=['z1fk'], writes=['gtile'])
                E('vector', lambda h, grp=grp: h.scalar_tensor_tensor(
                    util[:, grp * GW:(grp + 1) * GW], gtile[:, grp * GW:(grp + 1) * GW],
                    -1.0, z1f0[:, 0:GW], ALU.mult, ALU.add),
                  reads=['gtile', 'z1fk'], writes=['util'])
                E('vector', lambda h: h.tensor_reduce(tsum2[:], z1f0[:, 0:GW], AX.X,
                                                      ALU.max, apply_absolute_value=True),
                  reads=['z1fk'], writes=['tsum2'])
                E('vector', lambda h, t=t: h.tensor_tensor(
                    hamax[:, t:t + 1], hamax[:, t:t + 1], tsum2[:], ALU.max),
                  reads=['hamax', 'tsum2'], writes=['hamax'])
            E('gpsimd', lambda h, t=t, ih=ih: h.dma_start(
                out=D['hd'][t * 128:(t + 1) * 128, ih * IH:(ih + 1) * IH],
                in_=gtile[:, :]),
              reads=['gtile'], writes=[wkey('hd', f'hd{ih}_{t}')], dma=True)
            E('gpsimd', lambda h, t=t, ih=ih: h.dma_start(
                out=D['hd2'][t * 128:(t + 1) * 128, ih * IH:(ih + 1) * IH],
                in_=util[:, :]),
              reads=['util'], writes=[wkey('hd2', f'hd2{ih}_{t}')], dma=True)

    if debug_out == 'gu':
        for t in range(NTO):
            for j, src in ((0, 'hd'), (1, 'hd2')):
                E('sync', lambda h, t=t, src=src: h.dma_start(
                    out=twb0[:, 0:DM], in_=D[src][t * 128:(t + 1) * 128, 0:DM]),
                  reads=rkeys('hd') + rkeys('hd2'), writes=['twbk'], dma=True)
                E('vector', lambda h: h.tensor_copy(xf_a[:], twb0[:, 0:DM]),
                  reads=['twbk'], writes=['xfa'])
                E('scalar', lambda h, t=t, j=j: h.dma_start(
                    out=dbg_ap[(t * 2 + j) * 128:(t * 2 + j + 1) * 128, :], in_=xf_a[:]),
                  reads=['xfa'], dma=True)
        pb.wait_all_done('sync')
        pb.build()
        return nc, out_names

    if debug_out == 'hh':
        for t in range(NTO):
            for j in range(2):
                E('sync', lambda h, t=t, j=j: h.dma_start(
                    out=xf_a[:].bitcast(BF16)[:, 0:DM],
                    in_=D['hd'][t * 128:(t + 1) * 128, j * DM:(j + 1) * DM]),
                  reads=rkeys('hd'), writes=['xfa'], dma=True) if False else None
                E('sync', lambda h, t=t, j=j: h.dma_start(
                    out=twb0[:, 0:DM], in_=D['hd'][t * 128:(t + 1) * 128, j * DM:(j + 1) * DM]),
                  reads=rkeys('hd'), writes=['twbk'], dma=True)
                E('vector', lambda h: h.tensor_copy(xf_a[:], twb0[:, 0:DM]),
                  reads=['twbk'], writes=['xfa'])
                E('scalar', lambda h, t=t, j=j: h.dma_start(
                    out=dbg_ap[(t * 2 + j) * 128:(t * 2 + j + 1) * 128, :], in_=xf_a[:]),
                  reads=['xfa'], dma=True)
        pb.wait_all_done('sync')
        pb.build()
        return nc, out_names

    NCI = INNER // 128
    TdV = X[:, 0:NCI * DM]
    E('sync', lambda h: h.dma_start(
        out=TdV.rearrange('p (c w) -> p c w', w=DM),
        in_=D['Tdd'].rearrange('(c p) n -> p c n', p=128)),
      reads=rkeys('Tdd'), writes=['TdV', 'TgV', 'TuV'], dma=True)
    E('vector', lambda h: h.tensor_scalar(dsc[:], hamax[:, 0:NTO], wmd[:, 0:1],
                                          float(1.0 / 127.0), ALU.mult, ALU.mult),
      reads=['wmd', 'hamax'], writes=['dsc'])
    E('vector', lambda h: h.tensor_scalar(hscl[:], hamax[:, 0:NTO], None, None,
                                          ALU.bypass), reads=['hamax'], writes=['hsclx'],
      inc=True) if False else None

    for t in range(NTO):
        E('vector', lambda h, t=t: h.tensor_single_scalar(
            hscl[:, t:t + 1], hamax[:, t:t + 1], 1e-5, ALU.max),
          reads=['hamax'], writes=[f'hs{t}'])
        E('vector', lambda h, t=t: h.reciprocal(hrec[:, t:t + 1], hscl[:, t:t + 1]),
          reads=[f'hs{t}'], writes=[f'hr{t}'])
        E('vector', lambda h, t=t: h.tensor_single_scalar(
            hrec[:, t:t + 1], hrec[:, t:t + 1], 127.0, ALU.mult),
          reads=[f'hr{t}'], writes=[f'hr{t}'])
        E('vector', lambda h, t=t: h.tensor_scalar(
            dsc[:, t:t + 1], hscl[:, t:t + 1], wmd[:, 0:1], float(1.0 / 127.0),
            ALU.mult, ALU.mult),
          reads=[f'hs{t}', 'wmd'], writes=[f'dsc{t}'])
        par = t % 2
        ht_, htk = (htile, 'htile') if par == 0 else (htile2, 'htile1')
        hT_, hTpre = (hqT, 'hqT0_') if par == 0 else (hqT2, 'hqT1_')
        xfa_, xfak = xfa_t[par], ('xfa' if par == 0 else 'xfa1')
        zf32, zfk = z1f[par], ('z1fk' if par == 0 else 'z1fbk')
        nt_, ntk = nrm_t[par], ('nrm0k' if par == 0 else 'nrm1k')
        psY, psYk = (ps_y, 'ps_y') if par == 0 else (ps_big, 'ps_big')
        hqpre = f'hq{par}_'
        E('gpsimd', lambda h, t=t, ht_=ht_: h.dma_start(
            out=ht_[:, :], in_=D['hd'][t * 128:(t + 1) * 128, :]),
          reads=rkeys('hd'),
          writes=[htk] + [hqpre + str(s2) for s2 in range(INNER // DM)], dma=True)
        if par == 0:
            lo_f = None
            E('gpsimd', lambda h, t=t: h.dma_start(
                out=gtile[:, :], in_=D['hd2'][t * 128:(t + 1) * 128, 0:INNER // 2]),
              reads=rkeys('hd2'), writes=['gtile'], dma=True)
            E('gpsimd', lambda h, t=t: h.dma_start(
                out=util[:, :], in_=D['hd2'][t * 128:(t + 1) * 128, INNER // 2:INNER]),
              reads=rkeys('hd2'), writes=['util'], dma=True)
            lokeys = ['gtile', 'util']
        else:
            lo_f = WA[0:128, 0:INNER] if _use_arena else None
            if lo_f is not None:
                E('gpsimd', lambda h, t=t, lo_f=lo_f: h.dma_start(
                    out=lo_f[:, :], in_=D['hd2'][t * 128:(t + 1) * 128, :]),
                  reads=rkeys('hd2') + ['kbufs0', 'vbufs0'], writes=['lo1k'], dma=True)
                lokeys = ['lo1k']
            else:
                E('gpsimd', lambda h, t=t: h.dma_start(
                    out=gtile[:, :], in_=D['hd2'][t * 128:(t + 1) * 128, 0:INNER // 2]),
                  reads=rkeys('hd2'), writes=['gtile'], dma=True)
                E('gpsimd', lambda h, t=t: h.dma_start(
                    out=util[:, :], in_=D['hd2'][t * 128:(t + 1) * 128, INNER // 2:INNER]),
                  reads=rkeys('hd2'), writes=['util'], dma=True)
                lokeys = ['gtile', 'util']
        for sub in range(INNER // DM):
            vv_ = q_v[par]
            E('vector', lambda h, sub=sub, t=t, ht_=ht_, vv_=vv_: h.tensor_scalar(
                vv_[:], ht_[:, sub * DM:(sub + 1) * DM], hrec[:, t:t + 1],
                None, ALU.mult),
              reads=[htk, f'hr{t}'], writes=[zfk])
            IH2 = INNER // 2
            if lo_f is not None:
                lo_ap = lo_f[:, sub * DM:(sub + 1) * DM]
            else:
                lo_ap = (gtile[:, sub * DM:(sub + 1) * DM] if (sub + 1) * DM <= IH2
                         else util[:, sub * DM - IH2:(sub + 1) * DM - IH2])
            E('vector', lambda h, lo_ap=lo_ap, t=t, vv_=vv_: h.scalar_tensor_tensor(
                vv_[:], lo_ap, hrec[:, t:t + 1], vv_[:], ALU.mult, ALU.add),
              reads=lokeys + [f'hr{t}', zfk], writes=[zfk])
            xqh = quant_round_full(par, vv_, zfk,
                                   eng='vector' if par == 0 else 'gpsimd')
            E('vector', lambda h, sub=sub, xqh=xqh[0], ht_=ht_: h.tensor_copy(
                ht_[:, sub * DM:(sub + 1) * DM], xqh),
              reads=[xqh[1], htk], writes=[hqpre + str(sub)])
        hqr = [hqpre + str(s2) for s2 in range(INNER // DM)]
        for b in range(NCI // NC):
            for cc in range(NC):
                E('tensor', lambda h, b=b, cc=cc, ht_=ht_: h.transpose(
                    ps_tb[:, cc * 128:(cc + 1) * 128],
                    ht_[:, (b * NC + cc) * 128:(b * NC + cc + 1) * 128], identb[:]),
                  reads=hqr + ['identb'], writes=['ps_tb'], inc=(cc == NC - 1))
            E('scalar', lambda h, b=b, hT_=hT_: h.activation(
                out=hT_[:, b * NC * 128:(b + 1) * NC * 128], in_=ps_tb[:, :NC * 128],
                func=AF.Copy),
              reads=['ps_tb'], writes=[hTpre + str(b)])
        hqTr = [hTpre + str(b) for b in range(NCI // NC)]
        for kc in range(NCI):
            for hi_, (c0, cw) in enumerate(HL):
                E('tensor', lambda h, kc=kc, c0=c0, cw=cw, hT_=hT_, psY=psY: h.matmul(
                    psY[:, c0:c0 + cw], hT_[:, kc * 128:(kc + 1) * 128],
                    TdV[:, kc * DM + c0: kc * DM + c0 + cw],
                    start=(kc == 0), stop=(kc == NCI - 1)),
                  reads=hqTr + ['TdV'], writes=[psYk],
                  inc=(kc == NCI - 1 and hi_ == len(HL) - 1))
        E('scalar', lambda h, t=t, nt_=nt_, psY=psY: h.activation(
            out=nt_[:], in_=psY[:, :DM], func=AF.Identity, bias=np.float32(0.0),
            scale=dsc[:, t:t + 1]),
          reads=[psYk, f'dsc{t}'], writes=[ntk])
        rs_, nm_, rsk, nmk = tile_stats(nt_, ntk, LNEPS_O, par)
        E('scalar', lambda h, nt_=nt_, zf32=zf32, rs_=rs_, nm_=nm_: h.activation(
            out=zf32[:], in_=nt_[:], func=AF.Identity, bias=nm_[:, 0:1], scale=rs_[:, 0:1]),
          reads=[ntk, rsk, nmk], writes=[zfk])
        E('sync', lambda h, t=t, xfa_=xfa_: h.dma_start(
            out=xfa_[:], in_=D['x3d'][t * 128:(t + 1) * 128, :]),
          reads=rkeys('x3d'), writes=[xfak], dma=True)
        E('vector', lambda h, xfa_=xfa_, zf32=zf32: h.tensor_tensor(
            xfa_[:], xfa_[:], zf32[:], ALU.add),
          reads=[xfak, zfk], writes=[xfak])
        E('sync', lambda h, t=t, xfa_=xfa_: h.dma_start(
            out=out_ap[t * 128:(t + 1) * 128, :], in_=xfa_[:]),
          reads=[xfak], dma=True)

    pb.wait_all_done('sync')
    pb.build()
    print('SBUF bytes/partition:', pb.sbuf_bytes)
    return nc, out_names


from concourse.bass_utils import run_bass_kernel_spmd

_BUILD_CACHE = {}


def kernel(**inputs):
    cfg = CFG()
    maps = host_stage(inputs, cfg)
    if 'nc' not in _BUILD_CACHE:
        _BUILD_CACHE['nc'] = build(cfg, debug_out=None)
    nc, out_names = _BUILD_CACHE['nc']
    res = run_bass_kernel_spmd(nc, maps, core_ids=list(range(len(maps))))
    B = len(maps) // 2
    out = np.zeros((B, cfg.SEQ, cfg.DM), np.float32)
    for c in range(len(maps)):
        b, half = c // 2, c % 2
        out[b, half * cfg.OWN:(half + 1) * cfg.OWN] = np.asarray(res.results[c]['out'])
    return out


# version 21
# speedup vs baseline: 1.7784x; 1.7784x over previous
import numpy as np
import contextlib
from collections import defaultdict

import concourse.bass as bass
import concourse.mybir as mybir

F32 = mybir.dt.float32
BF16 = mybir.dt.bfloat16
I8 = mybir.dt.int8
I32 = mybir.dt.int32
U32 = mybir.dt.uint32
AF = mybir.ActivationFunctionType
ALU = mybir.AluOpType
AX = mybir.AxisListType

ENGINES = ('sync', 'scalar', 'vector', 'tensor', 'gpsimd')


class PB:

    def __init__(self, nc):
        self.nc = nc
        self.es = contextlib.ExitStack()
        self.q = {e: [] for e in ENGINES}
        self.sem_names = []
        self.sems = {}
        self.cnt = defaultdict(int)
        self.wm = {e: defaultdict(int) for e in ENGINES}
        self.res = {}
        self._tensors = {}
        self.n_dma_sems = 12
        self._dma_rr = defaultdict(int)
        self._pending = {e: ([], []) for e in ENGINES}

    def sem(self, name):
        if name not in self.sems:
            self.sems[name] = self.es.enter_context(self.nc.semaphore(name))
            self.sem_names.append(name)
        return self.sems[name]

    def sbuf(self, name, shape, dt):
        t = self.es.enter_context(self.nc.sbuf_tensor(name, list(shape), dt))
        self._tensors[name] = t
        import math
        nb = {2: 2, 4: 4, 1: 1}[mybir.dt.size(dt)]
        self.sbuf_bytes = getattr(self, 'sbuf_bytes', 0) +             int(np.prod(shape[1:])) * mybir.dt.size(dt)
        return t

    def psum(self, name, shape, dt=F32):
        t = self.es.enter_context(self.nc.psum_tensor(name, list(shape), dt))
        self._tensors[name] = t
        return t

    def _need(self, key, for_write):
        rec = self.res.get(key)
        if rec is None:
            return []
        out = list(rec['w'])
        if for_write:
            out.extend(rec['r'])
        return out

    def emit(self, eng, fn, reads=(), writes=(), sem=None, dma=False,
             inc=True, tag=None):
        needed = {}
        for r in reads:
            for (s, c) in self._need(r, False):
                needed[s] = max(needed.get(s, 0), c)
        for w in writes:
            for (s, c) in self._need(w, True):
                needed[s] = max(needed.get(s, 0), c)
        if dma:
            sname = sem or ('DMA_%s_%d' % (eng, self._dma_rr[eng] % self.n_dma_sems))
            if sem is None:
                self._dma_rr[eng] += 1
            prev = self.cnt[sname]
            if prev:
                needed[sname] = max(needed.get(sname, 0), prev)
        else:
            sname = sem or ('E' + eng)
        self.sem(sname)
        wl = []
        for s, c in sorted(needed.items()):
            if self.wm[eng][s] < c:
                wl.append((s, c))
                self.wm[eng][s] = c
        amt = 16 if dma else 1
        if inc:
            self.cnt[sname] += amt
            cafter = self.cnt[sname]
        else:
            cafter = None

        def run(h, wl=wl, fn=fn, sname=sname, amt=amt, inc=inc):
            for (s, c) in wl:
                h.wait_ge(self.sems[s], c)
            try:
                r = fn(h)
            except Exception:
                import traceback as _tb
                _tb.print_exc()
                raise
            if inc:
                if r is None:
                    raise RuntimeError(f'op on {eng} returned None (need inst)')
                r.then_inc(self.sems[sname], amt)
        self.q[eng].append(run)
        if inc:
            pr, pw = self._pending[eng]
            allw = list(writes) + pw
            allr = list(reads) + pr
            self._pending[eng] = ([], [])
            for w in allw:
                prev = self.res.get(w)
                if dma and prev is not None and prev.get('dma'):
                    chain = (prev['w'] + [(sname, cafter)])[-8:]
                else:
                    chain = [(sname, cafter)]
                self.res[w] = {'w': chain, 'r': [], 'dma': dma}
            for r in allr:
                rec = self.res.get(r)
                if rec is not None:
                    rec['r'].append((sname, cafter))
        else:
            pr, pw = self._pending[eng]
            pr.extend(reads)
            pw.extend(writes)
        return (sname, cafter)

    def raw(self, eng, fn):
        self.q[eng].append(fn)

    def wait_all_done(self, eng='sync'):
        snap = dict(self.cnt)
        def run(h, snap=snap):
            for s, c in sorted(snap.items()):
                if c > self.wm[eng].get(s, 0):
                    h.wait_ge(self.sems[s], c)
        self.q[eng].append(run)

    def build(self):
        with self.nc.Block() as block:
            for e in ENGINES:
                ops = self.q[e]
                if not ops:
                    continue
                def body(h, ops=ops):
                    for op in ops:
                        op(h)
                getattr(block, e)(body)
        self.es.close()


def dsplit(x):
    hi = np.float32(x)
    lo = np.float32(np.float64(x) - np.float64(hi))
    return float(hi), float(lo)



MSK = -1.0e30


def get_slopes(n):
    import math

    def pow2(n):
        start = 2 ** (-(2 ** -(math.log2(n) - 3)))
        return [start * start ** i for i in range(n)]
    if math.log2(n).is_integer():
        return pow2(n)
    c = 2 ** math.floor(math.log2(n))
    return pow2(c) + get_slopes(2 * c)[0::2][:n - c]


class CFG:
    def __init__(self, SEQ=4096, DM=1024, INNER=4096, H=16, HD=64, QS=512):
        self.SEQ, self.DM, self.INNER, self.H, self.HD = SEQ, DM, INNER, H, HD
        self.OWN = SEQ // 2
        self.NT = SEQ // 128
        self.NTO = self.OWN // 128
        self.NC = DM // 128
        self.QS = QS
        assert DM % 128 == 0 and SEQ % 512 == 0 and H * HD == DM and HD == 64
        self.KE = 74 if DM >= 128 else 0


def host_stage(inputs, cfg):
    import numpy as np
    import ml_dtypes
    bf = ml_dtypes.bfloat16
    SEQ, DM, INNER, H = cfg.SEQ, cfg.DM, cfg.INNER, cfg.H
    u = np.ascontiguousarray(inputs['L0_attn_u'])
    B = u.shape[0]

    def T(x):
        return np.ascontiguousarray(np.asarray(x).T)

    shared = {
        'Bt': T(inputs['L0_attn_Bmat']),
        'Ct': T(inputs['L0_attn_Cmat']),
        'Dt': T(inputs['L0_attn_Dmat']),
        'Adiag': np.ascontiguousarray(np.asarray(inputs['L0_attn_A_diag']).reshape(cfg.NC, 128).T),
        'h0v': np.ascontiguousarray(np.asarray(inputs['L0_attn_h0']).reshape(cfg.NC, 128).T),
        'lg': np.ascontiguousarray(inputs['L0_attn_ln_gamma']),
        'lb': np.ascontiguousarray(inputs['L0_attn_ln_beta']),
        'W8t': T(inputs['L0_ffn_weight_q']),
        'wsc': np.ascontiguousarray(inputs['L0_ffn_weight_scale']),
        'fbias': np.ascontiguousarray(inputs['L0_ffn_bias']),
        'Wqt': np.ascontiguousarray(inputs['L1_attn_Wq']),
        'Wkt': np.ascontiguousarray(inputs['L1_attn_Wk']),
        'Wvt': np.ascontiguousarray(inputs['L1_attn_Wv']),
        'Wot': np.ascontiguousarray(inputs['L1_attn_Wo']),
        'bo': np.ascontiguousarray(inputs['L1_attn_bo']),
        'Wgt': T(inputs['L1_ffn_w_gate']),
        'Wut': T(inputs['L1_ffn_w_up']),
        'Wdt': T(inputs['L1_ffn_w_down']),
    }
    ident = np.zeros((128, 128), dtype=bf)
    np.fill_diagonal(ident, 1.0)
    shared['identb'] = ident
    identf = np.zeros((128, 128), dtype=np.float32)
    np.fill_diagonal(identf, 1.0)
    shared['identf'] = identf
    QSc = cfg.QS
    nblk = QSc // 128
    mb = np.zeros((nblk, 128, QSc), dtype=np.float32)
    for i in range(nblk):
        for b in range(nblk):
            blk = mb[i][:, b * 128:(b + 1) * 128]
            if b < i:
                blk[:] = MSK
            elif b == i:
                blk[:] = np.where(np.arange(128)[None, :] >= np.arange(128)[:, None],
                                  0.0, MSK)
    shared['maskb'] = mb.transpose(1, 0, 2).reshape(128, nblk * QSc).astype(bf)
    md = np.where(np.arange(128)[None, :] >= np.arange(128)[:, None], 0.0, MSK)
    shared['maskdiag'] = md.astype(np.float32)
    slopes = np.array(get_slopes(H), dtype=np.float64)

    maps = []
    for c in range(2 * B):
        b, half = c // 2, c % 2
        Q0 = cfg.OWN * half
        m = dict(shared)
        if half == 0:
            m['u'] = np.ascontiguousarray(
                np.concatenate([u[b][cfg.OWN:], u[b][:cfg.OWN]], axis=0))
        else:
            m['u'] = u[b]
        m['mflag'] = np.full((128, 1), float(half), dtype=np.float32)
        qpos = np.arange(Q0, Q0 + cfg.OWN, dtype=np.float64)
        RA = cfg.OWN * (1 - half)
        kslot_tok = (np.arange(SEQ) + RA) % SEQ
        kvalid = np.zeros(SEQ)
        if half == 0:
            kvalid[:cfg.OWN] = MSK
        qext = np.zeros((H, 10, cfg.OWN), dtype=np.float64)
        kext = np.zeros((H, 10, SEQ), dtype=np.float64)
        ihi, ilo = np.floor(qpos / 16.0), qpos % 16.0
        jhi, jlo = np.floor(kslot_tok / 16.0), kslot_tok % 16.0
        for h in range(H):
            sl = slopes[h]
            e = 2.0 * sl
            f = sl / 8.0
            e_hi = np.float32(e); e_lo = np.float32(e - np.float64(e_hi))
            f_hi = np.float32(f); f_lo = np.float32(f - np.float64(f_hi))
            qext[h, 1], qext[h, 2], qext[h, 3], qext[h, 4] = e_hi, e_lo, f_hi, f_lo
            qext[h, 5] = ihi; qext[h, 6] = ihi; qext[h, 7] = ilo; qext[h, 8] = ilo
            qext[h, 9] = 1.0
            kext[h, 0] = 1.0
            kext[h, 1] = jhi; kext[h, 2] = jhi; kext[h, 3] = jlo; kext[h, 4] = jlo
            g2 = -e; g3 = -f
            g2_hi = np.float32(g2); g2_lo = np.float32(g2 - np.float64(g2_hi))
            g3_hi = np.float32(g3); g3_lo = np.float32(g3 - np.float64(g3_hi))
            kext[h, 5], kext[h, 6], kext[h, 7], kext[h, 8] = g2_hi, g2_lo, g3_hi, g3_lo
            kext[h, 9] = kvalid
        for h in range(H):
            for (r_hi, r_lo) in ((1, 2), (3, 4)):
                v = qext[h, r_hi] + qext[h, r_lo]
                hi = np.asarray(v, dtype=bf).astype(np.float64)
                qext[h, r_hi] = hi
                qext[h, r_lo] = v - hi
            for (r_hi, r_lo) in ((5, 6), (7, 8)):
                v = kext[h, r_hi] + kext[h, r_lo]
                hi = np.asarray(v, dtype=bf).astype(np.float64)
                kext[h, r_hi] = hi
                kext[h, r_lo] = v - hi
        m['qext'] = qext.astype(np.float32)
        m['kext'] = kext.astype(np.float32)
        maps.append(m)
    return maps


def build(cfg, debug_out=None):
    nc = bass.Bass('TRN2', target_bir_lowering=False)
    SEQ, DM, INNER, H, HD = cfg.SEQ, cfg.DM, cfg.INNER, cfg.H, cfg.HD
    NT, NC, OWN, NTO, QS = cfg.NT, cfg.NC, cfg.OWN, cfg.NTO, cfg.QS
    LNEPS_O = np.float32(1e-5)
    LNEPS_I = np.float32(1e-5)
    FEPS = 1e-6

    HL = [(i * 512, min(512, DM - i * 512)) for i in range((DM + 511) // 512)]

    D = {}

    def din(name, shape, dt):
        D[name] = nc.dram_tensor(name, list(shape), dt, kind='ExternalInput').ap()
        return D[name]

    def dint(name, shape, dt):
        D[name] = nc.dram_tensor(name, list(shape), dt).ap()
        return D[name]

    din('u', [SEQ, DM], F32)
    din('Bt', [DM, DM], F32); din('Ct', [DM, DM], F32); din('Dt', [DM, DM], F32)
    din('Adiag', [128, NC], F32); din('h0v', [128, NC], F32)
    din('lg', [DM], F32); din('lb', [DM], F32)
    din('W8t', [DM, DM], I8); din('wsc', [DM], F32); din('fbias', [DM], F32)
    din('Wqt', [DM, DM], F32); din('Wkt', [DM, DM], F32)
    din('Wvt', [DM, DM], F32); din('Wot', [DM, DM], F32); din('bo', [DM], F32)
    din('Wgt', [DM, INNER], F32); din('Wut', [DM, INNER], F32)
    din('Wdt', [INNER, DM], F32)
    din('identb', [128, 128], BF16)
    din('identf', [128, 128], F32)
    din('maskdiag', [128, 128], F32)
    din('maskb', [128, (QS // 128) * QS], BF16)
    din('qext', [H, 10, OWN], F32)
    din('kext', [H, 10, SEQ], F32)
    din('mflag', [128, 1], F32)

    out_ap = nc.dram_tensor('out', [OWN, DM], F32, kind='ExternalOutput').ap()
    out_names = ['out']
    dbg_ap = None
    if debug_out:
        dbg_ap = nc.dram_tensor('dbg', [SEQ, DM], F32, kind='ExternalOutput').ap()
        out_names.append('dbg')

    dint('qTd', [DM, OWN], BF16)
    dint('x1d', [SEQ, DM], F32)
    dint('x2d', [SEQ, DM], F32)
    dint('x3d', [OWN, DM], F32)
    dint('kppd', [H, cfg.KE, SEQ], BF16)
    dint('vppd', [SEQ, DM], BF16)
    dint('Tgd', [DM, INNER], BF16)
    dint('Tud', [DM, INNER], BF16)
    dint('Tdd', [INNER, DM], BF16)
    dint('hd', [OWN, INNER], BF16)
    dint('hd2', [OWN, INNER], BF16)

    pb = PB(nc)
    S = pb.sbuf
    E = pb.emit

    X = S('X', [128, NT * DM], BF16)
    _sp = S('spool', [128, NT * 16 + NC * 4 + 128 * 3 + 280], F32)
    _spc = [0]

    def SP(w, p0=128):
        o = _spc[0]
        _spc[0] += w
        assert _spc[0] <= _sp.shape[1]
        return _sp[0:p0, o:o + w]

    stat = SP(NT * 12)
    mv = SP(NT * 2)
    rs_t = SP(NT)
    nmrs = SP(NT)
    sq = SP(NT)
    amax = SP(NT)
    qrec = SP(NT)
    qscl = SP(NT)

    bc1 = S('bc1', [128, DM], BF16)
    bc2 = S('bc2', [128, DM], BF16)
    gbc, bbc, wscbc, fbbc, bobc = bc1, bc2, bc1, bc2, bc1
    identb = S('identb_s', [128, 128], BF16)
    adec = SP(NC)
    h0s = SP(NC)
    zrow = S('zrow', [128, 1], BF16)
    mflag = S('mflag_s', [128, 1], F32)

    WA = S('WA', [128, NC * DM], BF16)
    WB = S('WB', [128, NC * DM], BF16)

    z0f0_ = S('z0f0', [128, DM], BF16)
    z0f = [z0f0_, z0f0_]
    z1f0 = S('z1f0', [128, DM], F32)
    z1f = [z1f0, z1f0]
    xf_a = S('xf_a', [128, DM], F32)
    qws = S('qws', [128, DM], F32)
    xf_b = z1f0
    zt0 = S('zt0', [128, NC * 128], BF16)
    zt = [zt0, zt0]
    acc0_ = S('acc0', [128, DM], BF16)
    acc_t = [acc0_, acc0_]
    nrm0 = S('nrm0', [128, DM], BF16)
    nrm_t = [nrm0, nrm0]
    res0 = acc0_
    res_t = [res0, res0]
    m320 = S('m320', [128, 512], U32)
    m32_t = [m320, m320]
    z0Ts0 = S('z0Ts0', [128, NC * 512], BF16)
    z0Ts = [z0Ts0, z0Ts0]

    buT = S('buT', [128, NC * SEQ], BF16)
    xf_a2 = S('xf_a2', [128, DM], F32)
    z1fb = S('z1fb', [128, DM], F32)
    qws2 = qws
    IH0 = INNER // 2
    _arena_need = 4 * DM + 2 * IH0 + 2 * INNER
    _use_arena = (NC * OWN) >= _arena_need
    _aoff = [0]

    def bview(w, nm):
        if _use_arena:
            o = _aoff[0]
            _aoff[0] += w
            return buT[:, o:o + w]
        return S(nm, [128, w], BF16)[:]

    q_rv = bview(DM, 'q_rA')
    q_dv = bview(DM, 'q_dA')
    q_cv = bview(DM, 'q_cA')
    xq_tv = bview(DM, 'xq_tA')
    gtile = bview(IH0, 'gtileA')
    util = bview(IH0, 'utilA')
    htile = bview(INNER, 'htileA')
    hqT = bview(INNER, 'hqTA')
    HALF0 = NC * OWN

    def bview2(off, w, nm):
        if _use_arena:
            return buT[:, HALF0 + off:HALF0 + off + w]
        return S(nm, [128, w], BF16)[:]

    zt2v = bview2(0, NC * 128, 'zt2A')
    xq_t2v = bview2(DM, DM, 'xq2A')
    q_c2v = bview2(2 * DM, DM, 'c2A')
    acc2_ = bview2(3 * DM, DM, 'acc2A')
    nrm2_ = bview2(4 * DM, DM, 'nrm2A')
    htile2 = bview2(5 * DM, INNER, 'htile2A')
    hqT2 = bview2(5 * DM + INNER, INNER, 'hqT2A')
    PAR1KEYS = ['zt1k', 'xq1k', 'qc1k', 'acc1k', 'nrm1k', 'htile1', 'hqT1_0', 'hqT1_1']

    q_v = [z1f0, z1fb]
    q_r = [q_rv, q_rv]
    q_d = [q_dv, q_dv]
    q_c = [q_cv, q_c2v]
    xq_t = [xq_tv, xq_t2v]
    qws_t = [qws, qws2]
    xfa_t = [xf_a, xf_a2]
    q_i = [None, None]
    zt[1] = zt2v
    acc_t[1] = acc2_
    nrm_t[1] = nrm2_
    z1f[1] = z1fb

    ps_big = pb.psum('ps_big', [128, 1024], F32)
    ps_y = pb.psum('ps_y', [128, DM], F32)
    ps_t = pb.psum('ps_t', [128, 512], F32)
    ps_c = pb.psum('ps_c', [128, 512], F32)
    ps_tb = pb.psum('ps_tb', [128, NC * 128], BF16)

    ur = D['u'].rearrange('(t p) d -> p t d', p=128)
    Xr = X[:].rearrange('p (t d) -> p t d', d=DM)
    ng = 4
    for g in range(ng):
        t0, t1 = g * NT // ng, (g + 1) * NT // ng
        E('gpsimd', lambda h, t0=t0, t1=t1: h.dma_start(out=Xr[:, t0:t1], in_=ur[:, t0:t1]),
          writes=[f'X{t}' for t in range(t0, t1)], dma=True)

    E('sync', lambda h: h.dma_start(out=mflag[:], in_=D['mflag'][:]), writes=['mflag'], dma=True)
    E('sync', lambda h: h.dma_start(out=identb[:], in_=D['identb'][:]), writes=['identb'], dma=True)

    E('sync', lambda h: h.dma_start(out=adec[:], in_=D['Adiag'][:]), writes=['adec'], dma=True)
    E('sync', lambda h: h.dma_start(out=h0s[:], in_=D['h0v'][:]), writes=['h0s'], dma=True)
    E('vector', lambda h: h.memset(zrow[:], 0.0), writes=['zrow'])

    def bcast_load(dst, src_name, key, dt):
        src = D[src_name]
        srcb = bass.AP(src.tensor, 0, [[0, 128], [1, DM]])
        E('gpsimd', lambda h: h.dma_start(out=dst[:], in_=srcb), writes=[key], dma=True)
    bcast_load(gbc, 'lg', 'gbc', BF16)
    bcast_load(bbc, 'lb', 'bbc', BF16)

    E('scalar', lambda h: h.activation(out=adec[:], in_=adec[:], func=AF.Exp),
      reads=['adec'], writes=['adec'])
    E('scalar', lambda h: h.activation(out=adec[:], in_=adec[:], func=AF.Ln, bias=np.float32(1.0), scale=np.float32(1.0)),
      reads=['adec'], writes=['adec'])
    E('vector', lambda h: h.tensor_single_scalar(adec[:], adec[:], -1.0, ALU.mult),
      reads=['adec'], writes=['adec'])

    def wload(dst, src_name, key, width):
        src = D[src_name].rearrange('(c p) o -> p c o', p=128)
        E('gpsimd', lambda h: h.dma_start(
            out=dst[:].rearrange('p (c o) -> p c o', o=width), in_=src),
          writes=[key], dma=True)
    wload(WA, 'Bt', 'WA', DM)
    wload(WB, 'Ct', 'WB', DM)

    def ln_stats(src_key_fn, src_ap_fn, tag, n=None):
        n = n if n is not None else NT
        for t in range(n):
            ap = src_ap_fn(t)
            for hh, (c0, cw) in enumerate(HL):
                E('vector', lambda h, t=t, hh=hh, c0=c0, cw=cw, ap=ap: h.bn_stats(
                    stat[:, t * 12 + hh * 6: t * 12 + hh * 6 + 6],
                    ap[:, c0:c0 + cw]),
                  reads=[src_key_fn(t)], writes=[f'st{t}'])
            E('vector', lambda h, t=t: h.bn_aggr(
                mv[:, t * 2: t * 2 + 2], stat[:, t * 12: t * 12 + 6 * len(HL)]),
              reads=[f'st{t}'], writes=[f'mv{t}'])
        mvr = [f'mv{t}' for t in range(n)]
        E('scalar', lambda h: h.activation(out=sq[:], in_=mv[:, 1::2], func=AF.Sqrt,
                                           bias=LNEPS_O, scale=np.float32(1.0)),
          reads=mvr, writes=['sq'])
        E('vector', lambda h: h.reciprocal(rs_t[:], sq[:]), reads=['sq'], writes=['rs'])
        E('vector', lambda h: h.scalar_tensor_tensor(nmrs[:], mv[:, 0::2], -1.0, rs_t[:],
                                                     ALU.mult, ALU.mult),
          reads=mvr + ['rs'], writes=['nmrs'])

    def apply_ln(dst, dst_key, t, src_ap, src_key, func=AF.Identity):
        E('scalar', lambda h: h.activation(out=dst[:], in_=src_ap, func=func,
                                           bias=nmrs[:, t:t + 1], scale=rs_t[:, t:t + 1]),
          reads=[src_key, 'rs', 'nmrs'], writes=[dst_key])

    gstat2 = SP(12)
    gmv2 = SP(2)
    gsq2 = SP(1)
    grs2 = SP(1)
    gnm2 = SP(1)

    def tile_stats(ap_, key_, eps, par=0):
        st_, mv_, sq_, rs_, nm_ = ((ginner_stat, gmv, gsq, grs, gnm) if par == 0
                                   else (gstat2, gmv2, gsq2, grs2, gnm2))
        sfx = '' if par == 0 else '2'
        for hh, (c0, cw) in enumerate(HL):
            E('vector', lambda h, hh=hh, c0=c0, cw=cw, st_=st_: h.bn_stats(
                st_[:, hh * 6:hh * 6 + 6], ap_[:, c0:c0 + cw]),
              reads=[key_], writes=['gst' + sfx])
        E('vector', lambda h, st_=st_, mv_=mv_: h.bn_aggr(mv_[:], st_[:, :6 * len(HL)]),
          reads=['gst' + sfx], writes=['gmv' + sfx])
        E('scalar', lambda h, eps=eps, mv_=mv_, sq_=sq_: h.activation(
            out=sq_[:], in_=mv_[:, 1:2], func=AF.Sqrt, bias=eps, scale=np.float32(1.0)),
          reads=['gmv' + sfx], writes=['gsq' + sfx])
        E('vector', lambda h, sq_=sq_, rs_=rs_: h.reciprocal(rs_[:], sq_[:]),
          reads=['gsq' + sfx], writes=['grs' + sfx])
        E('vector', lambda h, mv_=mv_, rs_=rs_, nm_=nm_: h.scalar_tensor_tensor(
            nm_[:], mv_[:, 0:1], -1.0, rs_[:], ALU.mult, ALU.mult),
          reads=['gmv' + sfx, 'grs' + sfx], writes=['gnm' + sfx])
        return rs_, nm_, 'grs' + sfx, 'gnm' + sfx

    def quant_round(i, vv_, vkey, K=1):
        r, d, xq, xi = q_r[i], q_d[i], xq_t[i], q_i[i]
        rk, dk, xqk, xik = ('q_rv', 'q_dv', 'xq_tv', 'q_iv')
        for (c0, cw) in HL:
            sl = slice(c0, c0 + cw)
            E('vector', lambda h, sl=sl, cw=cw: h.tensor_copy(xi[:, 0:cw], vv_[:, sl]),
              reads=[vkey], writes=[xik])
            E('vector', lambda h, sl=sl, cw=cw, K=K: h.tensor_single_scalar(
                r[:, sl], xi[:, 0:cw], float(-K), ALU.add),
              reads=[xik], writes=[rk])
        E('vector', lambda h: h.scalar_tensor_tensor(d, r, -1.0, vv_[:],
                                                     ALU.mult, ALU.add),
          reads=[rk, vkey], writes=[dk])
        for j in range(1, 2 * K + 1):
            th = float(j) - 0.5
            if j < 2 * K:
                E('vector', lambda h, th=th: h.scalar_tensor_tensor(
                    r, d, th, r, ALU.is_gt, ALU.add),
                  reads=[dk, rk], writes=[rk])
            else:
                E('vector', lambda h, th=th: h.scalar_tensor_tensor(
                    xq, d, th, r, ALU.is_gt, ALU.add),
                  reads=[dk, rk], writes=[xqk])
        return xq, xqk

    def quant_round_full(i, vv_, vkey, wsave=None, wk2=None, eng='vector'):
        xq, xqk = xq_t[i], f'xq{i}k'
        c, ck = q_c[i], f'qc{i}k'
        if wsave is None:
            wsave, wk2 = qws_t[i], 'qwsk'
        E(eng, lambda h: h.tensor_single_scalar(vv_[:], vv_[:], 128.5, ALU.add),
          reads=[vkey], writes=[vkey])
        if eng == 'vector':
            E(eng, lambda h, wsave=wsave: h.tensor_copy(wsave[:], vv_[:]),
              reads=[vkey], writes=[wk2])
            for s in (128.0, 64.0, 32.0, 16.0, 8.0, 4.0, 2.0, 1.0):
                E(eng, lambda h, s=s: h.tensor_single_scalar(c, vv_[:], s, ALU.is_ge),
                  reads=[vkey], writes=[ck])
                E(eng, lambda h, s=s: h.scalar_tensor_tensor(
                    vv_[:], c, -s, vv_[:], ALU.mult, ALU.add),
                  reads=[ck, vkey], writes=[vkey])
            E(eng, lambda h, wsave=wsave: h.tensor_single_scalar(
                wsave[:], wsave[:], -128.0, ALU.add),
              reads=[wk2], writes=[wk2])
            E(eng, lambda h, wsave=wsave: h.scalar_tensor_tensor(
                xq, vv_[:], -1.0, wsave[:], ALU.mult, ALU.add),
              reads=[vkey, wk2], writes=[xqk])
        else:
            E(eng, lambda h: h.tensor_scalar(xq, vv_[:], 0.0, -128.0,
                                             ALU.mult, ALU.add),
              reads=[vkey], writes=[xqk])
            for s in (128.0, 64.0, 32.0, 16.0, 8.0, 4.0, 2.0, 1.0):
                E(eng, lambda h, s=s: h.tensor_scalar(c, vv_[:], s, -s,
                                                      ALU.is_ge, ALU.mult),
                  reads=[vkey], writes=[ck])
                E(eng, lambda h, s=s: h.tensor_tensor(vv_[:], vv_[:], c, ALU.add),
                  reads=[ck, vkey], writes=[vkey])
                E(eng, lambda h, s=s: h.tensor_tensor(xq, xq, c, ALU.subtract),
                  reads=[ck, xqk], writes=[xqk])
        return xq, xqk

    regions = {}

    def wkey(region, key):
        regions.setdefault(region, []).append(key)
        return key

    def rkeys(region):
        return list(regions.get(region, []))

    def _dbg_exit():
        E('vector', lambda h: h.tensor_copy(z1f0[:], Xt(0)), reads=['X0'], writes=['z1fk'])
        E('sync', lambda h: h.dma_start(out=out_ap[0:128, :], in_=z1f0[:]),
          reads=['z1fk'], dma=True)
        pb.wait_all_done('sync')
        pb.build()
        return nc, out_names

    def Xt(t):
        return X[:, t * DM:(t + 1) * DM]

    if debug_out == 'p0':
        return _dbg_exit()
    ln_stats(lambda t: f'X{t}', Xt, 'ln0')
    if debug_out == 'ln0':
        return _dbg_exit()

    nslice = SEQ // 512
    for ts in range(nslice):
        zb = z0Ts[0]
        zbk = 'z0Ts0'
        for i in range(4):
            t = ts * 4 + i
            zf = z0f[0]
            apply_ln(zf, 'z0f0k', t, Xt(t), f'X{t}')
            for c in range(NC):
                E('tensor', lambda h, c=c, zf=zf: h.transpose(
                    ps_tb[:, c * 128:(c + 1) * 128], zf[:, c * 128:(c + 1) * 128], identb[:]),
                  reads=['z0f0k', 'identb'], writes=['ps_tb'],
                  inc=(c == NC - 1))
            E('scalar', lambda h, i=i, zb=zb: h.activation(
                out=zb[:].rearrange('p (c q) -> p c q', q=512)[:, :, i * 128:(i + 1) * 128],
                in_=ps_tb[:, :NC * 128].rearrange('p (c q) -> p c q', q=128),
                func=AF.Copy),
              reads=['ps_tb'], writes=[f'{zbk}_{i}'])
        for j in range(NC):
            pst, pstk = (ps_t, 'ps_t') if j % 2 == 0 else (ps_c, 'ps_c')
            for kc in range(NC):
                E('tensor', lambda h, j=j, kc=kc, zb=zb, pst=pst: h.matmul(
                    pst[:, :512], WA[:, kc * DM + j * 128: kc * DM + (j + 1) * 128],
                    zb[:, kc * 512:(kc + 1) * 512],
                    start=(kc == 0), stop=(kc == NC - 1)),
                  reads=['WA'] + [f'{zbk}_{i}' for i in range(4)],
                  writes=[pstk], inc=(kc == NC - 1))
            E('scalar', lambda h, j=j, ts=ts, pst=pst: h.activation(
                out=buT[:, j * SEQ + ts * 512: j * SEQ + (ts + 1) * 512],
                in_=pst[:, :512], func=AF.Copy),
              reads=[pstk], writes=[f'bu{j}_{ts}'])

    if debug_out == 'bu':
        return _dbg_exit()
    if debug_out == 'buc':
        for j in range(NC):
            for q in range(SEQ // DM):
                E('vector', lambda h, j=j, q=q: h.tensor_copy(
                    z1f0[:], buT[:, j * SEQ + q * DM: j * SEQ + (q + 1) * DM]),
                  reads=[f'bu{j}_{ts}' for ts in range(nslice)], writes=['z1fk'])
                E('sync', lambda h, j=j, q=q: h.dma_start(
                    out=dbg_ap[(j * (SEQ // DM) + q) * 128:(j * (SEQ // DM) + q + 1) * 128, :],
                    in_=z1f0[:]),
                  reads=['z1fk'], dma=True)
        pb.wait_all_done('sync')
        pb.build()
        return nc, out_names
    L2 = SEQ // 2
    PW = min(L2, 1024)
    if _use_arena:
        pwt = z0Ts0[:, 0:PW]
    else:
        pwt = S('pwt', [128, PW], BF16)[:]
    RW = min(PW, DM)
    lnab = SP(NC)
    sc_end = SP(2)
    sc_ini = SP(2)
    sc_tmp = SP(2)
    aL2 = SP(NC)
    mneg = SP(1)

    E('scalar', lambda h: h.activation(out=lnab[:], in_=adec[:], func=AF.Ln,
                                       scale=np.float32(-1.0)),
      reads=['adec'], writes=['lnab'])
    E('vector', lambda h: h.tensor_single_scalar(aL2[:], lnab[:], float(L2), ALU.mult),
      reads=['lnab'], writes=['aL2'])
    E('scalar', lambda h: h.activation(out=aL2[:], in_=aL2[:], func=AF.Exp),
      reads=['aL2'], writes=['aL2'])
    E('vector', lambda h: h.engine_nop(),
      reads=['z0Ts0_0', 'z0Ts0_1', 'z0Ts0_2', 'z0Ts0_3'], writes=['pwt'])
    E('vector', lambda h: h.tensor_single_scalar(mneg[:], mflag[:], -1.0, ALU.mult),
      reads=['mflag'], writes=['mneg'])
    E('vector', lambda h: h.tensor_single_scalar(mneg[:], mneg[:], 1.0, ALU.add),
      reads=['mneg'], writes=['mneg'])
    for j in range(NC):
        burd = [f'bu{j}_{ts}' for ts in range(nslice)]
        for hf in range(2):
            E('vector', lambda h, j=j, hf=hf: h.tensor_tensor_scan(
                buT[:, j * SEQ + hf * L2: j * SEQ + (hf + 1) * L2],
                adec[:, j:j + 1].broadcast_to([128, L2]),
                buT[:, j * SEQ + hf * L2: j * SEQ + (hf + 1) * L2],
                0.0, ALU.mult, ALU.add),
              reads=burd + ['adec'], writes=[f'xsl{j}_{hf}'])
        E('vector', lambda h, j=j: h.tensor_copy(
            sc_end[:, 0:1], buT[:, j * SEQ + L2 - 1: j * SEQ + L2]),
          reads=[f'xsl{j}_0'], writes=['sc_end'])
        E('vector', lambda h, j=j: h.tensor_copy(
            sc_end[:, 1:2], buT[:, j * SEQ + SEQ - 1: j * SEQ + SEQ]),
          reads=[f'xsl{j}_1'], writes=['sc_end2'])
        E('vector', lambda h, j=j: h.tensor_scalar(
            sc_tmp[:, 0:2], sc_end[:, 0:2], aL2[:, j:j + 1], None, ALU.bypass),
          reads=['sc_end', 'sc_end2', 'aL2'], writes=['sc_tmpz'], inc=True) if False else None
        E('vector', lambda h, j=j: h.scalar_tensor_tensor(
            sc_tmp[:, 0:1], h0s[:, j:j + 1], aL2[:, j:j + 1], sc_end[:, 0:1],
            ALU.mult, ALU.add),
          reads=['h0s', 'aL2', 'sc_end'], writes=['sc_tmpL'])
        E('vector', lambda h, j=j: h.scalar_tensor_tensor(
            sc_tmp[:, 1:2], h0s[:, j:j + 1], aL2[:, j:j + 1], sc_end[:, 1:2],
            ALU.mult, ALU.add),
          reads=['h0s', 'aL2', 'sc_end2'], writes=['sc_tmpR'])
        E('vector', lambda h, j=j: h.tensor_scalar(
            sc_ini[:, 0:1], h0s[:, j:j + 1], mflag[:, 0:1], None, ALU.mult),
          reads=['h0s', 'mflag'], writes=['sc_iniL'])
        E('vector', lambda h, j=j: h.scalar_tensor_tensor(
            sc_ini[:, 0:1], sc_tmp[:, 1:2], mneg[:, 0:1], sc_ini[:, 0:1],
            ALU.mult, ALU.add),
          reads=['sc_tmpR', 'mneg', 'sc_iniL'], writes=['sc_iniL'])
        E('vector', lambda h, j=j: h.tensor_scalar(
            sc_ini[:, 1:2], h0s[:, j:j + 1], mneg[:, 0:1], None, ALU.mult),
          reads=['h0s', 'mneg'], writes=['sc_iniR'])
        E('vector', lambda h, j=j: h.scalar_tensor_tensor(
            sc_ini[:, 1:2], sc_tmp[:, 0:1], mflag[:, 0:1], sc_ini[:, 1:2],
            ALU.mult, ALU.add),
          reads=['sc_tmpL', 'mflag', 'sc_iniR'], writes=['sc_iniR'])
        E('vector', lambda h: h.tensor_single_scalar(sc_tmp[:, 0:2], sc_ini[:, 0:2],
                                                     -1.0, ALU.mult),
          reads=['sc_iniL', 'sc_iniR'], writes=['sc_nin'])
        for sg in range(L2 // PW):
            for rh in range(PW // RW):
                E('gpsimd', lambda h, sg=sg, rh=rh: h.iota(
                    z1f0[:, 0:RW], [[1, RW]], base=1 + sg * PW + rh * RW,
                    channel_multiplier=0, allow_small_or_imprecise_dtypes=True),
                  writes=['z1fk'])
                E('scalar', lambda h, j=j, rh=rh: h.activation(
                    out=pwt[:, rh * RW:(rh + 1) * RW], in_=z1f0[:, 0:RW], func=AF.Exp,
                    scale=lnab[:, j:j + 1]),
                  reads=['z1fk', 'lnab'], writes=['pwt'])
            for hf in range(2):
                col = hf
                sb_ = buT[:, j * SEQ + hf * L2 + sg * PW: j * SEQ + hf * L2 + (sg + 1) * PW]
                E('vector', lambda h, j=j, hf=hf, col=col, sb_=sb_: h.scalar_tensor_tensor(
                    sb_[:, 0::2], pwt[:, 0::2], sc_tmp[:, col:col + 1], sb_[:, 0::2],
                    ALU.mult, ALU.add),
                  reads=['pwt', 'sc_nin', f'xsl{j}_{hf}'], writes=[f'xsl{j}_{hf}'])
                E('vector', lambda h, j=j, hf=hf, col=col, sb_=sb_: h.scalar_tensor_tensor(
                    sb_[:, 1::2], pwt[:, 1::2], sc_ini[:, col:col + 1], sb_[:, 1::2],
                    ALU.mult, ALU.add),
                  reads=['pwt', 'sc_iniL', 'sc_iniR', f'xsl{j}_{hf}'],
                  writes=[f'xsl{j}_{hf}'] + ([f'xs{j}'] if hf == 1 and sg == L2 // PW - 1 else []))

    if debug_out == 'scan':
        return _dbg_exit()
    if debug_out == 'xs':
        for j in range(NC):
            for q in range(SEQ // DM):
                E('vector', lambda h, j=j, q=q: h.tensor_copy(
                    z1f0[:], buT[:, j * SEQ + q * DM: j * SEQ + (q + 1) * DM]),
                  reads=[f'xs{j}'], writes=['z1fk'])
                E('sync', lambda h, j=j, q=q: h.dma_start(
                    out=dbg_ap[(j * (SEQ // DM) + q) * 128:(j * (SEQ // DM) + q + 1) * 128, :],
                    in_=z1f0[:]),
                  reads=['z1fk'], dma=True)
        pb.wait_all_done('sync')
        pb.build()
        return nc, out_names

    wload(WA, 'Dt', 'WA', DM)
    ginner_stat = SP(12)
    gmv = SP(2)
    gsq = SP(1)
    grs = SP(1)
    gnm = SP(1)
    for t in range(NT):
        zf = z0f[0]
        zc = zt[0]
        zck = 'zt0k'
        apply_ln(zf, 'z0f0k', t, Xt(t), f'X{t}')
        for c in range(NC):
            E('tensor', lambda h, c=c, zf=zf: h.transpose(
                ps_tb[:, c * 128:(c + 1) * 128], zf[:, c * 128:(c + 1) * 128], identb[:]),
              reads=['z0f0k', 'identb'], writes=['ps_tb'],
              inc=(c == NC - 1))
        E('scalar', lambda h, zc=zc: h.activation(out=zc[:, :NC * 128], in_=ps_tb[:, :NC * 128],
                                                  func=AF.Copy),
          reads=['ps_tb'], writes=[zck])
        for j in range(NC):
            for hi_, (c0, cw) in enumerate(HL):
                E('tensor', lambda h, j=j, t=t, c0=c0, cw=cw: h.matmul(
                    ps_y[:, c0:c0 + cw], buT[:, j * SEQ + t * 128: j * SEQ + (t + 1) * 128],
                    WB[:, j * DM + c0: j * DM + c0 + cw],
                    start=(j == 0), stop=False),
                  reads=[f'xs{j}', 'WB'], writes=['ps_y'], inc=False)
        for j in range(NC):
            for hi_, (c0, cw) in enumerate(HL):
                E('tensor', lambda h, j=j, zc=zc, c0=c0, cw=cw: h.matmul(
                    ps_y[:, c0:c0 + cw], zc[:, j * 128:(j + 1) * 128],
                    WA[:, j * DM + c0: j * DM + c0 + cw],
                    start=False, stop=(j == NC - 1)),
                  reads=[zck, 'WA'], writes=['ps_y'],
                  inc=(j == NC - 1 and hi_ == len(HL) - 1))
        at = acc_t[0]
        atk = 'acck'
        E('scalar', lambda h, at=at: h.activation(out=at[:], in_=ps_y[:, :DM], func=AF.Gelu),
          reads=['ps_y'], writes=[atk])
        for hh, (c0, cw) in enumerate(HL):
            E('vector', lambda h, hh=hh, c0=c0, cw=cw, at=at: h.bn_stats(
                ginner_stat[:, hh * 6:hh * 6 + 6], at[:, c0:c0 + cw]),
              reads=[atk], writes=['gst'])
        E('vector', lambda h: h.bn_aggr(gmv[:], ginner_stat[:, :6 * len(HL)]),
          reads=['gst'], writes=['gmv'])
        E('scalar', lambda h: h.activation(out=gsq[:], in_=gmv[:, 1:2], func=AF.Sqrt,
                                           bias=LNEPS_I, scale=np.float32(1.0)),
          reads=['gmv'], writes=['gsq'])
        E('vector', lambda h: h.reciprocal(grs[:], gsq[:]), reads=['gsq'], writes=['grs'])
        E('vector', lambda h: h.scalar_tensor_tensor(gnm[:], gmv[:, 0:1], -1.0, grs[:],
                                                     ALU.mult, ALU.mult),
          reads=['gmv', 'grs'], writes=['gnm'])
        nt_ = nrm_t[0]
        ntk = 'nrm0k'
        E('scalar', lambda h, at=at, nt_=nt_: h.activation(
            out=nt_[:], in_=at[:], func=AF.Identity, bias=gnm[:, 0:1], scale=grs[:, 0:1]),
          reads=[atk, 'grs', 'gnm'], writes=[ntk])
        E('vector', lambda h, nt_=nt_: h.tensor_tensor(nt_[:], nt_[:], gbc[:], ALU.mult),
          reads=[ntk, 'gbc'], writes=[ntk])
        E('vector', lambda h, nt_=nt_: h.tensor_tensor(nt_[:], nt_[:], bbc[:], ALU.add),
          reads=[ntk, 'bbc'], writes=[ntk])
        E('vector', lambda h, nt_=nt_: h.tensor_scalar(nt_[:], nt_[:], -1e6, 1e6,
                                                       ALU.max, ALU.min),
          reads=[ntk], writes=[ntk])
        mt = m32_t[0]
        for hh, (c0, cw) in enumerate(HL):
            E('vector', lambda h, nt_=nt_, mt=mt, c0=c0, cw=cw: h.tensor_tensor(
                mt[:, 0:cw], nt_[:, c0:c0 + cw], nt_[:, c0:c0 + cw], ALU.not_equal),
              reads=[ntk], writes=['m32k'])
            E('vector', lambda h, nt_=nt_, mt=mt, c0=c0, cw=cw: h.copy_predicated(
                nt_[:, c0:c0 + cw], mt[:, 0:cw], zrow[:, 0:1].broadcast_to([128, cw])),
              reads=['m32k', 'zrow', ntk], writes=[ntk])
        for hh, (c0, cw) in enumerate(HL):
            E('vector', lambda h, hh=hh, c0=c0, cw=cw, nt_=nt_: h.bn_stats(
                ginner_stat[:, hh * 6:hh * 6 + 6], nt_[:, c0:c0 + cw]),
              reads=[ntk], writes=['gst'])
        E('vector', lambda h: h.bn_aggr(gmv[:], ginner_stat[:, :6 * len(HL)]),
          reads=['gst'], writes=['gmv'])
        E('scalar', lambda h: h.activation(out=gsq[:], in_=gmv[:, 1:2], func=AF.Sqrt,
                                           bias=LNEPS_O, scale=np.float32(1.0)),
          reads=['gmv'], writes=['gsq'])
        E('vector', lambda h: h.reciprocal(grs[:], gsq[:]), reads=['gsq'], writes=['grs'])
        E('vector', lambda h: h.scalar_tensor_tensor(gnm[:], gmv[:, 0:1], -1.0, grs[:],
                                                     ALU.mult, ALU.mult),
          reads=['gmv', 'grs'], writes=['gnm'])
        rt = res_t[0]
        rtk = 'acck'
        E('scalar', lambda h, nt_=nt_, rt=rt: h.activation(
            out=rt[:], in_=nt_[:], func=AF.Identity, bias=gnm[:, 0:1], scale=grs[:, 0:1]),
          reads=[ntk, 'grs', 'gnm'], writes=[rtk])
        E('sync', lambda h, t=t: h.dma_start(out=xf_a[:], in_=D['u'][t * 128:(t + 1) * 128, :]),
          writes=['xfa'], dma=True)
        E('vector', lambda h, rt=rt: h.tensor_tensor(xf_a[:], xf_a[:], rt[:], ALU.add),
          reads=['xfa', rtk], writes=['xfa'])
        E('scalar', lambda h, t=t: h.dma_start(out=D['x1d'][t * 128:(t + 1) * 128, :],
                                               in_=xf_a[:]),
          reads=['xfa'], writes=[wkey('x1d', f'x1d{t}')], dma=True)
        E('vector', lambda h, rt=rt, t=t: h.tensor_tensor(Xt(t), Xt(t), rt[:], ALU.add),
          reads=[rtk, f'X{t}'], writes=[f'X{t}'])

    if debug_out == 'p1':
        return _dbg_exit()
    if debug_out == 'x1':
        for t in range(NT):
            E('sync', lambda h, t=t: h.dma_start(out=xf_a[:], in_=D['x1d'][t * 128:(t + 1) * 128, :]),
              reads=rkeys('x1d'), writes=['xfa'], dma=True)
            E('scalar', lambda h, t=t: h.dma_start(out=dbg_ap[t * 128:(t + 1) * 128, :], in_=xf_a[:]),
              reads=['xfa'], dma=True)
        pb.wait_all_done('sync')
        pb.build()
        return nc, out_names

    bcast_load(wscbc, 'wsc', 'gbc', BF16)
    bcast_load(fbbc, 'fbias', 'bbc', BF16)
    wload(WB, 'W8t', 'WB', DM)
    for t in range(NT):
        par = t % 2
        xfa_, xfak = xfa_t[par], ('xfa' if par == 0 else 'xfa1')
        zf32, zfk = z1f[par], ('z1fk' if par == 0 else 'z1fbk')
        zc, zck = zt[par], f'zt{par}k'
        at, atk = acc_t[par], ('acck' if par == 0 else 'acc1k')
        nt_, ntk = nrm_t[par], ('nrm0k' if par == 0 else 'nrm1k')
        psY, psYk = (ps_y, 'ps_y') if par == 0 else (ps_big, 'ps_big')
        E('sync', lambda h, t=t, xfa_=xfa_: h.dma_start(
            out=xfa_[:], in_=D['x1d'][t * 128:(t + 1) * 128, :]),
          reads=rkeys('x1d'), writes=[xfak], dma=True)
        rs_, nm_, rsk, nmk = tile_stats(xfa_, xfak, LNEPS_O, par)
        E('scalar', lambda h, zf32=zf32, xfa_=xfa_, rs_=rs_, nm_=nm_: h.activation(
            out=zf32[:], in_=xfa_[:], func=AF.Identity, bias=nm_[:, 0:1], scale=rs_[:, 0:1]),
          reads=[xfak, rsk, nmk], writes=[zfk])
        E('vector', lambda h, zf32=zf32, t=t: h.reduce_max(
            amax[:, t:t + 1], zf32[:], AX.X, apply_absolute_value=True),
          reads=[zfk], writes=[f'am{t}'])
        E('vector', lambda h, t=t: h.tensor_scalar(
            qscl[:, t:t + 1], amax[:, t:t + 1], FEPS, float(1.0 / 127.0), ALU.max, ALU.mult),
          reads=[f'am{t}'], writes=[f'qs{t}'])
        E('vector', lambda h, t=t: h.reciprocal(qrec[:, t:t + 1], qscl[:, t:t + 1]),
          reads=[f'qs{t}'], writes=[f'qr{t}'])
        vv_ = q_v[par]
        E('vector', lambda h, vv_=vv_, zf32=zf32, t=t: h.tensor_scalar(
            vv_[:], zf32[:], qrec[:, t:t + 1], None, ALU.mult),
          reads=[zfk, f'qr{t}'], writes=[zfk])
        xq = quant_round_full(par, vv_, zfk)
        for c in range(NC):
            E('tensor', lambda h, c=c, xq=xq[0]: h.transpose(
                ps_tb[:, c * 128:(c + 1) * 128], xq[:, c * 128:(c + 1) * 128], identb[:]),
              reads=[xq[1], 'identb'], writes=['ps_tb'], inc=(c == NC - 1))
        E('scalar', lambda h, zc=zc: h.activation(out=zc[:, :NC * 128],
                                                  in_=ps_tb[:, :NC * 128], func=AF.Copy),
          reads=['ps_tb'], writes=[zck])
        if debug_out == 'xq0':
            E('vector', lambda h, zc=zc, zf32=zf32: h.tensor_copy(zf32[:], zc[:, :NC * 128]),
              reads=[zck], writes=[zfk])
            E('sync', lambda h, t=t, zf32=zf32: h.dma_start(
                out=dbg_ap[t * 128:(t + 1) * 128, :], in_=zf32[:]),
              reads=[zfk], dma=True)
        for kc in range(NC):
            for hi_, (c0, cw) in enumerate(HL):
                E('tensor', lambda h, kc=kc, zc=zc, c0=c0, cw=cw, psY=psY: h.matmul(
                    psY[:, c0:c0 + cw], zc[:, kc * 128:(kc + 1) * 128],
                    WB[:, kc * DM + c0: kc * DM + c0 + cw],
                    start=(kc == 0), stop=(kc == NC - 1)),
                  reads=[zck, 'WB'], writes=[psYk],
                  inc=(kc == NC - 1 and hi_ == len(HL) - 1))
        E('scalar', lambda h, at=at, t=t, psY=psY: h.activation(
            out=at[:], in_=psY[:, :DM], func=AF.Identity, bias=np.float32(0.0),
            scale=qscl[:, t:t + 1]),
          reads=[psYk, f'qs{t}'], writes=[atk])
        E('vector', lambda h, at=at, nt_=nt_: h.scalar_tensor_tensor(
            nt_[:], at[:], 1.0, wscbc[:], ALU.mult, ALU.mult),
          reads=[atk, 'gbc'], writes=[ntk])
        E('vector', lambda h, nt_=nt_: h.tensor_tensor(nt_[:], nt_[:], fbbc[:], ALU.add),
          reads=[ntk, 'bbc'], writes=[ntk])
        rs_, nm_, rsk, nmk = tile_stats(nt_, ntk, LNEPS_O, par)
        E('scalar', lambda h, nt_=nt_, zf32=zf32, rs_=rs_, nm_=nm_: h.activation(
            out=zf32[:], in_=nt_[:], func=AF.Identity, bias=nm_[:, 0:1], scale=rs_[:, 0:1]),
          reads=[ntk, rsk, nmk], writes=[zfk])
        E('vector', lambda h, xfa_=xfa_, zf32=zf32: h.tensor_tensor(
            xfa_[:], xfa_[:], zf32[:], ALU.add),
          reads=[xfak, zfk], writes=[xfak])
        E('scalar', lambda h, t=t, xfa_=xfa_: h.dma_start(
            out=D['x2d'][t * 128:(t + 1) * 128, :], in_=xfa_[:]),
          reads=[xfak], writes=[wkey('x2d', f'x2d{t}')], dma=True)
        E('vector', lambda h, t=t, xfa_=xfa_: h.tensor_copy(Xt(t), xfa_[:]),
          reads=[xfak, f'X{t}'], writes=[f'X{t}'])

    if debug_out == 'x2':
        for t in range(NT):
            fo = z1f[0]
            E('vector', lambda h, fo=fo, t=t: h.tensor_copy(fo[:], Xt(t)),
              reads=[f'X{t}'], writes=['z1fk'])
            E('sync', lambda h, fo=fo, t=t: h.dma_start(
                out=dbg_ap[t * 128:(t + 1) * 128, :], in_=fo[:]),
              reads=['z1fk'], dma=True)
        E('sync', lambda h: h.dma_start(out=out_ap[0:128, :], in_=z1f[0][:]),
          reads=['z1f0'], dma=True)
        pb.wait_all_done('sync')
        pb.build()
        return nc, out_names

    _tw = max(INNER, DM)
    if _use_arena and NC * OWN + 3 * _tw <= NC * SEQ:
        HALF_ = NC * OWN
        twb0 = buT[:, HALF_:HALF_ + _tw]
        tc1 = buT[:, HALF_ + _tw:HALF_ + 2 * _tw]
        tc2 = buT[:, HALF_ + 2 * _tw:HALF_ + 3 * _tw]
    else:
        twb0 = S('twb0', [128, _tw], BF16)[:]
        tc1 = S('tc1', [128, _tw], BF16)[:]
        tc2 = S('tc2', [128, _tw], BF16)[:]
    twb = [twb0, twb0]
    tsum = SP(1)
    tsum2 = SP(1)
    tth = SP(1)
    tthn = SP(1)
    trow = SP(128, p0=1)
    identf = SP(128)
    wmg = SP(1)
    wmu = SP(1)
    wmd = SP(1)
    onescol = SP(1)
    onesrow = SP(128, p0=1)
    kmaxc = SP(1)
    kmx0 = SP(1, p0=1)
    E('sync', lambda h: h.dma_start(out=identf[:], in_=D['identf'][:]),
      writes=['identf'], dma=True)
    E('vector', lambda h: h.memset(onescol[:], 1.0), writes=['onescol'])
    E('vector', lambda h: h.memset(onesrow[:], 1.0), writes=['onesrow'])
    E('vector', lambda h: h.memset(kmaxc[:], 0.0), writes=['kmaxc'])

    tern_units = []

    def TU(fn):
        tern_units.append(fn)

    TU(lambda: E('vector', lambda h: h.engine_nop(),
                 reads=['zt1k', 'xq1k', 'qc1k', 'acc1k', 'nrm1k', 'z1fbk', 'xfa1',
                        'qws1k', 'acck', 'kslb', 'nrm0k'],
                 writes=['twbk', 'tc1', 'tc2']))

    for wsrc, tdst, wmt, wmk, nrow, ncol in (
            ('Wgt', 'Tgd', wmg, 'wmg', DM, INNER),
            ('Wut', 'Tud', wmu, 'wmu', DM, INNER),
            ('Wdt', 'Tdd', wmd, 'wmd', INNER, DM)):
        nch = nrow // 128
        TU(lambda: E('vector', lambda h: h.memset(tsum[:], 0.0), writes=['tsum']))
        for c in range(nch):
            def u1(c=c, wsrc=wsrc, ncol=ncol):
                tb, tbk = twb[c % 2], 'twbk'
                E('gpsimd', lambda h, c=c, tb=tb, wsrc=wsrc, ncol=ncol: h.dma_start(
                    out=tb[:, 0:ncol], in_=D[wsrc][c * 128:(c + 1) * 128, :]),
                  writes=[tbk], dma=True)
                E('vector', lambda h, tb=tb, ncol=ncol: h.tensor_reduce(
                    tsum2[:], tb[:, 0:ncol], AX.X, ALU.add, apply_absolute_value=True),
                  reads=[tbk], writes=['tsum2'])
                E('vector', lambda h: h.tensor_tensor(tsum[:], tsum[:], tsum2[:], ALU.add),
                  reads=['tsum', 'tsum2'], writes=['tsum'])
            TU(u1)
        def uder(wmt=wmt, wmk=wmk, nrow=nrow, ncol=ncol):
            E('tensor', lambda h: h.transpose(
                ps_t[:, 0:128], tsum[:, 0:1].broadcast_to([128, 128]), identf[:]),
              reads=['tsum', 'identf'], writes=['ps_t'])
            E('vector', lambda h: h.tensor_copy(trow[:], ps_t[0:1, 0:128]),
              reads=['ps_t'], writes=['trow'])
            E('vector', lambda h: h.tensor_reduce(tth[0:1, 0:1], trow[:], AX.X, ALU.add),
              reads=['trow'], writes=['tth1'])
            E('vector', lambda h, nrow=nrow, ncol=ncol: h.tensor_scalar(
                tth[0:1, 0:1], tth[0:1, 0:1], float(1.0 / (nrow * ncol)), 1e-5,
                ALU.mult, ALU.max),
              reads=['tth1'], writes=['tth1'])
            E('vector', lambda h, wmt=wmt: h.tensor_copy(wmt[0:1, 0:1], tth[0:1, 0:1]),
              reads=['tth1'], writes=[wmk + '0'])
            E('tensor', lambda h, wmt=wmt: h.matmul(ps_t[:, 0:1], onesrow[0:1, 0:128],
                                                    wmt[0:1, 0:1], start=True, stop=True),
              reads=[wmk + '0', 'onesrow'], writes=['ps_t'])
            E('vector', lambda h, wmt=wmt: h.tensor_copy(wmt[:, 0:1], ps_t[:, 0:1]),
              reads=['ps_t'], writes=[wmk])
            E('tensor', lambda h: h.matmul(ps_t[:, 0:1], onesrow[0:1, 0:128],
                                           tth[0:1, 0:1], start=True, stop=True),
              reads=['tth1', 'onesrow'], writes=['ps_t'])
            E('vector', lambda h: h.tensor_copy(tth[:, 0:1], ps_t[:, 0:1]),
              reads=['ps_t'], writes=['tthb'])
            E('vector', lambda h: h.tensor_scalar(tth[:, 0:1], tth[:, 0:1],
                                                  0.5, None, ALU.mult),
              reads=['tthb'], writes=['tthb'])
            E('vector', lambda h: h.tensor_single_scalar(tthn[:, 0:1], tth[:, 0:1], -1.0,
                                                         ALU.mult),
              reads=['tthb'], writes=['tthn'])
        TU(uder)
        jw = min(DM, ncol)
        for c in range(nch):
            def u2(c=c, wsrc=wsrc, tdst=tdst, ncol=ncol, jw=jw):
                for j in range(ncol // jw):
                    E('sync', lambda h, c=c, j=j, wsrc=wsrc, jw=jw: h.dma_start(
                        out=xf_a[:, 0:jw],
                        in_=D[wsrc][c * 128:(c + 1) * 128, j * jw:(j + 1) * jw]),
                      writes=['xfa'], dma=True)
                    E('vector', lambda h, j=j, jw=jw: h.tensor_scalar(
                        tc1[:, j * jw:(j + 1) * jw], xf_a[:, 0:jw], tth[:, 0:1],
                        None, ALU.is_gt),
                      reads=['xfa', 'tthb'], writes=['tc1'])
                    E('vector', lambda h, j=j, jw=jw: h.tensor_scalar(
                        tc2[:, j * jw:(j + 1) * jw], xf_a[:, 0:jw], tthn[:, 0:1],
                        None, ALU.is_lt),
                      reads=['xfa', 'tthn'], writes=['tc2'])
                E('vector', lambda h, ncol=ncol: h.tensor_tensor(
                    tc1[:, 0:ncol], tc1[:, 0:ncol], tc2[:, 0:ncol], ALU.subtract),
                  reads=['tc1', 'tc2'], writes=['tc1'])
                E('scalar', lambda h, c=c, tdst=tdst, ncol=ncol: h.dma_start(
                    out=D[tdst][c * 128:(c + 1) * 128, :], in_=tc1[:, 0:ncol]),
                  reads=['tc1'], writes=[wkey(tdst, f'{tdst}_{c}')], dma=True)
            TU(u2)

    if debug_out in ('tw', 'twu', 'twd'):
        for u in tern_units:
            u()
        tern_units = []
        src = {'tw': 'Tgd', 'twu': 'Tud', 'twd': 'Tdd'}[debug_out]
        nr = DM if src != 'Tdd' else INNER
        nc_ = INNER if src != 'Tdd' else DM
        for c in range(nr // 128):
            E('sync', lambda h, c=c, src=src, nc_=nc_: h.dma_start(
                out=twb0[:, 0:nc_], in_=D[src][c * 128:(c + 1) * 128, :]),
              reads=rkeys(src), writes=['twbk'], dma=True)
            for j in range(nc_ // DM):
                E('vector', lambda h, j=j: h.tensor_copy(xf_a[:], twb0[:, j * DM:(j + 1) * DM]),
                  reads=['twbk'], writes=['xfa'])
                E('scalar', lambda h, c=c, j=j, nc_=nc_: h.dma_start(
                    out=dbg_ap[(c * (nc_ // DM) + j) * 128:(c * (nc_ // DM) + j + 1) * 128, :],
                    in_=xf_a[:]),
                  reads=['xfa'], dma=True)
        pb.wait_all_done('sync')
        pb.build()
        return nc, out_names

    ln_stats(lambda t: f'X{t}', Xt, 'ln2')

    nslice = SEQ // 512
    ksl = zt2v[:, 0:512] if _use_arena else S('kslb', [128, 512], BF16)[:]
    vsl = acc_t[1]

    def build_z2Ts(ts):
        zb = z0Ts[0]
        zbk = 'z0Ts0'
        for i in range(4):
            t = ts * 4 + i
            zf = z0f[0]
            apply_ln(zf, 'z0f0k', t, Xt(t), f'X{t}')
            for c in range(NC):
                E('tensor', lambda h, c=c, zf=zf: h.transpose(
                    ps_tb[:, c * 128:(c + 1) * 128], zf[:, c * 128:(c + 1) * 128], identb[:]),
                  reads=['z0f0k', 'identb'], writes=['ps_tb'], inc=(c == NC - 1))
            E('scalar', lambda h, i=i, zb=zb: h.activation(
                out=zb[:].rearrange('p (c q) -> p c q', q=512)[:, :, i * 128:(i + 1) * 128],
                in_=ps_tb[:, :NC * 128].rearrange('p (c q) -> p c q', q=128),
                func=AF.Copy),
              reads=['ps_tb'], writes=[f'{zbk}_{i}'])
        return zb, [f'{zbk}_{i}' for i in range(4)]

    E('vector', lambda h: h.engine_nop(),
      reads=['pwt'], writes=['z0Ts0_0', 'z0Ts0_1', 'z0Ts0_2', 'z0Ts0_3'])
    wload(WA, 'Wkt', 'WA', DM)
    for ts in range(nslice):
        zb, zrd = build_z2Ts(ts)
        for oc in range(NC):
            alt = oc % 2 == 1 and DM >= 512
            pst, pstk = (ps_c, 'ps_c') if alt else (ps_t, 'ps_t')
            ksl_o, kslk = (vsl, 'acc1k') if alt else (ksl, 'kslb')
            for kc in range(NC):
                E('tensor', lambda h, oc=oc, kc=kc, zb=zb, pst=pst: h.matmul(
                    pst[:, :512], WA[:, kc * DM + oc * 128: kc * DM + (oc + 1) * 128],
                    zb[:, kc * 512:(kc + 1) * 512],
                    start=(kc == 0), stop=(kc == NC - 1)),
                  reads=['WA'] + zrd, writes=[pstk], inc=(kc == NC - 1))
            E('scalar', lambda h, pst=pst, ksl_o=ksl_o: h.activation(
                out=ksl_o[:, 0:512], in_=pst[:, :512], func=AF.Copy),
              reads=[pstk], writes=[kslk])
            E('vector', lambda h, ksl_o=ksl_o: h.tensor_reduce(
                tsum2[:], ksl_o[:, 0:512], AX.X, ALU.max, apply_absolute_value=True),
              reads=[kslk], writes=['tsum2'])
            E('vector', lambda h: h.tensor_tensor(kmaxc[:], kmaxc[:], tsum2[:], ALU.max),
              reads=['kmaxc', 'tsum2'], writes=['kmaxc'])
            for sub in range(2):
                E('scalar', lambda h, oc=oc, sub=sub, ts=ts, ksl_o=ksl_o: h.dma_start(
                    out=D['kppd'][2 * oc + sub, 0:64, ts * 512:(ts + 1) * 512],
                    in_=ksl_o[64 * sub:64 * sub + 64, 0:512]),
                  reads=[kslk],
                  writes=[wkey('kppd', f'kpp{ts}_{oc}_{sub}')], dma=True)

    wload(WB, 'Wvt', 'WB', DM)
    for ts in range(nslice):
        zb, zrd = build_z2Ts(ts)
        for i in range(4):
            psd, psk = (ps_y, 'ps_y') if i % 2 == 0 else (ps_big, 'ps_big')
            for kc in range(NC):
                for hi_, (c0, cw) in enumerate(HL):
                    E('tensor', lambda h, i=i, kc=kc, zb=zb, c0=c0, cw=cw, psd=psd: h.matmul(
                        psd[:, c0:c0 + cw], zb[:, kc * 512 + i * 128: kc * 512 + (i + 1) * 128],
                        WB[:, kc * DM + c0: kc * DM + c0 + cw],
                        start=(kc == 0), stop=(kc == NC - 1)),
                      reads=['WB'] + zrd, writes=[psk],
                      inc=(kc == NC - 1 and hi_ == len(HL) - 1))
            vsl_i, vslk = (vsl, 'acc1k') if i % 2 == 0 else (nrm_t[0], 'nrm0k')
            E('scalar', lambda h, psd=psd, vsl_i=vsl_i: h.activation(
                out=vsl_i[:, :DM], in_=psd[:, :DM], func=AF.Copy),
              reads=[psk], writes=[vslk])
            E('scalar', lambda h, i=i, ts=ts, vsl_i=vsl_i: h.dma_start(
                out=D['vppd'][ts * 512 + i * 128: ts * 512 + (i + 1) * 128, :],
                in_=vsl_i[:, :DM]),
              reads=[vslk], writes=[wkey('vppd', f'vpp{ts}_{i}')], dma=True)

    wload(WA, 'Wqt', 'WA', DM)
    E('vector', lambda h: h.tensor_single_scalar(WA[:], WA[:], 0.125, ALU.mult),
      reads=['WA'], writes=['WA'])
    for ts in range(nslice):
        ov0 = max(ts * 512, SEQ // 2)
        ov1 = min((ts + 1) * 512, SEQ)
        if ov1 <= ov0:
            continue
        zb, zrd = build_z2Ts(ts)
        lo = ov0 - ts * 512
        w_ = ov1 - ov0
        qo = ov0 - SEQ // 2
        for oc in range(NC):
            alt = oc % 2 == 1 and DM >= 512
            pst, pstk = (ps_c, 'ps_c') if alt else (ps_t, 'ps_t')
            ksl_o, kslk = (vsl, 'acc1k') if alt else (ksl, 'kslb')
            for kc in range(NC):
                E('tensor', lambda h, oc=oc, kc=kc, zb=zb, pst=pst: h.matmul(
                    pst[:, :512], WA[:, kc * DM + oc * 128: kc * DM + (oc + 1) * 128],
                    zb[:, kc * 512:(kc + 1) * 512],
                    start=(kc == 0), stop=(kc == NC - 1)),
                  reads=['WA'] + zrd, writes=[pstk], inc=(kc == NC - 1))
            E('scalar', lambda h, pst=pst, ksl_o=ksl_o: h.activation(
                out=ksl_o[:, 0:512], in_=pst[:, :512], func=AF.Copy),
              reads=[pstk], writes=[kslk])
            E('scalar', lambda h, oc=oc, qo=qo, lo=lo, w_=w_, ksl_o=ksl_o: h.dma_start(
                out=D['qTd'][oc * 128:(oc + 1) * 128, qo:qo + w_],
                in_=ksl_o[:, lo:lo + w_]),
              reads=[kslk], writes=[wkey('qTd', f'qT{ts}_{oc}')], dma=True)

    E('tensor', lambda h: h.transpose(
        ps_t[:, 0:128], kmaxc[:, 0:1].broadcast_to([128, 128]), identf[:]),
      reads=['kmaxc', 'identf'], writes=['ps_t'])
    E('vector', lambda h: h.tensor_copy(trow[:], ps_t[0:1, 0:128]),
      reads=['ps_t'], writes=['trow'])
    E('vector', lambda h: h.tensor_reduce(kmx0[0:1, 0:1], trow[:], AX.X, ALU.max),
      reads=['trow'], writes=['kmx0'])

    crow = z1f0[0:1, 512:1024] if DM >= 1024 else S('crow', [1, 512], F32)[0:1, :]
    crowb = zt2v[0:1, 512:1024] if _use_arena else S('crowb', [1, 512], BF16)[0:1, :]

    def qpp(h_):
        return X[0:74, h_ * OWN:(h_ + 1) * OWN]
    qpr = {}
    for h_ in range(H):
        oc = (h_ * HD) // 128
        sub = h_ % 2
        E('gpsimd', lambda h, h_=h_, oc=oc, sub=sub: h.dma_start(
            out=qpp(h_)[0:64, :],
            in_=D['qTd'][oc * 128 + sub * 64: oc * 128 + sub * 64 + 64, 0:OWN]),
          reads=rkeys('qTd'),
          writes=[f'qpp{h_}'] + [f'X{t}' for t in range(NT)], dma=True)
        E('gpsimd', lambda h, h_=h_: h.dma_start(
            out=qpp(h_)[64:74, :], in_=D['qext'][h_]),
          writes=[f'qpe{h_}'], dma=True)
        qpr[h_] = [f'qpp{h_}', f'qpe{h_}', f'qpc{h_}']
    for h_ in range(H):
        qw = min(512, OWN)
        for cs0 in range(0, OWN, qw):
            E('scalar', lambda h, h_=h_, cs0=cs0, qw=qw: h.activation(
                out=z1f0[0:64, 0:qw], in_=qpp(h_)[0:64, cs0:cs0 + qw], func=AF.Square),
              reads=[f'qpp{h_}'], writes=['z1fk'])
            E('tensor', lambda h, qw=qw: h.matmul(
                ps_c[0:1, 0:qw], onescol[0:64, 0:1], z1f0[0:64, 0:qw],
                start=True, stop=True),
              reads=['z1fk', 'onescol'], writes=['ps_c'])
            E('scalar', lambda h, qw=qw: h.activation(
                out=crow[:, 0:qw], in_=ps_c[0:1, 0:qw], func=AF.Sqrt),
              reads=['ps_c'], writes=['crow'])
            E('vector', lambda h, qw=qw: h.tensor_scalar(
                crow[:, 0:qw], crow[:, 0:qw],
                kmx0[0:1, 0:1], -1.0, ALU.mult, ALU.mult),
              reads=['crow', 'kmx0'], writes=['crow'])
            E('vector', lambda h, qw=qw: h.tensor_copy(crowb[0:1, 0:qw], crow[:, 0:qw]),
              reads=['crow'], writes=['crowb'])
            E('gpsimd', lambda h, h_=h_, cs0=cs0, qw=qw: h.dma_start(
                out=qpp(h_)[64:65, cs0:cs0 + qw], in_=crowb[0:1, 0:qw]),
              reads=['crowb', f'qpe{h_}'], writes=[f'qpc{h_}'], dma=True)

    NKB = 1024 // QS
    nstrip = OWN // QS
    nblk = QS // 128
    KH = SEQ // 2
    KHC = KH // 128
    NKB = min(NKB, KHC)
    assert KHC % NKB == 0
    _vw = (SEQ // 128) * 65
    if _use_arena and _vw + SEQ <= NC * DM:
        vb0 = WA[0:128, 0:_vw]
        kb0 = WA[0:74, _vw:_vw + SEQ]
        vb1 = WB[0:128, 0:_vw]
        kb1 = WB[0:74, _vw:_vw + SEQ]
    else:
        vb0 = S('vbufs0', [128, _vw], BF16)[:, :]
        kb0 = S('kbufs0', [74, SEQ], BF16)[:, :]
        vb1, kb1 = vb0, kb0
    kbufs = [kb0, kb1]
    vbufs = [vb0, vb1]
    ps_z = pb.psum('ps_z', [128, 512], F32)
    assert NKB * QS + nblk * QS <= NC * 512 or not _use_arena
    if _use_arena and NC * 512 >= 2 * NKB * QS + nblk * QS:
        es0_ = z0Ts0[:, 0:NKB * QS]
        maskb_s = z0Ts0[:, NKB * QS:NKB * QS + nblk * QS]
        es1_ = z0Ts0[:, NKB * QS + nblk * QS:2 * NKB * QS + nblk * QS]
    elif _use_arena:
        es0_ = z0Ts0[:, 0:NKB * QS]
        maskb_s = z0Ts0[:, NKB * QS:NKB * QS + nblk * QS]
        es1_ = es0_
    else:
        es0_ = S('esb0', [128, NKB * QS], BF16)[:]
        maskb_s = S('maskb_s', [128, nblk * QS], BF16)[:]
        es1_ = S('esb1', [128, NKB * QS], BF16)[:]
    esb = [es0_, es1_]
    rzrow = z1f0[:, 0:QS]
    rz0 = z1f0[0:1, QS:2 * QS] if DM >= 2 * QS else S('rz0', [1, QS], F32)[0:1, :]
    rzbc = z0f0_[0:64, 0:QS] if DM >= QS else S('rzbc', [64, QS], BF16)[0:64, :]
    E('vector', lambda h: h.engine_nop(),
      reads=['z0Ts0_0', 'z0Ts0_1', 'z0Ts0_2', 'z0Ts0_3', 'z1fk'],
      writes=['esb0', 'maskb', 'rzrow', 'rz0k'])
    E('sync', lambda h: h.dma_start(out=maskb_s[:, :], in_=D['maskb'][:]),
      reads=['maskb'], writes=['maskb'], dma=True)
    E('vector', lambda h: h.memset(vb0[:, :], 1.0), reads=['WA'], writes=['vbinit0', 'kbwa'])
    for h_ in range(H):
        for qq in range(SEQ // DM):
            E('sync', lambda h, h_=h_, qq=qq: h.dma_start(
                out=xf_a[0:10, :], in_=D['kext'][h_][:, qq * DM:(qq + 1) * DM]),
              writes=['xfa'], dma=True)
            E('vector', lambda h: h.tensor_copy(zt[0][0:10, 0:DM], xf_a[0:10, :]),
              reads=['xfa'], writes=['zt0k'])
            E('gpsimd', lambda h, h_=h_, qq=qq: h.dma_start(
                out=D['kppd'][h_, 64:74, qq * DM:(qq + 1) * DM], in_=zt[0][0:10, 0:DM]),
              reads=['zt0k'], writes=[wkey('kppd', f'kppe{h_}_{qq}')], dma=True)
    tern_per_head = (len(tern_units) + H - 1) // H
    pcS = [(ps_c, 'ps_c'), (ps_t, 'ps_t')]
    if vb1 is not vb0:
        E('vector', lambda h: h.memset(vb1[:, :], 1.0), reads=['WB'],
          writes=['vbinit1', 'kbwb'])
    pend = [None]

    def emit_epilogue(h_, qs, par):
        pc, pck = pcS[par]
        E('vector', lambda h, pc=pc: h.reciprocal(rzrow[64:65, 0:QS], pc[64:65, 0:QS]),
          reads=[pck], writes=['rzrow'])
        E('gpsimd', lambda h: h.dma_start(out=rz0[0:1, :], in_=rzrow[64:65, 0:QS]),
          reads=['rzrow'], writes=['rz0k'], dma=True)
        E('tensor', lambda h: h.matmul(ps_z[0:64, 0:QS], onesrow[0:1, 0:64],
                                       rz0[0:1, :], start=True, stop=True),
          reads=['rz0k', 'onesrow'], writes=['ps_z'])
        E('scalar', lambda h: h.activation(out=rzbc[0:64, :], in_=ps_z[0:64, 0:QS],
                                           func=AF.Copy),
          reads=['ps_z'], writes=['rzbc'])
        ctv = buT[64 * (h_ % 2): 64 * (h_ % 2) + 64,
                  (h_ // 2) * OWN + qs * QS: (h_ // 2) * OWN + (qs + 1) * QS]
        E('vector', lambda h, ctv=ctv, pc=pc: h.scalar_tensor_tensor(
            ctv, pc[0:64, 0:QS], 1.0, rzbc[0:64, :], ALU.bypass, ALU.mult),
          reads=[pck, 'rzbc'], writes=[wkey('ctx', f'ctx{h_}_{qs}')])

    it_c = [0]

    def prefetch_head(h_):
        par = h_ % 2 if kb1 is not kb0 else 0
        kb, kbk = kbufs[par], f'kbufs{par}'
        vb, vbk = vbufs[par], f'vbufs{par}'
        kbbar = 'kbwa' if kb is kb0 else 'kbwb'
        E('sync', lambda h, h_=h_, kb=kb: h.dma_start(
            out=kb[0:74, :], in_=D['kppd'][h_, :, :]),
          reads=rkeys('kppd') + [kbbar], writes=[kbk], dma=True)
        E('sync', lambda h, h_=h_, vb=vb: h.dma_start(
            out=vb.rearrange('p (c w) -> p c w', w=65)[:, :, 0:64],
            in_=D['vppd'].rearrange('(c p) d -> p c d', p=128)[:, :, h_ * HD:(h_ + 1) * HD]),
          reads=rkeys('vppd') + ['vbinit0' if par == 0 else 'vbinit1'],
          writes=[vbk], dma=True)

    prefetch_head(0)
    for h_ in range(H):
        hu = tern_units[h_ * tern_per_head:(h_ + 1) * tern_per_head]
        if hu and pend[0] is not None:
            emit_epilogue(*pend[0])
            pend[0] = None
        for u in hu:
            u()
        if h_ + 1 < H and (kb1 is not kb0):
            prefetch_head(h_ + 1)
        par = h_ % 2 if kb1 is not kb0 else 0
        kb, kbk = kbufs[par], f'kbufs{par}'
        vb, vbk = vbufs[par], f'vbufs{par}'
        if kb1 is kb0 and h_ > 0:
            prefetch_head(h_)
        for qs in range(nstrip):
            kend = (OWN + (qs + 1) * QS) // 128
            band0 = kend - nblk
            psS = ([(ps_big, 'ps_big'), (ps_y, 'ps_y')] if DM >= NKB * QS
                   else [(ps_big, 'ps_big'), (ps_big, 'ps_big')])

            def emit_scores(idx, g0, gn, h_=h_, qs=qs, kend=kend, band0=band0,
                            kb=kb, kbk=kbk):
                ps, psk = psS[idx % 2]
                for gi in range(gn):
                    kc = g0 + gi
                    E('tensor', lambda h, h_=h_, kc=kc, gi=gi, qs=qs, kb=kb, ps=ps: h.matmul(
                        ps[:, gi * QS:(gi + 1) * QS],
                        kb[0:74, kc * 128:(kc + 1) * 128],
                        qpp(h_)[0:74, qs * QS:(qs + 1) * QS],
                        start=True, stop=True),
                      reads=[kbk] + qpr[h_], writes=[psk], inc=(gi == gn - 1))
                for gi in range(gn):
                    kc = g0 + gi
                    if kc >= band0:
                        ib = kc - band0
                        E('vector', lambda h, gi=gi, ib=ib, ps=ps: h.tensor_tensor(
                            ps[:, gi * QS:(gi + 1) * QS],
                            ps[:, gi * QS:(gi + 1) * QS],
                            maskb_s[:, ib * QS:(ib + 1) * QS], ALU.add),
                          reads=[psk, 'maskb'], writes=[psk])
                es = esb[idx % 2]
                esk = f'esb{idx % 2}'
                E('scalar', lambda h, gn=gn, es=es, ps=ps: h.activation(
                    out=es[:, :gn * QS], in_=ps[:, :gn * QS], func=AF.Exp),
                  reads=[psk], writes=[esk])

            pc_par = it_c[0] % 2
            pc, pck = pcS[pc_par]

            def emit_ctx(idx, g0, gn, kend=kend, vb=vb, vbk=vbk, pc=pc, pck=pck):
                es = esb[idx % 2]
                esk = f'esb{idx % 2}'
                for gi in range(gn):
                    kc = g0 + gi
                    E('tensor', lambda h, kc=kc, gi=gi, es=es, vb=vb, kend=kend,
                      pc=pc: h.matmul(
                        pc[0:65, 0:QS], vb[:, kc * 65:(kc + 1) * 65],
                        es[:, gi * QS:(gi + 1) * QS],
                        start=(kc == 0), stop=(kc == kend - 1)),
                      reads=[esk, vbk], writes=[pck], inc=(kc == kend - 1))

            groups = [(g0, min(NKB, kend - g0)) for g0 in range(0, kend, NKB)]
            prev = None
            for idx, (g0, gn) in enumerate(groups):
                emit_scores(idx, g0, gn)
                if prev is not None:
                    emit_ctx(prev[0], prev[1], prev[2])
                prev = (idx, g0, gn)
            emit_ctx(prev[0], prev[1], prev[2])
            if pend[0] is not None:
                emit_epilogue(*pend[0])
            pend[0] = (h_, qs, pc_par)
            it_c[0] += 1
    emit_epilogue(*pend[0])

    bcast_load(bobc, 'bo', 'gbc', BF16)
    if kb1 is not kb0:
        E('vector', lambda h: h.engine_nop(),
          reads=['kbufs1', 'vbufs1'], writes=['WB'])
    wload(WB, 'Wot', 'WB', DM)
    for t in range(NTO):
        for hi_, (c0, cw) in enumerate(HL):
            for kc in range(NC):
                E('tensor', lambda h, kc=kc, t=t, c0=c0, cw=cw: h.matmul(
                    ps_y[:, c0:c0 + cw],
                    buT[:, kc * OWN + t * 128: kc * OWN + (t + 1) * 128],
                    WB[:, kc * DM + c0: kc * DM + c0 + cw],
                    start=(kc == 0), stop=(kc == NC - 1)),
                  reads=rkeys('ctx') + ['WB'], writes=['ps_y'], inc=(kc == NC - 1))
        nt_, ntk = nrm_t[0], 'nrm0k'
        E('vector', lambda h, nt_=nt_: h.scalar_tensor_tensor(
            nt_[:], ps_y[:, :DM], 1.0, bobc[:], ALU.bypass, ALU.add),
          reads=['ps_y', 'gbc'], writes=[ntk])
        tile_stats(nt_, ntk, LNEPS_O)
        E('scalar', lambda h, nt_=nt_: h.activation(
            out=xf_b[:], in_=nt_[:], func=AF.Identity, bias=gnm[:, 0:1], scale=grs[:, 0:1]),
          reads=[ntk, 'grs', 'gnm', 'rzrow', 'rz0k'], writes=['z1fk'])
        E('sync', lambda h, t=t: h.dma_start(
            out=xf_a[:], in_=D['x2d'][OWN + t * 128: OWN + (t + 1) * 128, :]),
          reads=rkeys('x2d'), writes=['xfa'], dma=True)
        E('vector', lambda h: h.tensor_tensor(xf_a[:], xf_a[:], xf_b[:], ALU.add),
          reads=['xfa', 'z1fk'], writes=['xfa'])
        E('scalar', lambda h, t=t: h.dma_start(out=D['x3d'][t * 128:(t + 1) * 128, :],
                                               in_=xf_a[:]),
          reads=['xfa'], writes=[wkey('x3d', f'x3d{t}')], dma=True)

    if debug_out == 'x3':
        for t in range(NTO):
            E('sync', lambda h, t=t: h.dma_start(out=xf_a[:], in_=D['x3d'][t * 128:(t + 1) * 128, :]),
              reads=rkeys('x3d'), writes=['xfa'], dma=True)
            E('scalar', lambda h, t=t: h.dma_start(out=out_ap[t * 128:(t + 1) * 128, :], in_=xf_a[:]),
              reads=['xfa'], dma=True)
        pb.wait_all_done('sync')
        pb.build()
        return nc, out_names

    hamax = SP(NTO)
    hrec = SP(NTO)
    hscl = SP(NTO)
    gsc = SP(NTO)
    usc = SP(NTO)
    dsc = SP(NTO)
    E('vector', lambda h: h.memset(hamax[:], 0.0), writes=['hamax'])
    E('vector', lambda h: h.engine_nop(),
      reads=['twbk', 'tc1', 'tc2'], writes=PAR1KEYS + ['acc1k', 'qc1k'])
    for t in range(NTO):
        par = t % 2
        xfa_, xfak = xfa_t[par], ('xfa' if par == 0 else 'xfa1')
        zf32, zfk = z1f[par], ('z1fk' if par == 0 else 'z1fbk')
        zc, zck = zt[par], f'zt{par}k'
        E('sync', lambda h, t=t, xfa_=xfa_: h.dma_start(
            out=xfa_[:], in_=D['x3d'][t * 128:(t + 1) * 128, :]),
          reads=rkeys('x3d'), writes=[xfak], dma=True)
        rs_, nm_, rsk, nmk = tile_stats(xfa_, xfak, LNEPS_O, par)
        E('scalar', lambda h, zf32=zf32, xfa_=xfa_, rs_=rs_, nm_=nm_: h.activation(
            out=zf32[:], in_=xfa_[:], func=AF.Identity, bias=nm_[:, 0:1], scale=rs_[:, 0:1]),
          reads=[xfak, rsk, nmk], writes=[zfk])
        E('vector', lambda h, zf32=zf32, t=t: h.reduce_max(
            amax[:, t:t + 1], zf32[:], AX.X, apply_absolute_value=True),
          reads=[zfk], writes=[f'am{t}'])
        E('vector', lambda h, t=t: h.tensor_single_scalar(
            qscl[:, t:t + 1], amax[:, t:t + 1], 1e-5, ALU.max),
          reads=[f'am{t}'], writes=[f'qs{t}'])
        E('vector', lambda h, t=t: h.reciprocal(qrec[:, t:t + 1], qscl[:, t:t + 1]),
          reads=[f'qs{t}'], writes=[f'qr{t}'])
        E('vector', lambda h, t=t: h.tensor_single_scalar(
            qrec[:, t:t + 1], qrec[:, t:t + 1], 127.0, ALU.mult),
          reads=[f'qr{t}'], writes=[f'qr{t}'])
        vv_ = q_v[par]
        E('vector', lambda h, vv_=vv_, zf32=zf32, t=t: h.tensor_scalar(
            vv_[:], zf32[:], qrec[:, t:t + 1], None, ALU.mult),
          reads=[zfk, f'qr{t}'], writes=[zfk])
        xq = quant_round_full(par, vv_, zfk)
        for c in range(NC):
            E('tensor', lambda h, c=c, xq=xq[0]: h.transpose(
                ps_tb[:, c * 128:(c + 1) * 128], xq[:, c * 128:(c + 1) * 128], identb[:]),
              reads=[xq[1], 'identb'], writes=['ps_tb'], inc=(c == NC - 1))
        E('scalar', lambda h, zc=zc: h.activation(out=zc[:, :NC * 128],
                                                  in_=ps_tb[:, :NC * 128], func=AF.Copy),
          reads=['ps_tb'], writes=[zck])
        E('scalar', lambda h, t=t, zc=zc: h.dma_start(
            out=D['qTd'][0:DM, t * 128:(t + 1) * 128].rearrange('(c p) w -> p c w', p=128),
            in_=zc[:, :NC * 128].rearrange('p (c w) -> p c w', w=128)),
          reads=[zck] + [f'qpp{h2}' for h2 in range(H)],
          writes=[wkey('xqT', f'xqT{t}')], dma=True)
    if debug_out == 'xq':
        for c in range(DM // 128):
            E('sync', lambda h, c=c: h.dma_start(
                out=twb0[:, 0:OWN], in_=D['qTd'][c * 128:(c + 1) * 128, 0:OWN]),
              reads=rkeys('xqT'), writes=['twbk'], dma=True)
            for j in range(OWN // DM):
                E('vector', lambda h, j=j: h.tensor_copy(
                    xf_a[:], twb0[:, j * DM:(j + 1) * DM]),
                  reads=['twbk'], writes=['xfa'])
                E('scalar', lambda h, c=c, j=j: h.dma_start(
                    out=dbg_ap[(c * (OWN // DM) + j) * 128:(c * (OWN // DM) + j + 1) * 128, :],
                    in_=xf_a[:]),
                  reads=['xfa'], dma=True)
        pb.wait_all_done('sync')
        pb.build()
        return nc, out_names

    E('vector', lambda h: h.tensor_scalar(gsc[:], qscl[:, 0:NTO], wmg[:, 0:1],
                                          float(1.0 / 127.0), ALU.mult, ALU.mult),
      reads=['wmg'] + [f'qs{t}' for t in range(NTO)], writes=['gsc'])
    E('vector', lambda h: h.tensor_scalar(usc[:], qscl[:, 0:NTO], wmu[:, 0:1],
                                          float(1.0 / 127.0), ALU.mult, ALU.mult),
      reads=['wmu'] + [f'qs{t}' for t in range(NTO)], writes=['usc'])

    NIH = 2
    IH = INNER // NIH
    cgw = min(512, IH)
    for ih in range(NIH):
        TgV = X[:, 0:NC * IH]
        TuV = X[:, NC * IH: 2 * NC * IH]
        qfree = [f'qpp{h2}' for h2 in range(H)] + [f'qpe{h2}' for h2 in range(H)] +                 [f'qpc{h2}' for h2 in range(H)]
        E('sync', lambda h, ih=ih, TgV=TgV: h.dma_start(
            out=TgV.rearrange('p (c w) -> p c w', w=IH),
            in_=D['Tgd'].rearrange('(c p) n -> p c n', p=128)[:, :, ih * IH:(ih + 1) * IH]),
          reads=rkeys('Tgd'), writes=['TgV'] + (qfree if ih == 0 else []), dma=True)
        E('sync', lambda h, ih=ih, TuV=TuV: h.dma_start(
            out=TuV.rearrange('p (c w) -> p c w', w=IH),
            in_=D['Tud'].rearrange('(c p) n -> p c n', p=128)[:, :, ih * IH:(ih + 1) * IH]),
          reads=rkeys('Tud'), writes=['TuV'], dma=True)
        GW = min(DM, 1024, IH)
        for t in range(NTO):
            zc, zck = zt[0], 'zt0k'
            E('gpsimd', lambda h, t=t, zc=zc: h.dma_start(
                out=zc[:, :NC * 128].rearrange('p (c w) -> p c w', w=128),
                in_=D['qTd'][0:DM, t * 128:(t + 1) * 128].rearrange('(c p) w -> p c w', p=128)),
              reads=rkeys('xqT'), writes=[zck], dma=True)
            for grp in range(IH // GW):
                for kc in range(NC):
                    for part in ('g', 'u'):
                        WV, wk2 = (TgV, 'TgV') if part == 'g' else (TuV, 'TuV')
                        psd = ps_y if part == 'g' else ps_big
                        psk = 'ps_y' if part == 'g' else 'ps_big'
                        for cg in range(GW // cgw):
                            col0 = grp * GW + cg * cgw
                            E('tensor', lambda h, kc=kc, cg=cg, zc=zc, WV=WV,
                              col0=col0, psd=psd: h.matmul(
                                psd[:, cg * cgw:(cg + 1) * cgw], zc[:, kc * 128:(kc + 1) * 128],
                                WV[:, kc * IH + col0: kc * IH + col0 + cgw],
                                start=(kc == 0), stop=(kc == NC - 1)),
                              reads=[zck, wk2], writes=[psk],
                              inc=(kc == NC - 1 and part == 'u' and cg == GW // cgw - 1))
                E('scalar', lambda h, t=t: h.activation(
                    out=z1f0[:, 0:GW], in_=ps_y[:, 0:GW],
                    func=AF.Silu, bias=np.float32(0.0), scale=gsc[:, t:t + 1]),
                  reads=['ps_y', 'ps_big', 'gsc'], writes=['z1fk'])
                E('scalar', lambda h, t=t: h.activation(
                    out=xf_a[:, 0:GW], in_=ps_big[:, 0:GW],
                    func=AF.Identity, bias=np.float32(0.0), scale=usc[:, t:t + 1]),
                  reads=['ps_y', 'ps_big', 'usc'], writes=['xfa'])
                if debug_out == 'gu':
                    E('vector', lambda h, grp=grp: h.tensor_copy(
                        gtile[:, grp * GW:(grp + 1) * GW], z1f0[:, 0:GW]),
                      reads=['z1fk'], writes=['gtile'])
                    E('vector', lambda h, grp=grp: h.tensor_copy(
                        util[:, grp * GW:(grp + 1) * GW], xf_a[:, 0:GW]),
                      reads=['xfa'], writes=['util'])
                    continue
                E('vector', lambda h: h.tensor_tensor(z1f0[:, 0:GW], z1f0[:, 0:GW],
                                                      xf_a[:, 0:GW], ALU.mult),
                  reads=['z1fk', 'xfa'], writes=['z1fk'])
                E('vector', lambda h, grp=grp: h.tensor_copy(
                    gtile[:, grp * GW:(grp + 1) * GW], z1f0[:, 0:GW]),
                  reads=['z1fk'], writes=['gtile'])
                E('vector', lambda h, grp=grp: h.scalar_tensor_tensor(
                    util[:, grp * GW:(grp + 1) * GW], gtile[:, grp * GW:(grp + 1) * GW],
                    -1.0, z1f0[:, 0:GW], ALU.mult, ALU.add),
                  reads=['gtile', 'z1fk'], writes=['util'])
                E('vector', lambda h: h.tensor_reduce(tsum2[:], z1f0[:, 0:GW], AX.X,
                                                      ALU.max, apply_absolute_value=True),
                  reads=['z1fk'], writes=['tsum2'])
                E('vector', lambda h, t=t: h.tensor_tensor(
                    hamax[:, t:t + 1], hamax[:, t:t + 1], tsum2[:], ALU.max),
                  reads=['hamax', 'tsum2'], writes=['hamax'])
            E('gpsimd', lambda h, t=t, ih=ih: h.dma_start(
                out=D['hd'][t * 128:(t + 1) * 128, ih * IH:(ih + 1) * IH],
                in_=gtile[:, :]),
              reads=['gtile'], writes=[wkey('hd', f'hd{ih}_{t}')], dma=True)
            E('gpsimd', lambda h, t=t, ih=ih: h.dma_start(
                out=D['hd2'][t * 128:(t + 1) * 128, ih * IH:(ih + 1) * IH],
                in_=util[:, :]),
              reads=['util'], writes=[wkey('hd2', f'hd2{ih}_{t}')], dma=True)

    if debug_out == 'gu':
        for t in range(NTO):
            for j, src in ((0, 'hd'), (1, 'hd2')):
                E('sync', lambda h, t=t, src=src: h.dma_start(
                    out=twb0[:, 0:DM], in_=D[src][t * 128:(t + 1) * 128, 0:DM]),
                  reads=rkeys('hd') + rkeys('hd2'), writes=['twbk'], dma=True)
                E('vector', lambda h: h.tensor_copy(xf_a[:], twb0[:, 0:DM]),
                  reads=['twbk'], writes=['xfa'])
                E('scalar', lambda h, t=t, j=j: h.dma_start(
                    out=dbg_ap[(t * 2 + j) * 128:(t * 2 + j + 1) * 128, :], in_=xf_a[:]),
                  reads=['xfa'], dma=True)
        pb.wait_all_done('sync')
        pb.build()
        return nc, out_names

    if debug_out == 'hh':
        for t in range(NTO):
            for j in range(2):
                E('sync', lambda h, t=t, j=j: h.dma_start(
                    out=xf_a[:].bitcast(BF16)[:, 0:DM],
                    in_=D['hd'][t * 128:(t + 1) * 128, j * DM:(j + 1) * DM]),
                  reads=rkeys('hd'), writes=['xfa'], dma=True) if False else None
                E('sync', lambda h, t=t, j=j: h.dma_start(
                    out=twb0[:, 0:DM], in_=D['hd'][t * 128:(t + 1) * 128, j * DM:(j + 1) * DM]),
                  reads=rkeys('hd'), writes=['twbk'], dma=True)
                E('vector', lambda h: h.tensor_copy(xf_a[:], twb0[:, 0:DM]),
                  reads=['twbk'], writes=['xfa'])
                E('scalar', lambda h, t=t, j=j: h.dma_start(
                    out=dbg_ap[(t * 2 + j) * 128:(t * 2 + j + 1) * 128, :], in_=xf_a[:]),
                  reads=['xfa'], dma=True)
        pb.wait_all_done('sync')
        pb.build()
        return nc, out_names

    NCI = INNER // 128
    TdV = X[:, 0:NCI * DM]
    E('sync', lambda h: h.dma_start(
        out=TdV.rearrange('p (c w) -> p c w', w=DM),
        in_=D['Tdd'].rearrange('(c p) n -> p c n', p=128)),
      reads=rkeys('Tdd'), writes=['TdV', 'TgV', 'TuV'], dma=True)
    E('vector', lambda h: h.tensor_scalar(dsc[:], hamax[:, 0:NTO], wmd[:, 0:1],
                                          float(1.0 / 127.0), ALU.mult, ALU.mult),
      reads=['wmd', 'hamax'], writes=['dsc'])
    E('vector', lambda h: h.tensor_scalar(hscl[:], hamax[:, 0:NTO], None, None,
                                          ALU.bypass), reads=['hamax'], writes=['hsclx'],
      inc=True) if False else None

    for t in range(NTO):
        E('vector', lambda h, t=t: h.tensor_single_scalar(
            hscl[:, t:t + 1], hamax[:, t:t + 1], 1e-5, ALU.max),
          reads=['hamax'], writes=[f'hs{t}'])
        E('vector', lambda h, t=t: h.reciprocal(hrec[:, t:t + 1], hscl[:, t:t + 1]),
          reads=[f'hs{t}'], writes=[f'hr{t}'])
        E('vector', lambda h, t=t: h.tensor_single_scalar(
            hrec[:, t:t + 1], hrec[:, t:t + 1], 127.0, ALU.mult),
          reads=[f'hr{t}'], writes=[f'hr{t}'])
        E('vector', lambda h, t=t: h.tensor_scalar(
            dsc[:, t:t + 1], hscl[:, t:t + 1], wmd[:, 0:1], float(1.0 / 127.0),
            ALU.mult, ALU.mult),
          reads=[f'hs{t}', 'wmd'], writes=[f'dsc{t}'])
        par = t % 2
        ht_, htk = (htile, 'htile') if par == 0 else (htile2, 'htile1')
        hT_, hTpre = (hqT, 'hqT0_') if par == 0 else (hqT2, 'hqT1_')
        xfa_, xfak = xfa_t[par], ('xfa' if par == 0 else 'xfa1')
        zf32, zfk = z1f[par], ('z1fk' if par == 0 else 'z1fbk')
        nt_, ntk = nrm_t[par], ('nrm0k' if par == 0 else 'nrm1k')
        psY, psYk = (ps_y, 'ps_y') if par == 0 else (ps_big, 'ps_big')
        hqpre = f'hq{par}_'
        E('gpsimd', lambda h, t=t, ht_=ht_: h.dma_start(
            out=ht_[:, :], in_=D['hd'][t * 128:(t + 1) * 128, :]),
          reads=rkeys('hd'),
          writes=[htk] + [hqpre + str(s2) for s2 in range(INNER // DM)], dma=True)
        if par == 0:
            lo_f = None
            E('gpsimd', lambda h, t=t: h.dma_start(
                out=gtile[:, :], in_=D['hd2'][t * 128:(t + 1) * 128, 0:INNER // 2]),
              reads=rkeys('hd2'), writes=['gtile'], dma=True)
            E('gpsimd', lambda h, t=t: h.dma_start(
                out=util[:, :], in_=D['hd2'][t * 128:(t + 1) * 128, INNER // 2:INNER]),
              reads=rkeys('hd2'), writes=['util'], dma=True)
            lokeys = ['gtile', 'util']
        else:
            lo_f = WA[0:128, 0:INNER] if _use_arena else None
            if lo_f is not None:
                E('gpsimd', lambda h, t=t, lo_f=lo_f: h.dma_start(
                    out=lo_f[:, :], in_=D['hd2'][t * 128:(t + 1) * 128, :]),
                  reads=rkeys('hd2') + ['kbufs0', 'vbufs0'], writes=['lo1k'], dma=True)
                lokeys = ['lo1k']
            else:
                E('gpsimd', lambda h, t=t: h.dma_start(
                    out=gtile[:, :], in_=D['hd2'][t * 128:(t + 1) * 128, 0:INNER // 2]),
                  reads=rkeys('hd2'), writes=['gtile'], dma=True)
                E('gpsimd', lambda h, t=t: h.dma_start(
                    out=util[:, :], in_=D['hd2'][t * 128:(t + 1) * 128, INNER // 2:INNER]),
                  reads=rkeys('hd2'), writes=['util'], dma=True)
                lokeys = ['gtile', 'util']
        for sub in range(INNER // DM):
            vv_ = q_v[par]
            E('vector', lambda h, sub=sub, t=t, ht_=ht_, vv_=vv_: h.tensor_scalar(
                vv_[:], ht_[:, sub * DM:(sub + 1) * DM], hrec[:, t:t + 1],
                None, ALU.mult),
              reads=[htk, f'hr{t}'], writes=[zfk])
            IH2 = INNER // 2
            if lo_f is not None:
                lo_ap = lo_f[:, sub * DM:(sub + 1) * DM]
            else:
                lo_ap = (gtile[:, sub * DM:(sub + 1) * DM] if (sub + 1) * DM <= IH2
                         else util[:, sub * DM - IH2:(sub + 1) * DM - IH2])
            E('vector', lambda h, lo_ap=lo_ap, t=t, vv_=vv_: h.scalar_tensor_tensor(
                vv_[:], lo_ap, hrec[:, t:t + 1], vv_[:], ALU.mult, ALU.add),
              reads=lokeys + [f'hr{t}', zfk], writes=[zfk])
            xqh = quant_round_full(par, vv_, zfk)
            E('vector', lambda h, sub=sub, xqh=xqh[0], ht_=ht_: h.tensor_copy(
                ht_[:, sub * DM:(sub + 1) * DM], xqh),
              reads=[xqh[1], htk], writes=[hqpre + str(sub)])
        hqr = [hqpre + str(s2) for s2 in range(INNER // DM)]
        for b in range(NCI // NC):
            for cc in range(NC):
                E('tensor', lambda h, b=b, cc=cc, ht_=ht_: h.transpose(
                    ps_tb[:, cc * 128:(cc + 1) * 128],
                    ht_[:, (b * NC + cc) * 128:(b * NC + cc + 1) * 128], identb[:]),
                  reads=hqr + ['identb'], writes=['ps_tb'], inc=(cc == NC - 1))
            E('scalar', lambda h, b=b, hT_=hT_: h.activation(
                out=hT_[:, b * NC * 128:(b + 1) * NC * 128], in_=ps_tb[:, :NC * 128],
                func=AF.Copy),
              reads=['ps_tb'], writes=[hTpre + str(b)])
        hqTr = [hTpre + str(b) for b in range(NCI // NC)]
        for kc in range(NCI):
            for hi_, (c0, cw) in enumerate(HL):
                E('tensor', lambda h, kc=kc, c0=c0, cw=cw, hT_=hT_, psY=psY: h.matmul(
                    psY[:, c0:c0 + cw], hT_[:, kc * 128:(kc + 1) * 128],
                    TdV[:, kc * DM + c0: kc * DM + c0 + cw],
                    start=(kc == 0), stop=(kc == NCI - 1)),
                  reads=hqTr + ['TdV'], writes=[psYk],
                  inc=(kc == NCI - 1 and hi_ == len(HL) - 1))
        E('scalar', lambda h, t=t, nt_=nt_, psY=psY: h.activation(
            out=nt_[:], in_=psY[:, :DM], func=AF.Identity, bias=np.float32(0.0),
            scale=dsc[:, t:t + 1]),
          reads=[psYk, f'dsc{t}'], writes=[ntk])
        rs_, nm_, rsk, nmk = tile_stats(nt_, ntk, LNEPS_O, par)
        E('scalar', lambda h, nt_=nt_, zf32=zf32, rs_=rs_, nm_=nm_: h.activation(
            out=zf32[:], in_=nt_[:], func=AF.Identity, bias=nm_[:, 0:1], scale=rs_[:, 0:1]),
          reads=[ntk, rsk, nmk], writes=[zfk])
        E('sync', lambda h, t=t, xfa_=xfa_: h.dma_start(
            out=xfa_[:], in_=D['x3d'][t * 128:(t + 1) * 128, :]),
          reads=rkeys('x3d'), writes=[xfak], dma=True)
        E('vector', lambda h, xfa_=xfa_, zf32=zf32: h.tensor_tensor(
            xfa_[:], xfa_[:], zf32[:], ALU.add),
          reads=[xfak, zfk], writes=[xfak])
        E('sync', lambda h, t=t, xfa_=xfa_: h.dma_start(
            out=out_ap[t * 128:(t + 1) * 128, :], in_=xfa_[:]),
          reads=[xfak], dma=True)

    pb.wait_all_done('sync')
    pb.build()
    print('SBUF bytes/partition:', pb.sbuf_bytes)
    return nc, out_names


from concourse.bass_utils import run_bass_kernel_spmd

_BUILD_CACHE = {}


def kernel(**inputs):
    cfg = CFG()
    maps = host_stage(inputs, cfg)
    if 'nc' not in _BUILD_CACHE:
        _BUILD_CACHE['nc'] = build(cfg, debug_out=None)
    nc, out_names = _BUILD_CACHE['nc']
    res = run_bass_kernel_spmd(nc, maps, core_ids=list(range(len(maps))))
    B = len(maps) // 2
    out = np.zeros((B, cfg.SEQ, cfg.DM), np.float32)
    for c in range(len(maps)):
        b, half = c // 2, c % 2
        out[b, half * cfg.OWN:(half + 1) * cfg.OWN] = np.asarray(res.results[c]['out'])
    return out
